# version 01
# baseline (speedup 1.0000x reference)
import numpy as np
import ml_dtypes
import concourse.bass as bass
import concourse.mybir as mybir
from concourse.bass_utils import run_bass_kernel_spmd

F32 = mybir.dt.float32
BF16 = mybir.dt.bfloat16
I32 = mybir.dt.int32
ALU = mybir.AluOpType
AF = mybir.ActivationFunctionType
AX = mybir.AxisListType
BF = ml_dtypes.bfloat16

P = 128
D = 1024; DC = 8
T = 2048; NT4 = 4
NH = 16
FF = 4096; NFC = 32
EPS_LN = 1e-5
EPS_Q = 1e-8

OFF_X = 17408
OFF_WQ2T = 17408
OFF_B = 82944
OFF_QSB = OFF_B
OFF_KSB = OFF_B + 8192
OFF_VSB = OFF_B + 16384
OFF_USB = OFF_B + 24704
OFF_MASKS = OFF_B + 57472
OFF_WQ1T = OFF_B
OFF_H3Q0 = OFF_B
OFF_H3Q1 = OFF_B + 32768
OFF_SXROW = OFF_B
OFF_SXXROW = OFF_B + 8192
OFF_XBF = OFF_B + 24704
OFF_MH2 = OFF_B + 41088
OFF_C = 148480
OFF_D = 181248
OFF_ROWS = 214016
OFF_E = 226304
OFF_MODROW = OFF_B

ENGS = ("sync", "gpsimd", "tensor", "vector", "scalar")


class Ctr:
    __slots__ = ("sem", "n")
    def __init__(self, sem):
        self.sem = sem
        self.n = 0
    def bump(self, k=1):
        self.n += k
        return self.n


SIM_DRAINS = [False]


def dinc(e, sem, k=1):
    e.drain().then_inc(sem, k)


class DrainWrap:
    __slots__ = ("_e",)
    def __init__(self, e):
        object.__setattr__(self, "_e", e)
    def __getattr__(self, n):
        a = getattr(self._e, n)
        if callable(a) and n not in ("wait_ge", "drain", "nop", "sem_inc"):
            def w(*args, **kw):
                self._e.drain()
                return a(*args, **kw)
            return w
        return a


class Plan:
    def __init__(self):
        self.ops = {e: [] for e in ENGS}
    def op(self, eng, fn, waits=()):
        self.ops[eng].append((tuple(waits), fn))
    def emit(self, nc):
        with nc.Block() as b:
            for eng in ENGS:
                ops = self.ops[eng]
                if not ops:
                    continue
                def body(e, ops=ops):
                    if SIM_DRAINS[0]:
                        e = DrainWrap(e)
                    for waits, fn in ops:
                        for (c, v) in waits:
                            e.wait_ge(c.sem, v)
                        fn(e)
                getattr(b, eng)(body)


def build_nc(stage=5, debug=(), sim=False, skip_mod=False):
    SIM_DRAINS[0] = bool(sim)
    nc = bass.Bass("TRN2", target_bir_lowering=False, debug=False, dynamic_dma_scratch_size=16384,
                   detect_race_conditions=bool(sim))
    dbg = set(debug)

    x_in = nc.dram_tensor("x_fm", [D, T], F32, kind="ExternalInput").ap()
    wq_d = nc.dram_tensor("wq", [D, D], F32, kind="ExternalInput").ap()
    wk_d = nc.dram_tensor("wk", [D, D], F32, kind="ExternalInput").ap()
    wv_d = nc.dram_tensor("wv", [D, D], F32, kind="ExternalInput").ap()
    wo_d = nc.dram_tensor("wo", [D, D], F32, kind="ExternalInput").ap()
    wmod_d = nc.dram_tensor("wmod", [D, 6 * D], F32, kind="ExternalInput").ap()
    crow_d = nc.dram_tensor("crow", [1, D], F32, kind="ExternalInput").ap()
    w1n_d = nc.dram_tensor("w1n", [FF, D], F32, kind="ExternalInput").ap()
    w1t_d = nc.dram_tensor("w1t", [D, FF], F32, kind="ExternalInput").ap()
    w2n_d = nc.dram_tensor("w2n", [D, FF], F32, kind="ExternalInput").ap()
    w2t_d = nc.dram_tensor("w2t", [FF, D], F32, kind="ExternalInput").ap()
    b1_d = nc.dram_tensor("b1c", [P, NFC], F32, kind="ExternalInput").ap()
    b2_d = nc.dram_tensor("b2c", [P, DC], F32, kind="ExternalInput").ap()
    mask_d = nc.dram_tensor("masks", [4, P, 512], BF16, kind="ExternalInput").ap()
    y_out = nc.dram_tensor("y_fm", [D, T], F32, kind="ExternalOutput").ap()

    cc_in1 = nc.dram_tensor("cc_in1", [1, 8], F32)
    cc_out1 = nc.dram_tensor("cc_out1", [1, 8], F32)
    cc_in2 = nc.dram_tensor("cc_in2", [1, 8], F32)
    cc_out2 = nc.dram_tensor("cc_out2", [1, 8], F32)
    h3buf = nc.dram_tensor("h3buf", [NT4, NFC, P, 512], F32).ap()
    xbuf = nc.dram_tensor("xbuf", [DC, P, T], F32).ap()
    rowbuf = nc.dram_tensor("rowbuf", [1, 8 * D], F32).ap()

    AL = lambda name, shape, dt, off: nc.alloc_sbuf_tensor_at(name, shape, dt, offset=off)
    x_sb = AL("x_sb", [P, DC * T], F32, OFF_X)
    q_sb = AL("q_sb", [P, 2 * T], BF16, OFF_QSB)
    k_sb = AL("k_sb", [P, 2 * T], BF16, OFF_KSB)
    v_sb = AL("v_sb", [P, 16 * 260], BF16, OFF_VSB)
    u_sb = AL("u_sb", [P, 8 * T], BF16, OFF_USB)
    wq1t_sb = AL("wq1t_sb", [P, DC * FF], BF16, OFF_WQ1T)
    wq2t_sb = AL("wq2t_sb", [P, NFC * D], BF16, OFF_WQ2T)
    h3q_sb = [AL("h3q0_sb", [P, NFC * 512], BF16, OFF_H3Q0),
              AL("h3q1_sb", [P, NFC * 512], BF16, OFF_H3Q1)]
    h1_sb = AL("h1_sb", [P, DC * T], BF16, OFF_C)
    hq_sb = AL("hq_sb", [P, DC * T], BF16, OFF_C)
    modrow = AL("modrow", [1, 6 * D], F32, OFF_MODROW)

    xbf_stg = [AL("xbf_stg0", [P, T], BF16, OFF_XBF), AL("xbf_stg1", [P, T], BF16, OFF_XBF + 4096)]
    xsq_stg = [AL("xsq_stg0", [P, T], BF16, OFF_XBF + 8192), AL("xsq_stg1", [P, T], BF16, OFF_XBF + 12288)]
    wmod_stg = [AL("wmod_stg0", [P, DC * 512], BF16, OFF_D),
                AL("wmod_stg1", [P, DC * 512], BF16, OFF_D + 8192)]
    wmod32_stg = AL("wmod32_stg", [P, DC * 512], F32, OFF_D + 16384)
    wstg = AL("wstg", [P, 3 * DC * 256], BF16, OFF_D)
    qsq_stg = AL("qsq_stg", [P, T], BF16, OFF_D + 28672)
    wo_stg = AL("wo_stg", [P, DC * D], BF16, OFF_D + 12288)
    tmp_stg = AL("tmp_stg", [P, T], F32, OFF_D)
    h2stg = AL("h2stg", [P, T], F32, OFF_D + 8192)
    w1tstg = [AL("w1tstg0", [P, 512], F32, OFF_D + 16384), AL("w1tstg1", [P, 512], F32, OFF_D + 18432)]
    i32stg = AL("i32stg", [P, 512], I32, OFF_D + 20480)
    rwstg = [AL("rwstg0", [1, 512], F32, OFF_D + 22528), AL("rwstg1", [1, 512], F32, OFF_ROWS)]
    h3stg = [AL("h3stg0", [P, 512], F32, OFF_D + 24576), AL("h3stg1", [P, 512], F32, OFF_D + 26624)]
    w2nstg = [AL("w2nstg0", [P, D], F32, OFF_D + 16384), AL("w2nstg1", [P, D], F32, OFF_D + 20480)]
    w2tstg = [AL("w2tstg0", [P, 512], F32, OFF_D + 28672), AL("w2tstg1", [P, 512], F32, OFF_D + 30720)]
    rbstg = [AL("rbstg0", [P, 512], F32, OFF_D), AL("rbstg1", [P, 512], F32, OFF_D + 2048)]
    op1stg = [AL("op1stg0", [P, 512], F32, OFF_D), AL("op1stg1", [P, 512], F32, OFF_D + 2048)]
    op2stg = [AL("op2stg0", [P, 512], F32, OFF_D + 4096), AL("op2stg1", [P, 512], F32, OFF_D + 6144)]
    i32b_stg = AL("i32b_stg", [P, 512], I32, OFF_D + 4096)
    o1stg = [AL("o1stg0", [P, 512], F32, OFF_D + 8192), AL("o1stg1", [P, 512], F32, OFF_D + 18432)]
    o2stg = [AL("o2stg0", [P, 512], F32, OFF_D + 10240), AL("o2stg1", [P, 512], F32, OFF_D + 12288)]
    xinstg = [AL("xinstg0", [P, 512], F32, OFF_D + 14336), AL("xinstg1", [P, 512], F32, OFF_D + 16384)]

    sxrow = AL("sxrow", [1, T], F32, OFF_SXROW)
    sxxrow = AL("sxxrow", [1, T], F32, OFF_SXXROW)
    srow = AL("srow", [1, 512], F32, OFF_ROWS)
    rrow = AL("rrow", [1, 512], F32, OFF_ROWS + 2048)
    a_tiles = [AL(f"a_tile{i}", [P, 512], BF16, OFF_ROWS + 4096 + i * 1024) for i in range(4)]
    rb_sb = AL("rb_sb", [P, 512], F32, OFF_ROWS + 8192)
    mqrow = AL("mqrow", [1, 64], F32, OFF_ROWS + 10240)
    mhold = AL("mhold", [1, 64], F32, OFF_ROWS + 10496)
    negcrow = AL("negcrow", [1, NH], F32, OFF_ROWS + 11264)
    mhold2 = AL("mhold2", [1, 2 * T], F32, OFF_MH2)

    e = OFF_E
    def EA(name, shape, dt):
        nonlocal e
        t = AL(name, shape, dt, e)
        itm = {BF16: 2, F32: 4, I32: 4}[dt]
        nbytes = int(np.prod(shape[1:])) * itm
        e += -(-nbytes // 32) * 32
        return t
    masks_sb = AL("masks_sb", [P, 4 * 512], BF16, OFF_MASKS)
    sa_pp = EA("sa_pp", [P, DC], F32)
    sb_pp = EA("sb_pp", [P, DC], F32)
    ga_pp = EA("ga_pp", [P, DC], F32)
    gb_pp = EA("gb_pp", [P, DC], F32)
    capp_pp = EA("capp_pp", [P, DC], F32)
    cbpp_pp = EA("cbpp_pp", [P, DC], F32)
    cbr1_pp = EA("cbr1_pp", [P, DC], F32)
    sbr1_pp = EA("sbr1_pp", [P, DC], F32)
    ccol = EA("ccol", [P, DC], F32)
    scol = EA("scol", [P, DC], BF16)
    scol32 = EA("scol32", [P, DC], F32)
    ones_bf = EA("ones_bf", [P, 8], BF16)
    onesrow_f = EA("onesrow_f", [1, P], F32)
    negc_pp = EA("negc_pp", [P, NH], F32)
    sw1 = EA("sw1", [P, NFC], F32)
    sw1tmp = EA("sw1tmp", [P, NFC], F32)
    sw2tmp = EA("sw2tmp", [P, DC], F32)
    rw1 = EA("rw1", [P, NFC], F32)
    p1c = EA("p1c", [P, NFC], F32)
    pb1c = EA("pb1c", [P, NFC], F32)
    b1sb = EA("b1sb", [P, NFC], F32)
    bq1f = EA("bq1f", [P, NFC], F32)
    i32r = EA("i32r", [P, NFC], I32)
    sw2 = EA("sw2", [P, DC], F32)
    rw2 = EA("rw2", [P, DC], F32)
    p2c = EA("p2c", [P, DC], F32)
    p2gb = EA("p2gb", [P, DC], F32)
    p2gbb = EA("p2gbb", [P, DC], F32)
    b2sb = EA("b2sb", [P, DC], F32)
    bq2f = EA("bq2f", [P, DC], F32)
    i32r2 = EA("i32r2", [P, DC], I32)
    s1pp = EA("s1pp", [P, 1], F32)
    s2pp = EA("s2pp", [P, 1], F32)
    r1pp = EA("r1pp", [P, 1], F32)
    r2pp = EA("r2pp", [P, 1], F32)
    mx8 = EA("mx8", [P, DC], F32)
    mxf = EA("mxf", [P, NT4 * NFC], F32)
    gmaxs = EA("gmaxs", [1, 8], F32)
    tiny = EA("tiny", [1, 8], F32)
    assert e <= 224 * 1024 - 32, e

    psA = nc.alloc_psum_tensor("psA", [P, T], F32)
    psB = nc.alloc_psum_tensor("psB", [P, T], F32)

    dbg_out = {}
    for name, shp in (("mod", [1, 6 * D]), ("h1", [D, T]), ("rows", [2, T]), ("raw", [2, T]), ("u", [D, T]),
                      ("negc", [1, NH]), ("x2", [D, T]), ("hq", [D, T]),
                      ("sc1", [1, 8]), ("mxdbg", [4, P]), ("nbdbg", [2, 64]), ("mx2dbg", [4, P]), ("wq1dbg", [P, DC * FF]),
                      ("h3dbg", [NT4, NFC, P, 512]), ("sc2", [1, 8]), ("wq2dbg", [P, NFC * D]), ("b2dbg", [6, P])):
        if name in dbg:
            dbg_out[name] = nc.dram_tensor("dbg_" + name, shp, F32, kind="ExternalOutput").ap()

    C = {}
    def ctr(name):
        C[name] = Ctr(nc.alloc_semaphore(name))
        return C[name]
    for n in ("ld0", "silu", "m32", "mw", "mwf", "mr", "me", "pp", "vone",
              "l1sv", "l1ss", "l1sf", "l1mm", "l1rw",
              "qw", "qwf", "pjr", "pjf", "nbs", "nbm", "nbv", "nbf",
              "ats", "atx", "ata", "ate", "atf", "ato", "atr", "atb", "atrb", "atu",
              "opr", "opv", "opf", "wo_ok",
              "l2sv", "l2ss", "l2sf", "l2mm", "l2rw",
              "mxv", "cc1", "cc1c", "sc1c", "gh2", "rqv", "xo",
              "w1ld", "w1mx", "w1rw", "w1rl", "w1bc", "w1qv", "w1f",
              "f1r", "f1s", "f1v", "f1d", "f1free",
              "w2ld", "w2mx", "w2rw", "w2rl", "w2bc", "w2qv", "w2f",
              "cc2", "cc2c", "sc2c", "b2p",
              "rbl", "rbq", "f2r", "f2s", "f2v", "f2x", "yo", "dbgc"):
        ctr(n)

    def pair(name):
        return (ctr(name + "_a"), ctr(name + "_b"))
    mwp = pair("mwp"); w1mp = pair("w1mp"); w1qp = pair("w1qp"); w1rp = pair("w1rp")
    f1hp = pair("f1hp"); w2mp = pair("w2mp"); w2qp = pair("w2qp"); w2rp = pair("w2rp")
    rblp = pair("rblp"); f2xp = pair("f2xp"); yop = pair("yop")

    def dma2(pl, eng, out, in_, pr, idx, waits=(), ncd=False):
        c = pr[idx % 2]
        v = c.bump(16)
        pl.op(eng, lambda e, o=out, i=in_, cc=c, ncd=ncd: e.dma_start(
            out=o, in_=i, allow_slow_non_contiguous=ncd).then_inc(cc.sem, 16), waits=waits)
        return (c, v)

    def dma(pl, eng, out, in_, c, ncd=False):
        v = c.bump(16)
        pl.op(eng, lambda e, o=out, i=in_, cc=c, ncd=ncd: e.dma_start(
            out=o, in_=i, allow_slow_non_contiguous=ncd).then_inc(cc.sem, 16))
        return v

    def dump_sbuf(name, pieces):
        if name not in dbg_out:
            return
        pl = Plan()
        c = C["dbgc"]
        for dr, sb in pieces:
            dma(pl, "gpsimd", dr, sb, c)
        tgt = c.n
        pl.op("gpsimd", lambda e, t=tgt: None, waits=[(c, tgt)])
        pl.emit(nc)

    pl = Plan()
    ld0, silu_c, mw, mwf, mr, me = C["ld0"], C["silu"], C["mw"], C["mwf"], C["mr"], C["me"]
    for dc in range(DC):
        dma(pl, "sync", x_sb[:, dc * T:(dc + 1) * T], x_in[dc * P:(dc + 1) * P, :], ld0)
    dma(pl, "sync", masks_sb[:].rearrange("p (m c) -> p m c", c=512), mask_d[:, :, :].transpose([1, 0, 2]), ld0)
    dma(pl, "sync", ccol[:, :], crow_d[0:1, :].rearrange("o (c p) -> p (o c)", p=P), ld0, ncd=True)
    ld0_all = ld0.n
    pl.op("gpsimd", lambda e: e.memset(ones_bf[:], 1.0))
    pl.op("gpsimd", lambda e: e.memset(ones_bf[:, 1:3], 0.0))
    pl.op("gpsimd", lambda e: e.memset(ones_bf[0:64, 1:2], 1.0))
    pl.op("gpsimd", lambda e: e.memset(ones_bf[64:128, 2:3], 1.0))
    pl.op("gpsimd", lambda e: e.memset(onesrow_f[:], 1.0))
    pl.op("gpsimd", lambda e: e.memset(tiny[:], 0.0))
    mw_waits = []
    J32 = (4, 5, 10, 11)
    for j in range(0 if skip_mod == "all" else 12):
        waits = ([(mr, j - 1)] if j >= 2 else []) if not skip_mod else []
        if j in J32:
            c = C["m32"]
            w32 = ([(mr, j)] if j >= 1 else []) if not skip_mod else []
            for dc in range(DC):
                v = c.bump(16)
                pl.op("sync", lambda e, j=j, dc=dc, cc=c: e.dma_start(
                    out=wmod32_stg[:, dc * 512:(dc + 1) * 512],
                    in_=wmod_d[dc * P:(dc + 1) * P, j * 512:(j + 1) * 512]).then_inc(cc.sem, 16),
                    waits=w32 if dc == 0 else [])
            mw_waits.append((c, c.n))
        else:
            c = mwp[j % 2]
            for dc in range(DC):
                v = c.bump(16)
                pl.op("gpsimd", lambda e, j=j, dc=dc, cc=c: e.dma_start(
                    out=wmod_stg[j % 2][:, dc * 512:(dc + 1) * 512],
                    in_=wmod_d[dc * P:(dc + 1) * P, j * 512:(j + 1) * 512]).then_inc(cc.sem, 16),
                    waits=waits if dc == 0 else [])
            mw_waits.append((c, c.n))
    pl.op("scalar", lambda e: (e.activation(out=scol32[:, :], in_=ccol[:, :], func=AF.Silu),
                               dinc(e, silu_c.sem)), waits=[(ld0, ld0_all)])
    silu_c.bump()
    pl.op("vector", lambda e: (e.tensor_copy(out=scol[:, :], in_=scol32[:, :]),
                               dinc(e, silu_c.sem)), waits=[(silu_c, silu_c.n)])
    silu_c.bump()
    for j in range(12 if not skip_mod else 0):
        waits = [mw_waits[j]]
        if j <= 1:
            waits.append((silu_c, 2))
        if j >= 4:
            waits.append((me, j - 3))
        def mm_mod(e, j=j):
            if j in J32:
                for dc in range(DC):
                    ins = e.matmul(psA[0:1, (j % 4) * 512:(j % 4) * 512 + 512],
                                   scol32[:, dc:dc + 1],
                                   wmod32_stg[:, dc * 512:(dc + 1) * 512],
                                   start=(dc == 0), stop=(dc == DC - 1))
            else:
                for dc in range(DC):
                    ins = e.matmul(psA[0:1, (j % 4) * 512:(j % 4) * 512 + 512],
                                   scol[:, dc:dc + 1],
                                   wmod_stg[j % 2][:, dc * 512:(dc + 1) * 512],
                                   start=(dc == 0), stop=(dc == DC - 1))
            ins.then_inc(mr.sem, 1)
        pl.op("tensor", mm_mod, waits=waits)
        mr.bump()
        def ev_mod(e, j=j):
            e.tensor_copy(out=modrow[0:1, j * 512:(j + 1) * 512],
                          in_=psA[0:1, (j % 4) * 512:(j % 4) * 512 + 512])
            dinc(e, me.sem)
        pl.op("vector", ev_mod, waits=[(mr, j + 1)])
        me.bump()
    if skip_mod:
        pl.op("vector", lambda e: e.memset(modrow[:, :], 0.01))
    pl.emit(nc)

    pl = Plan()
    pp = C["pp"]
    def mod_post(e):
        e = DrainWrap(e)
        S = h2stg[0:1, 0:D]
        S2 = tmp_stg[0:1, 0:D]
        S3 = h2stg[0:1, D:2 * D]
        e.tensor_scalar(out=S2, in0=modrow[0:1, D:2 * D], scalar1=1.0, scalar2=None, op0=ALU.add)
        e.tensor_copy(out=modrow[0:1, D:2 * D], in_=S2)
        e.tensor_scalar(out=S2, in0=modrow[0:1, 4 * D:5 * D], scalar1=1.0, scalar2=None, op0=ALU.add)
        e.tensor_copy(out=modrow[0:1, 4 * D:5 * D], in_=S2)
        for off in (2 * D, 5 * D):
            gsl = modrow[0:1, off:off + D]
            e.tensor_scalar(out=S, in0=gsl, scalar1=0.0, scalar2=None, op0=ALU.is_ge)
            e.tensor_scalar(out=S2, in0=S, scalar1=2.0, scalar2=-1.0, op0=ALU.mult, op1=ALU.add)
            e.tensor_scalar(out=S, in0=gsl, scalar1=-1.0, scalar2=None, op0=ALU.mult)
            e.tensor_tensor(out=S3, in0=gsl, in1=S, op=ALU.max)
            e.tensor_scalar(out=S, in0=S3, scalar1=0.25, scalar2=None, op0=ALU.max)
            e.tensor_tensor(out=gsl, in0=S2, in1=S, op=ALU.mult)
        dinc(e, pp.sem)
    pl.op("vector", mod_post)
    pp.bump()
    order = [(0, sa_pp), (3 * D, sb_pp), (2 * D, ga_pp), (5 * D, gb_pp), (D, capp_pp), (4 * D, cbpp_pp)]
    w0 = pp.n
    for i, (off, _) in enumerate(order):
        dma(pl, "sync", rowbuf[0:1, i * D:(i + 1) * D], modrow[0:1, off:off + D], pp)
        if i == 0:
            pl.ops["sync"][-1] = ((pl.ops["sync"][-1][0] + ((pp, w0),)), pl.ops["sync"][-1][1])
    wmid = pp.n
    pl.op("sync", lambda e: None, waits=[(pp, wmid)])
    for i, (_, dst) in enumerate(order):
        dma(pl, "sync", dst[:, :], rowbuf[0:1, i * D:(i + 1) * D].rearrange("o (c p) -> p (o c)", p=P), pp, ncd=True)
    wend = pp.n
    pl.op("sync", lambda e: None, waits=[(pp, wend)])
    pl.emit(nc)

    if "mod" in dbg_out:
        dump_sbuf("mod", [(dbg_out["mod"][:, :], modrow[:, :])])

    def ln_stats(pl, pref, xsrc):
        sv, ss, sf, smm, srw = C[pref + "sv"], C[pref + "ss"], C[pref + "sf"], C[pref + "mm"], C[pref + "rw"]
        base_sf = sf.n
        for dc in range(DC):
            waits = [(sf, base_sf + dc - 1)] if dc >= 2 else []
            pl.op("vector", lambda e, dc=dc: (e.tensor_copy(
                out=xbf_stg[dc % 2][:, :], in_=xsrc[:, dc * T:(dc + 1) * T]), dinc(e, sv.sem)), waits=waits)
            sv.bump()
            pl.op("scalar", lambda e, dc=dc: (e.square(
                out=xsq_stg[dc % 2][:, :], in_=xsrc[:, dc * T:(dc + 1) * T]), dinc(e, ss.sem)), waits=waits)
            ss.bump()
            def mm_st(e, dc=dc):
                for tc in range(NT4):
                    e.matmul(psA[0:1, tc * 512:tc * 512 + 512], ones_bf[:, 0:1],
                             xbf_stg[dc % 2][:, tc * 512:tc * 512 + 512], start=(dc == 0), stop=(dc == DC - 1))
                    ins = e.matmul(psB[0:1, tc * 512:tc * 512 + 512], ones_bf[:, 0:1],
                                   xsq_stg[dc % 2][:, tc * 512:tc * 512 + 512], start=(dc == 0), stop=(dc == DC - 1))
                ins.then_inc(sf.sem, 1)
            pl.op("tensor", mm_st, waits=[(sv, sv.n), (ss, ss.n)])
            sf.bump()
        def rows1(e):
            e = DrainWrap(e)
            T4 = tmp_stg[0:1, 0:T]
            S = h2stg[0:1, 0:T]
            e.tensor_copy(out=sxrow[0:1, :], in_=psA[0:1, 0:T])
            e.tensor_copy(out=sxxrow[0:1, :], in_=psB[0:1, 0:T])
            if "raw" in dbg_out and pref == "l1":
                e.tensor_copy(out=mhold2[0:1, 0:T], in_=psA[0:1, 0:T])
                e.tensor_copy(out=mhold2[0:1, T:2 * T], in_=psB[0:1, 0:T])
            
            e.tensor_scalar(out=T4, in0=sxrow[0:1, :], scalar1=-1.0 / D, scalar2=None, op0=ALU.mult)
            e.tensor_scalar(out=S, in0=sxxrow[0:1, :], scalar1=1.0 / D, scalar2=None, op0=ALU.mult)
            e.tensor_tensor(out=sxrow[0:1, :], in0=T4, in1=T4, op=ALU.mult)
            e.tensor_tensor(out=sxxrow[0:1, :], in0=S, in1=sxrow[0:1, :], op=ALU.subtract)
            e.tensor_scalar(out=S, in0=sxxrow[0:1, :], scalar1=EPS_LN, scalar2=None, op0=ALU.add)
            e.reciprocal(out=sxxrow[0:1, :], in_=S)
            dinc(e, srw.sem)
        pl.op("vector", rows1, waits=[(sf, base_sf + DC)])
        srw.bump()
        pl.op("scalar", lambda e: (e.activation(out=h2stg[0:1, 0:T], in_=sxxrow[0:1, :],
                                                func=AF.Sqrt), dinc(e, srw.sem)), waits=[(srw, srw.n)])
        srw.bump()
        def rows2(e):
            e = DrainWrap(e)
            T4 = tmp_stg[0:1, 0:T]
            S = h2stg[0:1, 0:T]
            e.tensor_tensor(out=sxrow[0:1, :], in0=T4, in1=S, op=ALU.mult)
            e.tensor_copy(out=sxxrow[0:1, :], in_=S)
            dinc(e, srw.sem)
        pl.op("vector", rows2, waits=[(srw, srw.n)])
        srw.bump()

    def gh_bcast(pl, srw_like):
        def ghm(e):
            for tc in range(NT4):
                e.matmul(psA[:, tc * 512:tc * 512 + 512], onesrow_f[0:1, :],
                         sxxrow[0:1, tc * 512:tc * 512 + 512], start=True, stop=True)
                ins = e.matmul(psB[:, tc * 512:tc * 512 + 512], onesrow_f[0:1, :],
                               sxrow[0:1, tc * 512:tc * 512 + 512], start=True, stop=True)
            ins.then_inc(srw_like.sem, 1)
        pl.op("tensor", ghm, waits=[(srw_like, srw_like.n)])
        srw_like.bump()

    pl = Plan()
    ln_stats(pl, "l1", x_sb)
    gh_bcast(pl, C["l1rw"])
    ghv = C["l1rw"].n
    pl.emit(nc)
    if "rows" in dbg_out:
        dump_sbuf("rows", [(dbg_out["rows"][0:1, :], sxxrow[0:1, :]), (dbg_out["rows"][1:2, :], sxrow[0:1, :])])
    if "raw" in dbg_out:
        dump_sbuf("raw", [(dbg_out["raw"][0:1, :], mhold2[0:1, 0:T]), (dbg_out["raw"][1:2, :], mhold2[0:1, T:2 * T])])
    pl = Plan()
    def apply1(e):
        for dc in range(DC):
            e.tensor_tensor(out=tmp_stg[:, :], in0=x_sb[:, dc * T:(dc + 1) * T], in1=psA[:, 0:T], op=ALU.mult)
            e.tensor_tensor(out=h2stg[:, :], in0=tmp_stg[:, :], in1=psB[:, 0:T], op=ALU.add)
            e.tensor_scalar(out=h1_sb[:, dc * T:(dc + 1) * T], in0=h2stg[:, :],
                            scalar1=capp_pp[:, dc:dc + 1], scalar2=sa_pp[:, dc:dc + 1],
                            op0=ALU.mult, op1=ALU.add)
    pl.op("vector", apply1, waits=[(C["l1rw"], ghv)])
    pl.emit(nc)

    dump_sbuf("h1", [(dbg_out.get("h1") and dbg_out["h1"][dc * P:(dc + 1) * P, :], h1_sb[:, dc * T:(dc + 1) * T])
                     for dc in range(DC)] if "h1" in dbg_out else [])
    if stage < 2:
        return nc, dbg_out

    pl = Plan()
    qw, qwf = C["qw"], C["qwf"]
    vone = C["vone"]
    pl.op("gpsimd", lambda e: (e.memset(
        v_sb[:].rearrange("p (t h c) -> p t h c", h=4, c=65)[:, :, :, 64:65], 1.0), dinc(e, vone.sem)))
    vone.bump()
    pjr, pjf = C["pjr"], C["pjf"]
    nbs, nbm, nbv, nbf = C["nbs"], C["nbm"], C["nbv"], C["nbf"]
    ats, atx, ata, ate, atf, ato = C["ats"], C["atx"], C["ata"], C["ate"], C["atf"], C["ato"]
    atr, atb, atrb, atu = C["atr"], C["atb"], C["atrb"], C["atu"]

    NPJ_CONST = 32
    for qt in range(4):
        waits = [(pjf, NPJ_CONST * qt)] if qt >= 1 else []
        c0 = qt * 256
        for wi, wd in enumerate((wq_d, wk_d, wv_d)):
            for dc in range(DC):
                qw.bump(16)
                pl.op("gpsimd", lambda e, wd=wd, wi=wi, dc=dc, c0=c0: e.dma_start(
                    out=wstg[:, wi * 2048 + dc * 256:wi * 2048 + (dc + 1) * 256],
                    in_=wd[dc * P:(dc + 1) * P, c0:c0 + 256]).then_inc(qw.sem, 16),
                    waits=waits if (wi == 0 and dc == 0) else [])
    wo_ok = C["wo_ok"]
    def wo_load(e):
        for fc in range(DC):
            e.dma_start(out=wo_stg[:, fc * D:(fc + 1) * D],
                        in_=wo_d[fc * P:(fc + 1) * P, :]).then_inc(wo_ok.sem, 16)
    pl.op("gpsimd", wo_load)
    wo_ok.bump(16 * DC)

    proj_tiles = [("q", oc, tc) for oc in range(2) for tc in range(NT4)] + \
                 [("k", oc, tc) for oc in range(2) for tc in range(NT4)] + \
                 [("v", tt, 0) for tt in range(16)]
    NPJ = len(proj_tiles)

    for qt in range(4):
        for i, (kind, a, tc) in enumerate(proj_tiles):
            gi = qt * NPJ + i
            waits = [(qw, 384 * (qt + 1))] if i == 0 else []
            if i == 0 and qt >= 1:
                waits.append((atu, 16 * qt))
            if gi >= 4:
                waits.append((pjf, gi - 3))
            sl = psB[:, (gi % 4) * 512:(gi % 4) * 512 + 512]
            def mm_pj(e, kind=kind, a=a, tc=tc, sl=sl):
                for dc in range(DC):
                    if kind == "q":
                        ins = e.matmul(sl, wstg[:, dc * 256 + a * 128:dc * 256 + a * 128 + 128],
                                 h1_sb[:, dc * T + tc * 512:dc * T + tc * 512 + 512],
                                 start=(dc == 0), stop=(dc == DC - 1))
                    elif kind == "k":
                        ins = e.matmul(sl, wstg[:, 2048 + dc * 256 + a * 128:2048 + dc * 256 + a * 128 + 128],
                                 h1_sb[:, dc * T + tc * 512:dc * T + tc * 512 + 512],
                                 start=(dc == 0), stop=(dc == DC - 1))
                    else:
                        ins = e.matmul(sl[0:P, 0:256], h1_sb[:, dc * T + a * 128:dc * T + a * 128 + 128],
                                 wstg[:, 4096 + dc * 256:4096 + dc * 256 + 256],
                                 start=(dc == 0), stop=(dc == DC - 1))
                ins.then_inc(pjr.sem, 1)
            pl.op("tensor", mm_pj, waits=waits)
            pjr.bump()
            def ev_pj(e, kind=kind, a=a, tc=tc, sl=sl):
                if kind == "q":
                    ins = e.tensor_copy(out=q_sb[:, a * T + tc * 512:a * T + tc * 512 + 512], in_=sl)
                elif kind == "k":
                    ins = e.tensor_copy(out=k_sb[:, a * T + tc * 512:a * T + tc * 512 + 512], in_=sl)
                else:
                    ins = e.tensor_copy(out=v_sb[:, a * 260:a * 260 + 260].rearrange("p (h c) -> p h c", c=65)[:, :, 0:64],
                                  in_=sl[0:P, 0:256].rearrange("p (h c) -> p h c", c=64))
                dinc(e, pjf.sem)
            pl.op("vector", ev_pj, waits=[(pjr, pjr.n)])
            pjf.bump()

        for half in range(4):
            src = q_sb if half < 2 else k_sb
            oc = half % 2
            pl.op("scalar", lambda e, src=src, oc=oc: (e.square(
                out=qsq_stg[:, :], in_=src[:, oc * T:(oc + 1) * T]), dinc(e, nbs.sem)),
                waits=[(pjf, qt * NPJ + 4 * (half + 1))] + ([(nbm, nbm.n)] if half > 0 else []))
            nbs.bump()
            for hh in range(2):
                def mm_nb2(e, hh=hh):
                    for tc in range(NT4):
                        ins = e.matmul(psA[0:1, tc * 512:tc * 512 + 512],
                                 ones_bf[:, 1 + hh:2 + hh],
                                 qsq_stg[:, tc * 512:tc * 512 + 512],
                                 start=True, stop=True)
                    ins.then_inc(nbm.sem, 1)
                w = [(nbs, nbs.n), (nbv, nbv.n)] if hh == 0 else [(nbv, nbv.n)]
                if half == 0 and hh == 0 and qt >= 1:
                    w.append((atx, 80 * qt))
                pl.op("tensor", mm_nb2, waits=w)
                nbm.bump()
                def rd_nb(e, half=half, hh=hh):
                    col = (half * 2 + hh)
                    for tc in range(NT4):
                        e.reduce_max(out=mqrow[0:1, col * 8 + tc:col * 8 + tc + 1],
                                     in_=psA[0:1, tc * 512:tc * 512 + 512], axis=AX.X)
                    dinc(e, nbv.sem)
                pl.op("vector", rd_nb, waits=[(nbm, nbm.n)])
                nbv.bump()
        def nb_comb(e, qt=qt):
            e = DrainWrap(e)
            for hl in range(4):
                e.reduce_max(out=mhold[0:1, hl * 8:hl * 8 + 1], in_=mqrow[0:1, hl * 8:hl * 8 + 4], axis=AX.X)
                e.reduce_max(out=mhold[0:1, hl * 8 + 1:hl * 8 + 2], in_=mqrow[0:1, 32 + hl * 8:32 + hl * 8 + 4], axis=AX.X)
            e.tensor_tensor(out=mhold[0:1, :].rearrange("o (h c) -> o h c", c=8)[:, 0:4, 2:3],
                            in0=mhold[0:1, :].rearrange("o (h c) -> o h c", c=8)[:, 0:4, 0:1],
                            in1=mhold[0:1, :].rearrange("o (h c) -> o h c", c=8)[:, 0:4, 1:2],
                            op=ALU.mult)
            dinc(e, nbv.sem)
        pl.op("vector", nb_comb)
        nbv.bump()
        pl.op("scalar", lambda e: (e.activation(
            out=mhold[0:1, :].rearrange("o (h c) -> o h c", c=8)[:, 0:4, 4:5],
            in_=mhold[0:1, :].rearrange("o (h c) -> o h c", c=8)[:, 0:4, 2:3],
            func=AF.Sqrt), dinc(e, nbs.sem)), waits=[(nbv, nbv.n)])
        nbs.bump()
        def nb_scale(e, qt=qt):
            e = DrainWrap(e)
            if qt == 0 and "nbdbg" in dbg_out:
                e.tensor_copy(out=mhold2[0:1, 0:64], in_=mqrow[0:1, :])
                e.tensor_copy(out=mhold2[0:1, 64:128], in_=mhold[0:1, :])
            e.tensor_scalar(out=negcrow[0:1, qt * 4:qt * 4 + 4],
                            in0=mhold[0:1, :].rearrange("o (h c) -> o h c", c=8)[:, 0:4, 4:5],
                            scalar1=-0.125, scalar2=None, op0=ALU.mult)
            dinc(e, nbv.sem)
        pl.op("vector", nb_scale, waits=[(nbs, nbs.n)])
        nbv.bump()
        def nb_bc(e, qt=qt):
            e.matmul(psA[:, 0:512][0:P, 0:4], onesrow_f[0:1, :], negcrow[0:1, qt * 4:qt * 4 + 4],
                     start=True, stop=True).then_inc(nbm.sem, 1)
        pl.op("tensor", nb_bc, waits=[(nbv, nbv.n)])
        nbm.bump()
        def nb_ev(e, qt=qt):
            e.tensor_copy(out=negc_pp[:, qt * 4:qt * 4 + 4],
                          in_=psA[:, 0:512][0:P, 0:4])
            dinc(e, nbf.sem)
        pl.op("vector", nb_ev, waits=[(nbm, nbm.n)])
        nbf.bump()

        for pr in range(2):
            for qc in range(NT4):
                nkt = 4 * qc + 4
                uev_before = atu.n
                for kt in range(nkt):
                    gi = ats.n
                    waits = []
                    if gi == qt * 80:
                        waits.append((nbf, qt + 1))
                    if gi >= 2:
                        waits.append((atx, gi - 1))
                    sl1 = psA[:, (gi % 2) * 512:(gi % 2) * 512 + 512]
                    sl2 = psA[:, 1024 + (gi % 2) * 512:1024 + (gi % 2) * 512 + 512]
                    def mm_s(e, pr=pr, qc=qc, kt=kt, sl1=sl1, sl2=sl2):
                        e.matmul(sl1, k_sb[0:64, pr * T + kt * 128:pr * T + kt * 128 + 128],
                                 q_sb[0:64, pr * T + qc * 512:pr * T + qc * 512 + 512], start=True, stop=True)
                        e.matmul(sl2, k_sb[64:128, pr * T + kt * 128:pr * T + kt * 128 + 128],
                                 q_sb[64:128, pr * T + qc * 512:pr * T + qc * 512 + 512],
                                 start=True, stop=True).then_inc(ats.sem, 1)
                    pl.op("tensor", mm_s, waits=waits)
                    ats.bump()
                    hg = 4 * qt + 2 * pr
                    ewaits = [(ats, ats.n)]
                    if gi >= 2:
                        ewaits.append((atf, gi - 1))
                    diag = kt >= 4 * qc
                    def ex_s(e, gi=gi, hg=hg, sl1=sl1, sl2=sl2):
                        e.activation(out=a_tiles[gi % 2][:, :], in_=sl1, func=AF.Exp,
                                     bias=negc_pp[:, hg:hg + 1], scale=0.125)
                        e.activation(out=a_tiles[2 + gi % 2][:, :], in_=sl2, func=AF.Exp,
                                     bias=negc_pp[:, hg + 1:hg + 2], scale=0.125)
                        dinc(e, atx.sem)
                    pl.op("scalar", ex_s, waits=ewaits)
                    atx.bump()
                    if diag:
                        mi = kt - 4 * qc
                        def mk_s(e, gi=gi, mi=mi):
                            e.tensor_tensor(out=a_tiles[gi % 2][:, :], in0=a_tiles[gi % 2][:, :],
                                            in1=masks_sb[:, mi * 512:mi * 512 + 512], op=ALU.mult)
                            e.tensor_tensor(out=a_tiles[2 + gi % 2][:, :], in0=a_tiles[2 + gi % 2][:, :],
                                            in1=masks_sb[:, mi * 512:mi * 512 + 512], op=ALU.mult)
                            dinc(e, ata.sem)
                        pl.op("vector", mk_s, waits=[(atx, atx.n)])
                        ata.bump(1)
                        a_ready = (ata, ata.n)
                    else:
                        a_ready = (atx, atx.n)
                    avw = [a_ready]
                    if kt == 0:
                        avw.append((atu, uev_before))
                        avw.append((pjf, NPJ * (qt + 1)))
                        if gi == 0:
                            avw.append((vone, 1))
                    def mm_av(e, pr=pr, kt=kt, first=(kt == 0), last=(kt == nkt - 1), gi=gi):
                        h1l = 2 * pr
                        e.matmul(psB[0:65, 0:512], v_sb[:, kt * 260 + h1l * 65:kt * 260 + h1l * 65 + 65],
                                 a_tiles[gi % 2][:, :], start=first, stop=last)
                        e.matmul(psB[0:65, 512:1024], v_sb[:, kt * 260 + (h1l + 1) * 65:kt * 260 + (h1l + 1) * 65 + 65],
                                 a_tiles[2 + gi % 2][:, :], start=first, stop=last).then_inc(atf.sem, 1)
                    pl.op("tensor", mm_av, waits=avw)
                    atf.bump()
                glast = atf.n
                for hh in range(2):
                    hg = 4 * qt + 2 * pr + hh
                    def sums_ev(e, hh=hh):
                        e = DrainWrap(e)
                        e.tensor_copy(out=srow[0:1, :], in_=psB[64:65, hh * 512:hh * 512 + 512])
                        e.reciprocal(out=rrow[0:1, :], in_=srow[0:1, :])
                        dinc(e, atr.sem)
                    w = [(atf, glast)] if hh == 0 else [(atrb, atrb.n)]
                    pl.op("vector", sums_ev, waits=w)
                    atr.bump()
                    def rb_bc(e):
                        e.matmul(psB[0:64, 1536:2048], onesrow_f[0:1, 0:64], rrow[0:1, :],
                                 start=True, stop=True).then_inc(atb.sem, 1)
                    pl.op("tensor", rb_bc, waits=[(atr, atr.n)])
                    atb.bump()
                    def u_ev(e, hh=hh, hg=hg, qc=qc):
                        e.tensor_copy(out=rb_sb[0:64, :], in_=psB[0:64, 1536:2048])
                        dinc(e, atrb.sem)
                        rbase = (hg % 2) * 64
                        e.tensor_tensor(out=u_sb[rbase:rbase + 64, (hg // 2) * T + qc * 512:(hg // 2) * T + qc * 512 + 512],
                                        in0=psB[0:64, hh * 512:hh * 512 + 512], in1=rb_sb[0:64, :],
                                        op=ALU.mult)
                        dinc(e, atu.sem)
                    pl.op("vector", u_ev, waits=[(atb, atb.n)])
                    atrb.bump(); atu.bump()
    pl.emit(nc)

    if "nbdbg" in dbg_out:
        dump_sbuf("nbdbg", [(dbg_out["nbdbg"][0:1, 0:NH], negc_pp[0:1, :]), (dbg_out["nbdbg"][1:2, 0:NH], negc_pp[127:128, :])])
    dump_sbuf("u", [(dbg_out.get("u") and dbg_out["u"][fc * P:(fc + 1) * P, :], u_sb[:, fc * T:(fc + 1) * T])
                    for fc in range(DC)] if "u" in dbg_out else [])
    if "negc" in dbg_out:
        dump_sbuf("negc", [(dbg_out["negc"][:, :], negcrow[0:1, :])])

    pl = Plan()
    opr, opv, opf = C["opr"], C["opv"], C["opf"]
    pl.op("tensor", lambda e: None, waits=[(wo_ok, wo_ok.n)])
    for dc in range(DC):
        for tc in range(NT4):
            gi = opr.n
            waits = [(opf, gi - 1)] if gi >= 2 else []
            sl = psA[:, (gi % 2) * 512:(gi % 2) * 512 + 512]
            def mm_op(e, dc=dc, tc=tc, sl=sl):
                for fc in range(DC):
                    ins = e.matmul(sl, wo_stg[:, fc * D + dc * 128:fc * D + dc * 128 + 128],
                                   u_sb[:, fc * T + tc * 512:fc * T + tc * 512 + 512],
                                   start=(fc == 0), stop=(fc == DC - 1))
                ins.then_inc(opr.sem, 1)
            pl.op("tensor", mm_op, waits=waits)
            opr.bump()
            def ev_op(e, dc=dc, tc=tc, sl=sl, gi=gi):
                e.scalar_tensor_tensor(out=op2stg[gi % 2][:, :],
                                       in0=sl, scalar=ga_pp[:, dc:dc + 1],
                                       in1=x_sb[:, dc * T + tc * 512:dc * T + tc * 512 + 512],
                                       op0=ALU.mult, op1=ALU.add)
                dinc(e, opv.sem)
            pl.op("vector", ev_op, waits=[(opr, opr.n)] + ([(opf, gi - 1)] if gi >= 2 else []))
            opv.bump()
            def wb_op(e, dc=dc, tc=tc, gi=gi):
                e.copy(out=x_sb[:, dc * T + tc * 512:dc * T + tc * 512 + 512],
                       in_=op2stg[gi % 2][:, :])
                dinc(e, opf.sem)
            pl.op("scalar", wb_op, waits=[(opv, opv.n)])
            opf.bump()
    pl.emit(nc)

    dump_sbuf("x2", [(dbg_out.get("x2") and dbg_out["x2"][dc * P:(dc + 1) * P, :], x_sb[:, dc * T:(dc + 1) * T])
                     for dc in range(DC)] if "x2" in dbg_out else [])
    if stage < 3:
        return nc, dbg_out

    pl = Plan()
    ln_stats(pl, "l2", x_sb)
    gh_bcast(pl, C["l2rw"])
    ghv2 = C["l2rw"].n
    mxv, cc1, sc1c, gh2, rqv, xo = C["mxv"], C["cc1"], C["sc1c"], C["gh2"], C["rqv"], C["xo"]
    def maxpass(e):
        for dc in range(DC):
            e.tensor_tensor(out=tmp_stg[:, :], in0=x_sb[:, dc * T:(dc + 1) * T], in1=psA[:, 0:T], op=ALU.mult)
            e.tensor_tensor(out=h2stg[:, :], in0=tmp_stg[:, :], in1=psB[:, 0:T], op=ALU.add)
            e.tensor_scalar(out=tmp_stg[:, :], in0=h2stg[:, :],
                            scalar1=cbpp_pp[:, dc:dc + 1], scalar2=sb_pp[:, dc:dc + 1],
                            op0=ALU.mult, op1=ALU.add)
            e.reduce_max(out=mx8[:, dc:dc + 1], in_=tmp_stg[:, :], axis=AX.X)
        dinc(e, mxv.sem)
    pl.op("vector", maxpass, waits=[(C["l2rw"], ghv2)])
    mxv.bump()
    dma(pl, "sync", rowbuf[0:1, 4 * D:4 * D + 1024].rearrange("o (p c) -> (o p) c", c=8), mx8[:, :], xo)
    pl.ops["sync"][-1] = ((pl.ops["sync"][-1][0] + ((mxv, mxv.n),)), pl.ops["sync"][-1][1])
    dma(pl, "sync", h2stg[0:1, 0:1024], rowbuf[0:1, 4 * D:4 * D + 1024], xo)
    pl.ops["sync"][-1] = ((pl.ops["sync"][-1][0] + ((xo, xo.n - 16),)), pl.ops["sync"][-1][1])
    xov = xo.n
    def gmax_local(e):
        e = DrainWrap(e)
        e.memset(gmaxs[0:1, :], 0.0)
        e.reduce_max(out=gmaxs[0:1, 0:1], in_=h2stg[0:1, 0:1024], axis=AX.X)
        dinc(e, mxv.sem)
    pl.op("vector", gmax_local, waits=[(xo, xov)])
    mxv.bump()
    cc1c = C["cc1c"]
    def coll1(e):
        e.dma_start(out=cc_in1.ap(), in_=gmaxs[0:1, :]).then_inc(cc1.sem, 16)
        e.wait_ge(cc1.sem, 16)
        e.collective_compute("AllReduce", ALU.max, replica_groups=[list(range(8))],
                             ins=[cc_in1.ap().opt()], outs=[cc_out1.ap().opt()]).then_inc(cc1c.sem, 1)
        e.wait_ge(cc1c.sem, 1)
        e.dma_start(out=gmaxs[0:1, :], in_=cc_out1.ap()).then_inc(cc1.sem, 16)
    pl.op("gpsimd", coll1, waits=[(mxv, mxv.n)])
    cc1.bump(32); cc1c.bump(1)
    def s1row(e):
        e = DrainWrap(e)
        e.tensor_scalar(out=tiny[0:1, 0:8], in0=gmaxs[0:1, 0:8], scalar1=EPS_Q, scalar2=1.0 / 255.0,
                        op0=ALU.max, op1=ALU.mult)
        e.reciprocal(out=mqrow[0:1, 0:8], in_=tiny[0:1, 0:8])
        dinc(e, sc1c.sem)
    pl.op("vector", s1row, waits=[(cc1, cc1.n)])
    sc1c.bump()
    def s1bc(e):
        e.matmul(psA[:, 0:512][0:P, 0:1], onesrow_f[0:1, :], tiny[0:1, 0:1], start=True, stop=True)
        e.matmul(psA[:, 512:1024][0:P, 0:1], onesrow_f[0:1, :], mqrow[0:1, 0:1],
                 start=True, stop=True).then_inc(sc1c.sem, 1)
    pl.op("tensor", s1bc, waits=[(sc1c, sc1c.n)])
    sc1c.bump()
    def s1ev(e):
        e = DrainWrap(e)
        e.tensor_copy(out=s1pp[:, 0:1], in_=psA[:, 0:512][0:P, 0:1])
        e.tensor_copy(out=r1pp[:, 0:1], in_=psA[:, 512:1024][0:P, 0:1])
        e.tensor_scalar(out=cbr1_pp[:, :], in0=cbpp_pp[:, :], scalar1=r1pp[:, 0:1], scalar2=None, op0=ALU.mult)
        e.tensor_scalar(out=sbr1_pp[:, :], in0=sb_pp[:, :], scalar1=r1pp[:, 0:1], scalar2=None,
                        op0=ALU.mult)
        dinc(e, sc1c.sem)
    pl.op("vector", s1ev, waits=[(sc1c, sc1c.n)])
    sc1c.bump()
    pl.op("tensor", lambda e: None, waits=[(sc1c, sc1c.n)])
    gh_bcast(pl, gh2)
    gh2v = gh2.n
    for dc in range(DC):
        def rq(e, dc=dc):
            e.tensor_tensor(out=tmp_stg[:, :], in0=x_sb[:, dc * T:(dc + 1) * T], in1=psA[:, 0:T], op=ALU.mult)
            e.tensor_tensor(out=h2stg[:, :], in0=tmp_stg[:, :], in1=psB[:, 0:T], op=ALU.add)
            e.tensor_scalar(out=tmp_stg[:, :], in0=h2stg[:, :],
                            scalar1=cbr1_pp[:, dc:dc + 1], scalar2=sbr1_pp[:, dc:dc + 1],
                            op0=ALU.mult, op1=ALU.add)
            for tc in range(NT4):
                e.tensor_scalar(out=i32stg[:, :], in0=tmp_stg[:, tc * 512:tc * 512 + 512],
                                scalar1=0.0, scalar2=None, op0=ALU.max)
                e.tensor_copy(out=hq_sb[:, dc * T + tc * 512:dc * T + tc * 512 + 512], in_=i32stg[:, :])
            dinc(e, rqv.sem)
        w = [(sc1c, sc1c.n), (gh2, gh2v)] if dc == 0 else []
        pl.op("vector", rq, waits=w)
        rqv.bump()
        dma(pl, "sync", xbuf[dc, :, :], x_sb[:, dc * T:(dc + 1) * T], xo)
        pl.ops["sync"][-1] = ((pl.ops["sync"][-1][0] + ((rqv, rqv.n),)), pl.ops["sync"][-1][1])
    xo_done = xo.n
    pl.op("sync", lambda e: None, waits=[(xo, xo_done)])
    pl.emit(nc)

    if "sc1" in dbg_out:
        dump_sbuf("sc1", [(dbg_out["sc1"][:, :], tiny[0:1, :])])
    if "mxdbg" in dbg_out:
        dump_sbuf("mxdbg", [(dbg_out["mxdbg"][0:1, 0:DC], mx8[0:1, :]),
                            (dbg_out["mxdbg"][2:3, 0:P], h2stg[0:1, 0:P]),
                            (dbg_out["mxdbg"][3:4, 0:8], gmaxs[0:1, :])])
    dump_sbuf("hq", [(dbg_out.get("hq") and dbg_out["hq"][dc * P:(dc + 1) * P, :], hq_sb[:, dc * T:(dc + 1) * T])
                     for dc in range(DC)] if "hq" in dbg_out else [])
    if stage < 4:
        return nc, dbg_out

    pl = Plan()
    w1ld, w1mx, w1rw, w1rl, w1bc, w1qv, w1f = (C[k] for k in ("w1ld", "w1mx", "w1rw", "w1rl", "w1bc", "w1qv", "w1f"))
    for fc in range(NFC):
        buf = tmp_stg[:, (fc % 2) * 1024:(fc % 2) * 1024 + 1024]
        w = [(w1mx, fc - 1)] if fc >= 2 else []
        lw = dma2(pl, "sync", buf, w1n_d[fc * P:(fc + 1) * P, :], w1mp, fc, waits=w)
        def mx_w1(e, fc=fc, buf=buf):
            e.reduce_max(out=sw1[:, fc:fc + 1], in_=buf, axis=AX.X, apply_absolute_value=True)
            dinc(e, w1mx.sem)
        pl.op("vector", mx_w1, waits=[lw])
        w1mx.bump()
    def sw1f(e):
        e = DrainWrap(e)
        e.tensor_scalar(out=sw1tmp[:, :], in0=sw1[:, :], scalar1=EPS_Q, scalar2=1.0 / 127.0, op0=ALU.max, op1=ALU.mult)
        e.reciprocal(out=rw1[:, :], in_=sw1tmp[:, :])
        dinc(e, w1mx.sem)
    pl.op("vector", sw1f)
    w1mx.bump()
    dma(pl, "sync", rowbuf[0:1, 4 * D:4 * D + FF].rearrange("o (a b) -> o a b", b=P).transpose([0, 2, 1]),
        rw1[:, :], w1rw, ncd=True)
    pl.ops["sync"][-1] = ((pl.ops["sync"][-1][0] + ((w1mx, w1mx.n),)), pl.ops["sync"][-1][1])
    for cc in range(8):
        rlw = dma2(pl, "sync", rwstg[cc % 2][0:1, :],
                   rowbuf[0:1, 4 * D + cc * 512:4 * D + cc * 512 + 512], w1rp, cc,
                   waits=[(w1rw, w1rw.n)] if cc == 0 else ([(w1bc, cc - 1)] if cc >= 2 else []))
        def bc_w1(e, cc=cc):
            e.matmul(psA[:, (cc % 2) * 512:(cc % 2) * 512 + 512], onesrow_f[0:1, :],
                     rwstg[cc % 2][0:1, :], start=True, stop=True).then_inc(w1bc.sem, 1)
        bw = [rlw]
        if cc >= 2:
            bw.append((w1qv, (cc - 1) * 8))
        pl.op("tensor", bc_w1, waits=bw)
        w1bc.bump()
        for dc in range(DC):
            gidx = cc * DC + dc
            qlw = dma2(pl, "sync", w1tstg[dc % 2][:, :],
                       w1t_d[dc * P:(dc + 1) * P, cc * 512:cc * 512 + 512], w1qp, gidx,
                       waits=[(w1qv, gidx - 1)] if gidx >= 2 else [])
            def q_w1(e, cc=cc, dc=dc):
                e.tensor_tensor(out=i32stg[:, :], in0=w1tstg[dc % 2][:, :],
                                in1=psA[:, (cc % 2) * 512:(cc % 2) * 512 + 512], op=ALU.mult)
                e.tensor_copy(out=wq1t_sb[:, dc * FF + cc * 512:dc * FF + cc * 512 + 512],
                              in_=i32stg[:, :])
                dinc(e, w1qv.sem)
            pl.op("vector", q_w1, waits=[qlw, (w1bc, w1bc.n)])
            w1qv.bump()
    v = w1ld.bump(16)
    pl.op("sync", lambda e: e.dma_start(out=b1sb[:, :], in_=b1_d[:, :]).then_inc(w1ld.sem, 16))
    def b1prep(e):
        e = DrainWrap(e)
        e.tensor_scalar(out=p1c[:, :], in0=sw1tmp[:, :], scalar1=s1pp[:, 0:1], scalar2=None, op0=ALU.mult)
        e.reciprocal(out=pb1c[:, :], in_=p1c[:, :])
        e.tensor_tensor(out=i32r[:, :], in0=b1sb[:, :], in1=pb1c[:, :], op=ALU.mult)
        e.tensor_copy(out=bq1f[:, :], in_=i32r[:, :])
        e.tensor_tensor(out=pb1c[:, :], in0=p1c[:, :], in1=bq1f[:, :], op=ALU.mult)
        dinc(e, w1f.sem)
    pl.op("vector", b1prep, waits=[(w1ld, v)])
    w1f.bump()
    pl.emit(nc)

    dump_sbuf("wq1dbg", [(dbg_out.get("wq1dbg") and dbg_out["wq1dbg"][:, :], wq1t_sb[:, :])] if "wq1dbg" in dbg_out else [])

    pl = Plan()
    f1r, f1s, f1v, f1d, f1free = C["f1r"], C["f1s"], C["f1v"], C["f1d"], C["f1free"]
    w2ld, w2mx, w2rw, w2rl, w2bc, w2qv, w2f = (C[k] for k in ("w2ld", "w2mx", "w2rw", "w2rl", "w2bc", "w2qv", "w2f"))
    cc2, sc2c, b2p = C["cc2"], C["sc2c"], C["b2p"]

    ti = 0
    for tc in range(NT4):
        for fc in range(NFC):
            gi = ti; ti += 1
            waits = [(f1s, gi - 3)] if gi >= 4 else []
            sl = psB[:, (gi % 4) * 512:(gi % 4) * 512 + 512]
            def mm_f1(e, tc=tc, fc=fc, sl=sl):
                for dc in range(DC):
                    ins = e.matmul(sl, wq1t_sb[:, dc * FF + fc * 128:dc * FF + fc * 128 + 128],
                                   hq_sb[:, dc * T + tc * 512:dc * T + tc * 512 + 512],
                                   start=(dc == 0), stop=(dc == DC - 1))
                ins.then_inc(f1r.sem, 1)
            pl.op("tensor", mm_f1, waits=waits)
            f1r.bump()
            sw = [(f1r, f1r.n)]
            if gi >= 2:
                sw.append((f1d, gi - 1))
            def ev_f1(e, gi=gi, fc=fc, sl=sl):
                e.activation(out=h3stg[gi % 2][:, :], in_=sl, func=AF.Relu,
                             scale=p1c[:, fc:fc + 1], bias=pb1c[:, fc:fc + 1])
                dinc(e, f1s.sem)
            pl.op("scalar", ev_f1, waits=sw)
            f1s.bump()
            def mxf1(e, gi=gi):
                e.reduce_max(out=mxf[:, gi:gi + 1], in_=h3stg[gi % 2][:, :], axis=AX.X)
                dinc(e, f1v.sem)
            pl.op("vector", mxf1, waits=[(f1s, f1s.n)])
            f1v.bump()
            stw = dma2(pl, "sync", h3buf[tc, fc, :, :], h3stg[gi % 2][:, :], f1hp, gi,
                       waits=[(f1s, f1s.n)])
            def fd_mark(e, gi=gi):
                e.nop().then_inc(f1d.sem, 1)
            pl.op("sync", fd_mark, waits=[stw, (f1v, f1v.n)])
            f1d.bump()

    for dc in range(DC):
        for piece in range(4):
            idx = dc * 4 + piece
            lw2 = dma2(pl, "sync", w2nstg[idx % 2][:, :],
                       w2n_d[dc * P:(dc + 1) * P, piece * D:piece * D + D], w2mp, idx,
                       waits=[(w2mx, idx - 1)] if idx >= 2 else [])
            def mx_w2(e, dc=dc, piece=piece, idx=idx):
                e = DrainWrap(e)
                e.reduce_max(out=mx8[:, piece:piece + 1], in_=w2nstg[idx % 2][:, :],
                             axis=AX.X, apply_absolute_value=True)
                if piece == 3:
                    e.reduce_max(out=sw2[:, dc:dc + 1], in_=mx8[:, 0:4], axis=AX.X)
                dinc(e, w2mx.sem)
            pl.op("vector", mx_w2, waits=[lw2])
            w2mx.bump()
    def sw2f(e):
        e = DrainWrap(e)
        e.tensor_scalar(out=sw2tmp[:, :], in0=sw2[:, :], scalar1=EPS_Q, scalar2=1.0 / 127.0, op0=ALU.max, op1=ALU.mult)
        e.reciprocal(out=rw2[:, :], in_=sw2tmp[:, :])
        dinc(e, w2mx.sem)
    pl.op("vector", sw2f)
    w2mx.bump()
    dma(pl, "sync", rowbuf[0:1, 0:D].rearrange("o (a b) -> o a b", b=P).transpose([0, 2, 1]), rw2[:, :], w2rw, ncd=True)
    pl.ops["sync"][-1] = ((pl.ops["sync"][-1][0] + ((w2mx, w2mx.n),)), pl.ops["sync"][-1][1])
    for cc in range(2):
        rlw2 = dma2(pl, "sync", rwstg[cc % 2][0:1, :], rowbuf[0:1, cc * 512:cc * 512 + 512], w2rp, cc,
                    waits=[(w2rw, w2rw.n)] if cc == 0 else [])
        def bc_w2(e, cc=cc):
            e.matmul(psA[:, (cc % 2) * 512:(cc % 2) * 512 + 512], onesrow_f[0:1, :],
                     rwstg[cc % 2][0:1, :], start=True, stop=True).then_inc(w2bc.sem, 1)
        pl.op("tensor", bc_w2, waits=[rlw2])
        w2bc.bump()
        for fc in range(NFC):
            gidx2 = cc * NFC + fc
            qlw2 = dma2(pl, "sync", w2tstg[fc % 2][:, :],
                        w2t_d[fc * P:(fc + 1) * P, cc * 512:cc * 512 + 512], w2qp, gidx2,
                        waits=[(w2qv, gidx2 - 1)] if gidx2 >= 2 else [])
            def q_w2(e, cc=cc, fc=fc):
                e.tensor_tensor(out=i32stg[:, :], in0=w2tstg[fc % 2][:, :],
                                in1=psA[:, (cc % 2) * 512:(cc % 2) * 512 + 512], op=ALU.mult)
                e.tensor_copy(out=wq2t_sb[:, fc * D + cc * 512:fc * D + cc * 512 + 512],
                              in_=i32stg[:, :])
                dinc(e, w2qv.sem)
            pl.op("vector", q_w2, waits=[qlw2, (w2bc, w2bc.n)])
            w2qv.bump()

    NT_F1 = NT4 * NFC
    def mxfin(e):
        e = DrainWrap(e)
        for c in range(8):
            e.reduce_max(out=mx8[:, c:c + 1], in_=mxf[:, c * 16:(c + 1) * 16], axis=AX.X)
        dinc(e, f1v.sem)
    pl.op("vector", mxfin, waits=[(f1v, NT_F1)])
    f1v.bump()
    v = C["xo"].bump(16)
    pl.op("sync", lambda e: e.dma_start(
        out=rowbuf[0:1, 4 * D:4 * D + 1024].rearrange("o (p c) -> (o p) c", c=8),
        in_=mx8[:, :]).then_inc(C["xo"].sem, 16),
        waits=[(f1v, NT_F1 + 1)])
    v2 = C["xo"].bump(16)
    pl.op("sync", lambda e: e.dma_start(out=h2stg[0:1, 0:1024], in_=rowbuf[0:1, 4 * D:4 * D + 1024]).then_inc(C["xo"].sem, 16),
          waits=[(C["xo"], v)])
    def gmax2_local(e):
        e = DrainWrap(e)
        e.memset(gmaxs[0:1, :], 0.0)
        e.reduce_max(out=gmaxs[0:1, 0:1], in_=h2stg[0:1, 0:1024], axis=AX.X)
        dinc(e, f1v.sem)
    pl.op("vector", gmax2_local, waits=[(C["xo"], v2)])
    f1v.bump(1)
    cc2c = C["cc2c"]
    def coll2(e):
        e.dma_start(out=cc_in2.ap(), in_=gmaxs[0:1, :]).then_inc(cc2.sem, 16)
        e.wait_ge(cc2.sem, 16)
        e.collective_compute("AllReduce", ALU.max, replica_groups=[list(range(8))],
                             ins=[cc_in2.ap().opt()], outs=[cc_out2.ap().opt()]).then_inc(cc2c.sem, 1)
        e.wait_ge(cc2c.sem, 1)
        e.dma_start(out=gmaxs[0:1, :], in_=cc_out2.ap()).then_inc(cc2.sem, 16)
    pl.op("gpsimd", coll2, waits=[(f1v, f1v.n)])
    cc2.bump(32); cc2c.bump(1)
    def s2row(e):
        e = DrainWrap(e)
        e.tensor_scalar(out=tiny[0:1, 0:8], in0=gmaxs[0:1, 0:8], scalar1=EPS_Q, scalar2=1.0 / 255.0,
                        op0=ALU.max, op1=ALU.mult)
        e.reciprocal(out=mqrow[0:1, 0:8], in_=tiny[0:1, 0:8])
        dinc(e, sc2c.sem)
    pl.op("vector", s2row, waits=[(cc2, cc2.n)])
    sc2c.bump()
    def s2bc(e):
        e.matmul(psB[:, 0:512][0:P, 0:1], onesrow_f[0:1, :], tiny[0:1, 0:1], start=True, stop=True)
        e.matmul(psB[:, 512:1024][0:P, 0:1], onesrow_f[0:1, :], mqrow[0:1, 0:1],
                 start=True, stop=True).then_inc(sc2c.sem, 1)
    pl.op("tensor", s2bc, waits=[(sc2c, sc2c.n), (f1s, NT_F1)])
    sc2c.bump()
    v = w2ld.bump(16)
    pl.op("sync", lambda e: e.dma_start(out=b2sb[:, :], in_=b2_d[:, :]).then_inc(w2ld.sem, 16))
    def b2prep(e):
        e = DrainWrap(e)
        e.tensor_copy(out=r2pp[:, 0:1], in_=psB[:, 512:1024][0:P, 0:1])
        e.tensor_copy(out=s2pp[:, 0:1], in_=psB[:, 0:512][0:P, 0:1])
        e.tensor_scalar(out=p2c[:, :], in0=sw2tmp[:, :], scalar1=s2pp[:, 0:1], scalar2=None, op0=ALU.mult)
        e.reciprocal(out=p2gbb[:, :], in_=p2c[:, :])
        e.tensor_tensor(out=i32r2[:, :], in0=b2sb[:, :], in1=p2gbb[:, :], op=ALU.mult)
        e.tensor_copy(out=bq2f[:, :], in_=i32r2[:, :])
        e.tensor_tensor(out=p2gb[:, :], in0=p2c[:, :], in1=gb_pp[:, :], op=ALU.mult)
        e.tensor_tensor(out=p2gbb[:, :], in0=p2gb[:, :], in1=bq2f[:, :], op=ALU.mult)
        dinc(e, b2p.sem)
    pl.op("vector", b2prep, waits=[(sc2c, sc2c.n), (w2ld, v)])
    b2p.bump()
    pl.emit(nc)

    if "h3dbg" in dbg_out:
        plh = Plan()
        c = C["dbgc"]
        dma(plh, "sync", dbg_out["h3dbg"][:, :, :, :], h3buf[:, :, :, :], c)
        tgt = c.n
        plh.op("sync", lambda e: None, waits=[(c, tgt)])
        plh.emit(nc)
    if "sc2" in dbg_out:
        dump_sbuf("sc2", [(dbg_out["sc2"][:, :], tiny[0:1, :])])
    if "wq2dbg" in dbg_out:
        dump_sbuf("wq2dbg", [(dbg_out["wq2dbg"][:, :], wq2t_sb[:, :])])
    if "b2dbg" in dbg_out:
        dump_sbuf("b2dbg", [(dbg_out["b2dbg"][0:1, 0:DC], p2gb[0:1, :]),
                            (dbg_out["b2dbg"][1:2, 0:DC], p2gbb[0:1, :]),
                            (dbg_out["b2dbg"][2:3, 0:DC], bq2f[0:1, :]),
                            (dbg_out["b2dbg"][3:4, 0:DC], sw2[0:1, :]),
                            (dbg_out["b2dbg"][4:5, 0:DC], sw2tmp[0:1, :]),
                            (dbg_out["b2dbg"][5:6, 0:DC], gb_pp[0:1, :])])
    if "mx2dbg" in dbg_out:
        dump_sbuf("mx2dbg", [(dbg_out["mx2dbg"][0:1, 0:P], mxf[0:1, :]),
                             (dbg_out["mx2dbg"][1:2, 0:8], mx8[0:1, :]),
                             (dbg_out["mx2dbg"][2:3, 0:P], h2stg[0:1, 0:P]),
                             (dbg_out["mx2dbg"][3:4, 0:8], gmaxs[0:1, :])])
    if stage < 5:
        return nc, dbg_out

    pl = Plan()
    rbl, rbq, f2r, f2s, f2v, f2x, yo = (C[k] for k in ("rbl", "rbq", "f2r", "f2s", "f2v", "f2x", "yo"))
    for tc in range(NT4):
        for fc in range(NFC):
            qi = tc * NFC + fc
            w = [(rbq, qi - 1)] if qi >= 2 else []
            if tc >= 2 and fc == 0:
                w.append((f2r, (tc - 1) * DC))
            rlw3 = dma2(pl, "sync", rbstg[qi % 2][:, :], h3buf[tc, fc, :, :], rblp, qi, waits=w)
            def q_rb(e, tc=tc, fc=fc, qi=qi):
                e.tensor_scalar(out=i32b_stg[:, :], in0=rbstg[qi % 2][:, :],
                                scalar1=r2pp[:, 0:1], scalar2=None, op0=ALU.mult)
                e.tensor_copy(out=h3q_sb[tc % 2][:, fc * 512:fc * 512 + 512],
                              in_=i32b_stg[:, :])
                dinc(e, rbq.sem)
            qrw = [rlw3]
            if qi == 0:
                qrw.append((C["b2p"], 1))
            pl.op("vector", q_rb, waits=qrw)
            rbq.bump()
        for dc in range(DC):
            gi = tc * DC + dc
            waits = [(rbq, (tc + 1) * NFC)]
            if gi >= 2:
                waits.append((f2s, gi - 1))
            sl = psA[:, (gi % 2) * 512:(gi % 2) * 512 + 512]
            def mm_f2(e, tc=tc, dc=dc, sl=sl):
                for fc in range(NFC):
                    ins = e.matmul(sl, wq2t_sb[:, fc * D + dc * 128:fc * D + dc * 128 + 128],
                                   h3q_sb[tc % 2][:, fc * 512:fc * 512 + 512],
                                   start=(fc == 0), stop=(fc == NFC - 1))
                ins.then_inc(f2r.sem, 1)
            pl.op("tensor", mm_f2, waits=waits)
            f2r.bump()
            xlw = dma2(pl, "sync", xinstg[gi % 2][:, :], xbuf[dc, :, tc * 512:tc * 512 + 512],
                       f2xp, gi, waits=[(f2v, gi - 1)] if gi >= 2 else [])
            def ev_f2s(e, dc=dc, sl=sl, gi=gi):
                e.activation(out=o1stg[gi % 2][:, :], in_=sl, func=AF.Identity,
                             scale=p2gb[:, dc:dc + 1], bias=p2gbb[:, dc:dc + 1])
                dinc(e, f2s.sem)
            sw2_ = [(f2r, f2r.n)]
            if gi >= 2:
                sw2_.append((f2v, gi - 1))
            pl.op("scalar", ev_f2s, waits=sw2_)
            f2s.bump()
            def ev_f2v(e, gi=gi):
                e.tensor_tensor(out=o2stg[gi % 2][:, :], in0=o1stg[gi % 2][:, :], in1=xinstg[gi % 2][:, :],
                                op=ALU.add)
                dinc(e, f2v.sem)
            evw = [(f2s, f2s.n), xlw]
            if gi >= 2:
                evw.append((yop[gi % 2], 16 * (gi // 2)))
            pl.op("vector", ev_f2v, waits=evw)
            f2v.bump()
            dma2(pl, "sync", y_out[dc * P:(dc + 1) * P, tc * 512:tc * 512 + 512],
                 o2stg[gi % 2][:, :], yop, gi, waits=[(f2v, f2v.n)])
    pl.op("sync", lambda e: None, waits=[(yop[0], yop[0].n), (yop[1], yop[1].n)])
    pl.emit(nc)

    return nc, dbg_out


def make_host_masks():
    m = np.zeros((4, P, 512), dtype=BF)
    for kt in range(4):
        for r in range(P):
            m[kt, r, kt * P + r:] = 1.0
    return m


def prep_inputs(inputs):
    x = inputs["L0_attn_x"]
    w1 = np.ascontiguousarray(inputs["L0_ffn_w1"])
    w2 = np.ascontiguousarray(inputs["L0_ffn_w2"])
    shared = {
        "wq": np.ascontiguousarray(inputs["L0_attn_WQ"]),
        "wk": np.ascontiguousarray(inputs["L0_attn_WK"]),
        "wv": np.ascontiguousarray(inputs["L0_attn_WV"]),
        "wo": np.ascontiguousarray(inputs["L0_attn_WO"]),
        "wmod": np.ascontiguousarray(inputs["L0_wmod"]),
        "w1n": w1,
        "w1t": np.ascontiguousarray(w1.T),
        "w2n": w2,
        "w2t": np.ascontiguousarray(w2.T),
        "b1c": np.ascontiguousarray(inputs["L0_ffn_b1"].reshape(NFC, P).T),
        "b2c": np.ascontiguousarray(inputs["L0_ffn_b2"].reshape(DC, P).T),
        "masks": make_host_masks(),
    }
    cores = []
    for b in range(8):
        d = dict(shared)
        d["x_fm"] = np.ascontiguousarray(x[b].T)
        d["crow"] = np.ascontiguousarray(inputs["L0_c"][b:b + 1, :])
        cores.append(d)
    return cores


def kernel(**inputs):
    cores = prep_inputs(inputs)
    nc, _ = build_nc(stage=5)
    res = run_bass_kernel_spmd(nc, cores, core_ids=list(range(8)))
    out = np.stack([r["y_fm"].T for r in res.results], axis=0)
    return np.ascontiguousarray(out, dtype=np.float32)


# version 02
# speedup vs baseline: 1.0931x; 1.0931x over previous
import numpy as np
import ml_dtypes
import concourse.bass as bass
import concourse.mybir as mybir
from concourse.bass_utils import run_bass_kernel_spmd

F32 = mybir.dt.float32
BF16 = mybir.dt.bfloat16
I32 = mybir.dt.int32
ALU = mybir.AluOpType
AF = mybir.ActivationFunctionType
AX = mybir.AxisListType
BF = ml_dtypes.bfloat16

P = 128
D = 1024; DC = 8
T = 2048; NT4 = 4
NH = 16
FF = 4096; NFC = 32
EPS_LN = 1e-5
EPS_Q = 1e-8

OFF_X = 17408
OFF_WQ2T = 17408
OFF_B = 82944
OFF_QSB = OFF_B
OFF_KSB = OFF_B + 8192
OFF_VSB = OFF_B + 16384
OFF_USB = OFF_B + 24704
OFF_MASKS = OFF_B + 57472
OFF_WQ1T = OFF_B
OFF_H3Q0 = OFF_B
OFF_H3Q1 = OFF_B + 32768
OFF_SXROW = OFF_B
OFF_SXXROW = OFF_B + 8192
OFF_XBF = OFF_B + 24704
OFF_MH2 = OFF_B + 41088
OFF_C = 148480
OFF_D = 181248
OFF_ROWS = 214016
OFF_E = 226304
OFF_MODROW = OFF_B

ENGS = ("sync", "gpsimd", "tensor", "vector", "scalar")


class Ctr:
    __slots__ = ("sem", "n")
    def __init__(self, sem):
        self.sem = sem
        self.n = 0
    def bump(self, k=1):
        self.n += k
        return self.n


SIM_DRAINS = [False]


def dinc(e, sem, k=1):
    e.drain().then_inc(sem, k)


class DrainWrap:
    __slots__ = ("_e",)
    def __init__(self, e):
        object.__setattr__(self, "_e", e)
    def __getattr__(self, n):
        a = getattr(self._e, n)
        if callable(a) and n not in ("wait_ge", "drain", "nop", "sem_inc"):
            def w(*args, **kw):
                self._e.drain()
                return a(*args, **kw)
            return w
        return a


class Plan:
    def __init__(self):
        self.ops = {e: [] for e in ENGS}
    def op(self, eng, fn, waits=()):
        self.ops[eng].append((tuple(waits), fn))
    def emit(self, nc):
        with nc.Block() as b:
            for eng in ENGS:
                ops = self.ops[eng]
                if not ops:
                    continue
                def body(e, ops=ops):
                    if SIM_DRAINS[0]:
                        e = DrainWrap(e)
                    for waits, fn in ops:
                        for (c, v) in waits:
                            e.wait_ge(c.sem, v)
                        fn(e)
                getattr(b, eng)(body)


def build_nc(stage=5, debug=(), sim=False, skip_mod=False):
    SIM_DRAINS[0] = bool(sim)
    nc = bass.Bass("TRN2", target_bir_lowering=False, debug=False, dynamic_dma_scratch_size=16384,
                   detect_race_conditions=bool(sim))
    dbg = set(debug)

    x_in = nc.dram_tensor("x_fm", [D, T], F32, kind="ExternalInput").ap()
    wq_d = nc.dram_tensor("wq", [D, D], F32, kind="ExternalInput").ap()
    wk_d = nc.dram_tensor("wk", [D, D], F32, kind="ExternalInput").ap()
    wv_d = nc.dram_tensor("wv", [D, D], F32, kind="ExternalInput").ap()
    wo_d = nc.dram_tensor("wo", [D, D], F32, kind="ExternalInput").ap()
    wmod_d = nc.dram_tensor("wmod", [D, 6 * D], F32, kind="ExternalInput").ap()
    crow_d = nc.dram_tensor("crow", [1, D], F32, kind="ExternalInput").ap()
    w1n_d = nc.dram_tensor("w1n", [FF, D], F32, kind="ExternalInput").ap()
    w1t_d = nc.dram_tensor("w1t", [D, FF], F32, kind="ExternalInput").ap()
    w2n_d = nc.dram_tensor("w2n", [D, FF], F32, kind="ExternalInput").ap()
    w2t_d = nc.dram_tensor("w2t", [FF, D], F32, kind="ExternalInput").ap()
    b1_d = nc.dram_tensor("b1c", [P, NFC], F32, kind="ExternalInput").ap()
    b2_d = nc.dram_tensor("b2c", [P, DC], F32, kind="ExternalInput").ap()
    mask_d = nc.dram_tensor("masks", [4, P, 512], BF16, kind="ExternalInput").ap()
    y_out = nc.dram_tensor("y_fm", [D, T], F32, kind="ExternalOutput").ap()

    cc_in1 = nc.dram_tensor("cc_in1", [1, 8], F32)
    cc_out1 = nc.dram_tensor("cc_out1", [1, 8], F32)
    cc_in2 = nc.dram_tensor("cc_in2", [1, 8], F32)
    cc_out2 = nc.dram_tensor("cc_out2", [1, 8], F32)
    h3buf = nc.dram_tensor("h3buf", [NT4, NFC, P, 512], F32).ap()
    xbuf = nc.dram_tensor("xbuf", [DC, P, T], F32).ap()
    rowbuf = nc.dram_tensor("rowbuf", [1, 16 * D], F32).ap()

    AL = lambda name, shape, dt, off: nc.alloc_sbuf_tensor_at(name, shape, dt, offset=off)
    x_sb = AL("x_sb", [P, DC * T], F32, OFF_X)
    q_sb = AL("q_sb", [P, 2 * T], BF16, OFF_QSB)
    k_sb = AL("k_sb", [P, 2 * T], BF16, OFF_KSB)
    v_sb = AL("v_sb", [P, 16 * 260], BF16, OFF_VSB)
    u_sb = AL("u_sb", [P, 8 * T], BF16, OFF_USB)
    wq1t_sb = AL("wq1t_sb", [P, DC * FF], BF16, OFF_WQ1T)
    wq2t_sb = AL("wq2t_sb", [P, NFC * D], BF16, OFF_WQ2T)
    h3q_sb = [AL("h3q0_sb", [P, NFC * 512], BF16, OFF_H3Q0),
              AL("h3q1_sb", [P, NFC * 512], BF16, OFF_H3Q1)]
    h1_sb = AL("h1_sb", [P, DC * T], BF16, OFF_C)
    hq_sb = AL("hq_sb", [P, DC * T], BF16, OFF_C)
    modrow = AL("modrow", [1, 6 * D], F32, OFF_MODROW)

    xbf_stg = [AL("xbf_stg0", [P, T], BF16, OFF_XBF), AL("xbf_stg1", [P, T], BF16, OFF_XBF + 4096)]
    xsq_stg = [AL("xsq_stg0", [P, T], BF16, OFF_XBF + 8192), AL("xsq_stg1", [P, T], BF16, OFF_XBF + 12288)]
    wmod_stg = [AL("wmod_stg0", [P, DC * 512], BF16, OFF_D),
                AL("wmod_stg1", [P, DC * 512], BF16, OFF_D + 8192)]
    wmod32_stg = AL("wmod32_stg", [P, DC * 512], F32, OFF_D + 16384)
    wstg = AL("wstg", [P, 3 * DC * 256], BF16, OFF_D)
    qsq_stg = AL("qsq_stg", [P, T], BF16, OFF_D + 28672)
    wo_stg = AL("wo_stg", [P, DC * D], BF16, OFF_D + 12288)
    tmp_stg = AL("tmp_stg", [P, T], F32, OFF_D)
    h2stg = AL("h2stg", [P, T], F32, OFF_D + 8192)
    w1tstg = [AL("w1tstg0", [P, 512], F32, OFF_D + 16384), AL("w1tstg1", [P, 512], F32, OFF_D + 18432)]
    i32stg = AL("i32stg", [P, 512], I32, OFF_D + 20480)
    rwstg = [AL("rwstg0", [1, 512], F32, OFF_D + 22528), AL("rwstg1", [1, 512], F32, OFF_ROWS)]
    h3stg = [AL("h3stg0", [P, 512], F32, OFF_D + 24576), AL("h3stg1", [P, 512], F32, OFF_D + 26624)]
    w2nstg = [AL("w2nstg0", [P, D], F32, OFF_D), AL("w2nstg1", [P, D], F32, OFF_D + 4096)]
    w1nstg = [AL("w1nstg0", [P, D], F32, OFF_D), AL("w1nstg1", [P, D], F32, OFF_D + 8192)]
    i32w1 = [AL("i32w1_0", [P, 512], I32, OFF_D + 12288), AL("i32w1_1", [P, 512], I32, OFF_D + 14336)]
    w2rwstg = [AL("w2rwstg0", [1, 512], F32, OFF_ROWS + 4096), AL("w2rwstg1", [1, 512], F32, OFF_ROWS + 6144)]
    w2tstg = [AL("w2tstg0", [P, 512], F32, OFF_D + 28672), AL("w2tstg1", [P, 512], F32, OFF_D + 30720)]
    rbstg = [AL("rbstg%d" % i, [P, 512], F32, OFF_D + off) for i, off in
             enumerate((0, 2048, 4096, 20480, 22528, 24576))]
    op1stg = [AL("op1stg0", [P, 512], F32, OFF_D), AL("op1stg1", [P, 512], F32, OFF_D + 2048)]
    op2stg = [AL("op2stg0", [P, 512], F32, OFF_D + 4096), AL("op2stg1", [P, 512], F32, OFF_D + 6144)]
    i32b_stg = AL("i32b_stg", [P, 512], I32, OFF_D + 6144)
    o1stg = [AL("o1stg0", [P, 512], F32, OFF_D + 8192), AL("o1stg1", [P, 512], F32, OFF_D + 18432)]
    o2stg = [AL("o2stg0", [P, 512], F32, OFF_D + 10240), AL("o2stg1", [P, 512], F32, OFF_D + 12288)]
    xinstg = [AL("xinstg0", [P, 512], F32, OFF_D + 14336), AL("xinstg1", [P, 512], F32, OFF_D + 16384)]

    sxrow = AL("sxrow", [1, T], F32, OFF_SXROW)
    sxxrow = AL("sxxrow", [1, T], F32, OFF_SXXROW)
    srow = AL("srow", [1, 512], F32, OFF_ROWS)
    rrow = AL("rrow", [1, 512], F32, OFF_ROWS + 2048)
    a_tiles = [AL("a_tile0", [P, 1024], BF16, OFF_ROWS + 4096),
               AL("a_tile1", [P, 1024], BF16, OFF_ROWS + 6144),
               AL("a_tile2", [P, 1024], BF16, OFF_D + 28672),
               AL("a_tile3", [P, 1024], BF16, OFF_D + 30720)]
    rb_sb = AL("rb_sb", [P, 512], F32, OFF_ROWS + 8192)
    mqrow = AL("mqrow", [1, 64], F32, OFF_ROWS + 10240)
    mhold = AL("mhold", [1, 64], F32, OFF_ROWS + 10496)
    negcrow = AL("negcrow", [1, NH], F32, OFF_ROWS + 11264)
    mhold2 = AL("mhold2", [1, 2 * T], F32, OFF_MH2)

    e = OFF_E
    def EA(name, shape, dt):
        nonlocal e
        t = AL(name, shape, dt, e)
        itm = {BF16: 2, F32: 4, I32: 4}[dt]
        nbytes = int(np.prod(shape[1:])) * itm
        e += -(-nbytes // 32) * 32
        return t
    masks_sb = AL("masks_sb", [P, 4 * 512], BF16, OFF_MASKS)
    sa_pp = EA("sa_pp", [P, DC], F32)
    sb_pp = EA("sb_pp", [P, DC], F32)
    ga_pp = EA("ga_pp", [P, DC], F32)
    gb_pp = EA("gb_pp", [P, DC], F32)
    capp_pp = EA("capp_pp", [P, DC], F32)
    cbpp_pp = EA("cbpp_pp", [P, DC], F32)
    cbr1_pp = EA("cbr1_pp", [P, DC], F32)
    sbr1_pp = EA("sbr1_pp", [P, DC], F32)
    ccol = EA("ccol", [P, DC], F32)
    scol = EA("scol", [P, DC], BF16)
    scol32 = EA("scol32", [P, DC], F32)
    ones_bf = EA("ones_bf", [P, 8], BF16)
    onesrow_f = EA("onesrow_f", [1, P], F32)
    negc_pp = EA("negc_pp", [P, NH], F32)
    sw1 = EA("sw1", [P, NFC], F32)
    sw1tmp = EA("sw1tmp", [P, NFC], F32)
    sw2tmp = EA("sw2tmp", [P, DC], F32)
    rw1 = EA("rw1", [P, NFC], F32)
    p1c = EA("p1c", [P, NFC], F32)
    pb1c = EA("pb1c", [P, NFC], F32)
    b1sb = EA("b1sb", [P, NFC], F32)
    bq1f = EA("bq1f", [P, NFC], F32)
    i32r = EA("i32r", [P, NFC], I32)
    sw2 = EA("sw2", [P, DC], F32)
    rw2 = EA("rw2", [P, DC], F32)
    p2c = EA("p2c", [P, DC], F32)
    p2gb = EA("p2gb", [P, DC], F32)
    p2gbb = EA("p2gbb", [P, DC], F32)
    b2sb = EA("b2sb", [P, DC], F32)
    bq2f = EA("bq2f", [P, DC], F32)
    i32r2 = EA("i32r2", [P, DC], I32)
    s1pp = EA("s1pp", [P, 1], F32)
    s2pp = EA("s2pp", [P, 1], F32)
    r1pp = EA("r1pp", [P, 1], F32)
    r2pp = EA("r2pp", [P, 1], F32)
    mx8 = EA("mx8", [P, DC], F32)
    mxf = EA("mxf", [P, NT4 * NFC], F32)
    gmaxs = EA("gmaxs", [1, 8], F32)
    tiny = EA("tiny", [1, 8], F32)
    assert e <= 224 * 1024 - 32, e

    psA = nc.alloc_psum_tensor("psA", [P, T], F32)
    psB = nc.alloc_psum_tensor("psB", [P, T], F32)

    dbg_out = {}
    for name, shp in (("mod", [1, 6 * D]), ("h1", [D, T]), ("rows", [2, T]), ("raw", [2, T]), ("u", [D, T]),
                      ("negc", [1, NH]), ("x2", [D, T]), ("hq", [D, T]),
                      ("sc1", [1, 8]), ("mxdbg", [4, P]), ("nbdbg", [2, 64]), ("mx2dbg", [4, P]), ("wq1dbg", [P, DC * FF]),
                      ("h3dbg", [NT4, NFC, P, 512]), ("sc2", [1, 8]), ("wq2dbg", [P, NFC * D]), ("b2dbg", [6, P])):
        if name in dbg:
            dbg_out[name] = nc.dram_tensor("dbg_" + name, shp, F32, kind="ExternalOutput").ap()

    C = {}
    def ctr(name):
        C[name] = Ctr(nc.alloc_semaphore(name))
        return C[name]
    for n in ("ld0", "silu", "m32", "mw", "mwf", "mr", "me", "pp", "vone",
              "l1sv", "l1ss", "l1sf", "l1mm", "l1rw",
              "qw", "qwf", "pjr", "pjf", "nbs", "nbm", "nbv", "nbf",
              "ats", "atx", "ata", "atf", "atr", "atb", "atrb", "atu",
              "opr", "opv", "opf", "wo_ok",
              "l2sv", "l2ss", "l2sf", "l2mm", "l2rw",
              "mxv", "cc1", "cc1c", "sc1c", "gh2", "rqv", "xo",
              "w1ld", "w1mx", "w1rw", "w1bc", "w1qv", "w1tt", "w1f",
              "f1r", "f1s", "f1v", "f1d",
              "w2ld", "w2mx", "w2rw", "w2bc", "w2qv",
              "cc2", "cc2c", "sc2c", "b2p",
              "rbq", "f2r", "f2s", "f2v", "f2x", "yo", "dbgc"):
        ctr(n)

    def pair(name):
        return (ctr(name + "_a"), ctr(name + "_b"))
    mwp = pair("mwp"); w1mp = pair("w1mp"); w1qp = pair("w1qp"); w1rp = pair("w1rp")
    f1hp = pair("f1hp"); w2mp = pair("w2mp"); w2qp = pair("w2qp"); w2rp = pair("w2rp")
    rblp = pair("rblp"); f2xp = pair("f2xp"); yop = pair("yop")
    rbl6 = [ctr("rbl6_%d" % i) for i in range(6)]

    def dma2(pl, eng, out, in_, pr, idx, waits=(), ncd=False):
        c = pr[idx % 2]
        v = c.bump(16)
        pl.op(eng, lambda e, o=out, i=in_, cc=c, ncd=ncd: e.dma_start(
            out=o, in_=i, allow_slow_non_contiguous=ncd).then_inc(cc.sem, 16), waits=waits)
        return (c, v)

    def dma(pl, eng, out, in_, c, ncd=False):
        v = c.bump(16)
        pl.op(eng, lambda e, o=out, i=in_, cc=c, ncd=ncd: e.dma_start(
            out=o, in_=i, allow_slow_non_contiguous=ncd).then_inc(cc.sem, 16))
        return v

    def dump_sbuf(name, pieces):
        if name not in dbg_out:
            return
        pl = Plan()
        c = C["dbgc"]
        for dr, sb in pieces:
            dma(pl, "gpsimd", dr, sb, c)
        tgt = c.n
        pl.op("gpsimd", lambda e, t=tgt: None, waits=[(c, tgt)])
        pl.emit(nc)

    pl = Plan()
    ld0, silu_c, mw, mwf, mr, me = C["ld0"], C["silu"], C["mw"], C["mwf"], C["mr"], C["me"]
    for dc in range(DC):
        dma(pl, "sync", x_sb[:, dc * T:(dc + 1) * T], x_in[dc * P:(dc + 1) * P, :], ld0)
    dma(pl, "sync", masks_sb[:].rearrange("p (m c) -> p m c", c=512), mask_d[:, :, :].transpose([1, 0, 2]), ld0)
    dma(pl, "sync", ccol[:, :], crow_d[0:1, :].rearrange("o (c p) -> p (o c)", p=P), ld0, ncd=True)
    ld0_all = ld0.n
    pl.op("gpsimd", lambda e: e.memset(ones_bf[:], 1.0))
    pl.op("gpsimd", lambda e: e.memset(ones_bf[:, 1:3], 0.0))
    pl.op("gpsimd", lambda e: e.memset(ones_bf[0:64, 1:2], 1.0))
    pl.op("gpsimd", lambda e: e.memset(ones_bf[64:128, 2:3], 1.0))
    pl.op("gpsimd", lambda e: e.memset(onesrow_f[:], 1.0))
    pl.op("gpsimd", lambda e: e.memset(tiny[:], 0.0))
    mw_waits = []
    J32 = (4, 5, 10, 11)
    for j in range(0 if skip_mod == "all" else 12):
        waits = ([(mr, j - 1)] if j >= 2 else []) if not skip_mod else []
        if j in J32:
            c = C["m32"]
            w32 = ([(mr, j)] if j >= 1 else []) if not skip_mod else []
            for dc in range(DC):
                v = c.bump(16)
                pl.op("sync", lambda e, j=j, dc=dc, cc=c: e.dma_start(
                    out=wmod32_stg[:, dc * 512:(dc + 1) * 512],
                    in_=wmod_d[dc * P:(dc + 1) * P, j * 512:(j + 1) * 512]).then_inc(cc.sem, 16),
                    waits=w32 if dc == 0 else [])
            mw_waits.append((c, c.n))
        else:
            c = mwp[j % 2]
            for dc in range(DC):
                v = c.bump(16)
                pl.op("gpsimd", lambda e, j=j, dc=dc, cc=c: e.dma_start(
                    out=wmod_stg[j % 2][:, dc * 512:(dc + 1) * 512],
                    in_=wmod_d[dc * P:(dc + 1) * P, j * 512:(j + 1) * 512]).then_inc(cc.sem, 16),
                    waits=waits if dc == 0 else [])
            mw_waits.append((c, c.n))
    pl.op("scalar", lambda e: (e.activation(out=scol32[:, :], in_=ccol[:, :], func=AF.Silu),
                               dinc(e, silu_c.sem)), waits=[(ld0, ld0_all)])
    silu_c.bump()
    pl.op("vector", lambda e: (e.tensor_copy(out=scol[:, :], in_=scol32[:, :]),
                               dinc(e, silu_c.sem)), waits=[(silu_c, silu_c.n)])
    silu_c.bump()
    for j in range(12 if not skip_mod else 0):
        waits = [mw_waits[j]]
        if j <= 1:
            waits.append((silu_c, 2))
        if j >= 4:
            waits.append((me, j - 3))
        def mm_mod(e, j=j):
            if j in J32:
                for dc in range(DC):
                    ins = e.matmul(psA[0:1, (j % 4) * 512:(j % 4) * 512 + 512],
                                   scol32[:, dc:dc + 1],
                                   wmod32_stg[:, dc * 512:(dc + 1) * 512],
                                   start=(dc == 0), stop=(dc == DC - 1))
            else:
                for dc in range(DC):
                    ins = e.matmul(psA[0:1, (j % 4) * 512:(j % 4) * 512 + 512],
                                   scol[:, dc:dc + 1],
                                   wmod_stg[j % 2][:, dc * 512:(dc + 1) * 512],
                                   start=(dc == 0), stop=(dc == DC - 1))
            ins.then_inc(mr.sem, 1)
        pl.op("tensor", mm_mod, waits=waits)
        mr.bump()
        def ev_mod(e, j=j):
            e.tensor_copy(out=modrow[0:1, j * 512:(j + 1) * 512],
                          in_=psA[0:1, (j % 4) * 512:(j % 4) * 512 + 512])
            dinc(e, me.sem)
        pl.op("vector", ev_mod, waits=[(mr, j + 1)])
        me.bump()
    if skip_mod:
        pl.op("vector", lambda e: e.memset(modrow[:, :], 0.01))
    pl.emit(nc)

    pl = Plan()
    pp = C["pp"]
    def mod_post(e):
        e = DrainWrap(e)
        S = h2stg[0:1, 0:D]
        S2 = tmp_stg[0:1, 0:D]
        S3 = h2stg[0:1, D:2 * D]
        e.tensor_scalar(out=S2, in0=modrow[0:1, D:2 * D], scalar1=1.0, scalar2=None, op0=ALU.add)
        e.tensor_copy(out=modrow[0:1, D:2 * D], in_=S2)
        e.tensor_scalar(out=S2, in0=modrow[0:1, 4 * D:5 * D], scalar1=1.0, scalar2=None, op0=ALU.add)
        e.tensor_copy(out=modrow[0:1, 4 * D:5 * D], in_=S2)
        for off in (2 * D, 5 * D):
            gsl = modrow[0:1, off:off + D]
            e.tensor_scalar(out=S, in0=gsl, scalar1=0.0, scalar2=None, op0=ALU.is_ge)
            e.tensor_scalar(out=S2, in0=S, scalar1=2.0, scalar2=-1.0, op0=ALU.mult, op1=ALU.add)
            e.tensor_scalar(out=S, in0=gsl, scalar1=-1.0, scalar2=None, op0=ALU.mult)
            e.tensor_tensor(out=S3, in0=gsl, in1=S, op=ALU.max)
            e.tensor_scalar(out=S, in0=S3, scalar1=0.25, scalar2=None, op0=ALU.max)
            e.tensor_tensor(out=gsl, in0=S2, in1=S, op=ALU.mult)
        dinc(e, pp.sem)
    pl.op("vector", mod_post)
    pp.bump()
    order = [(0, sa_pp), (3 * D, sb_pp), (2 * D, ga_pp), (5 * D, gb_pp), (D, capp_pp), (4 * D, cbpp_pp)]
    w0 = pp.n
    for i, (off, _) in enumerate(order):
        dma(pl, "sync", rowbuf[0:1, i * D:(i + 1) * D], modrow[0:1, off:off + D], pp)
        if i == 0:
            pl.ops["sync"][-1] = ((pl.ops["sync"][-1][0] + ((pp, w0),)), pl.ops["sync"][-1][1])
    wmid = pp.n
    pl.op("sync", lambda e: None, waits=[(pp, wmid)])
    for i, (_, dst) in enumerate(order):
        dma(pl, "sync", dst[:, :], rowbuf[0:1, i * D:(i + 1) * D].rearrange("o (c p) -> p (o c)", p=P), pp, ncd=True)
    wend = pp.n
    pl.op("sync", lambda e: None, waits=[(pp, wend)])
    pl.emit(nc)

    if "mod" in dbg_out:
        dump_sbuf("mod", [(dbg_out["mod"][:, :], modrow[:, :])])

    def ln_stats(pl, pref, xsrc):
        sv, ss, sf, smm, srw = C[pref + "sv"], C[pref + "ss"], C[pref + "sf"], C[pref + "mm"], C[pref + "rw"]
        base_sf = sf.n
        for dc in range(DC):
            waits = [(sf, base_sf + dc - 1)] if dc >= 2 else []
            pl.op("vector", lambda e, dc=dc: (e.tensor_copy(
                out=xbf_stg[dc % 2][:, :], in_=xsrc[:, dc * T:(dc + 1) * T]), dinc(e, sv.sem)), waits=waits)
            sv.bump()
            pl.op("scalar", lambda e, dc=dc: (e.square(
                out=xsq_stg[dc % 2][:, :], in_=xsrc[:, dc * T:(dc + 1) * T]), dinc(e, ss.sem)), waits=waits)
            ss.bump()
            def mm_st(e, dc=dc):
                for tc in range(NT4):
                    e.matmul(psA[0:1, tc * 512:tc * 512 + 512], ones_bf[:, 0:1],
                             xbf_stg[dc % 2][:, tc * 512:tc * 512 + 512], start=(dc == 0), stop=(dc == DC - 1))
                    ins = e.matmul(psB[0:1, tc * 512:tc * 512 + 512], ones_bf[:, 0:1],
                                   xsq_stg[dc % 2][:, tc * 512:tc * 512 + 512], start=(dc == 0), stop=(dc == DC - 1))
                ins.then_inc(sf.sem, 1)
            pl.op("tensor", mm_st, waits=[(sv, sv.n), (ss, ss.n)])
            sf.bump()
        def rows1(e):
            e = DrainWrap(e)
            T4 = tmp_stg[0:1, 0:T]
            S = h2stg[0:1, 0:T]
            e.tensor_copy(out=sxrow[0:1, :], in_=psA[0:1, 0:T])
            e.tensor_copy(out=sxxrow[0:1, :], in_=psB[0:1, 0:T])
            if "raw" in dbg_out and pref == "l1":
                e.tensor_copy(out=mhold2[0:1, 0:T], in_=psA[0:1, 0:T])
                e.tensor_copy(out=mhold2[0:1, T:2 * T], in_=psB[0:1, 0:T])
            
            e.tensor_scalar(out=T4, in0=sxrow[0:1, :], scalar1=-1.0 / D, scalar2=None, op0=ALU.mult)
            e.tensor_scalar(out=S, in0=sxxrow[0:1, :], scalar1=1.0 / D, scalar2=None, op0=ALU.mult)
            e.tensor_tensor(out=sxrow[0:1, :], in0=T4, in1=T4, op=ALU.mult)
            e.tensor_tensor(out=sxxrow[0:1, :], in0=S, in1=sxrow[0:1, :], op=ALU.subtract)
            e.tensor_scalar(out=S, in0=sxxrow[0:1, :], scalar1=EPS_LN, scalar2=None, op0=ALU.add)
            e.reciprocal(out=sxxrow[0:1, :], in_=S)
            dinc(e, srw.sem)
        pl.op("vector", rows1, waits=[(sf, base_sf + DC)])
        srw.bump()
        pl.op("scalar", lambda e: (e.activation(out=h2stg[0:1, 0:T], in_=sxxrow[0:1, :],
                                                func=AF.Sqrt), dinc(e, srw.sem)), waits=[(srw, srw.n)])
        srw.bump()
        def rows2(e):
            e = DrainWrap(e)
            T4 = tmp_stg[0:1, 0:T]
            S = h2stg[0:1, 0:T]
            e.tensor_tensor(out=sxrow[0:1, :], in0=T4, in1=S, op=ALU.mult)
            e.tensor_copy(out=sxxrow[0:1, :], in_=S)
            dinc(e, srw.sem)
        pl.op("vector", rows2, waits=[(srw, srw.n)])
        srw.bump()

    def gh_bcast(pl, srw_like):
        def ghm(e):
            for tc in range(NT4):
                e.matmul(psA[:, tc * 512:tc * 512 + 512], onesrow_f[0:1, :],
                         sxxrow[0:1, tc * 512:tc * 512 + 512], start=True, stop=True)
                ins = e.matmul(psB[:, tc * 512:tc * 512 + 512], onesrow_f[0:1, :],
                               sxrow[0:1, tc * 512:tc * 512 + 512], start=True, stop=True)
            ins.then_inc(srw_like.sem, 1)
        pl.op("tensor", ghm, waits=[(srw_like, srw_like.n)])
        srw_like.bump()

    pl = Plan()
    ln_stats(pl, "l1", x_sb)
    gh_bcast(pl, C["l1rw"])
    ghv = C["l1rw"].n
    pl.emit(nc)
    if "rows" in dbg_out:
        dump_sbuf("rows", [(dbg_out["rows"][0:1, :], sxxrow[0:1, :]), (dbg_out["rows"][1:2, :], sxrow[0:1, :])])
    if "raw" in dbg_out:
        dump_sbuf("raw", [(dbg_out["raw"][0:1, :], mhold2[0:1, 0:T]), (dbg_out["raw"][1:2, :], mhold2[0:1, T:2 * T])])
    pl = Plan()
    def apply1(e):
        for dc in range(DC):
            e.tensor_tensor(out=tmp_stg[:, :], in0=x_sb[:, dc * T:(dc + 1) * T], in1=psA[:, 0:T], op=ALU.mult)
            e.tensor_tensor(out=h2stg[:, :], in0=tmp_stg[:, :], in1=psB[:, 0:T], op=ALU.add)
            e.tensor_scalar(out=h1_sb[:, dc * T:(dc + 1) * T], in0=h2stg[:, :],
                            scalar1=capp_pp[:, dc:dc + 1], scalar2=sa_pp[:, dc:dc + 1],
                            op0=ALU.mult, op1=ALU.add)
    pl.op("vector", apply1, waits=[(C["l1rw"], ghv)])
    pl.emit(nc)

    dump_sbuf("h1", [(dbg_out.get("h1") and dbg_out["h1"][dc * P:(dc + 1) * P, :], h1_sb[:, dc * T:(dc + 1) * T])
                     for dc in range(DC)] if "h1" in dbg_out else [])
    if stage < 2:
        return nc, dbg_out

    pl = Plan()
    qw, qwf = C["qw"], C["qwf"]
    vone = C["vone"]
    pl.op("gpsimd", lambda e: (e.memset(
        v_sb[:].rearrange("p (t h c) -> p t h c", h=4, c=65)[:, :, :, 64:65], 1.0), dinc(e, vone.sem)))
    vone.bump()
    pjr, pjf = C["pjr"], C["pjf"]
    nbs, nbm, nbv, nbf = C["nbs"], C["nbm"], C["nbv"], C["nbf"]
    ats, atx, ata, atf = C["ats"], C["atx"], C["ata"], C["atf"]
    atr, atb, atrb, atu = C["atr"], C["atb"], C["atrb"], C["atu"]

    NPJ_CONST = 32
    for qt in range(4):
        waits = [(pjf, NPJ_CONST * qt)] if qt >= 1 else []
        c0 = qt * 256
        for wi, wd in enumerate((wq_d, wk_d, wv_d)):
            for dc in range(DC):
                qw.bump(16)
                pl.op("gpsimd", lambda e, wd=wd, wi=wi, dc=dc, c0=c0: e.dma_start(
                    out=wstg[:, wi * 2048 + dc * 256:wi * 2048 + (dc + 1) * 256],
                    in_=wd[dc * P:(dc + 1) * P, c0:c0 + 256]).then_inc(qw.sem, 16),
                    waits=waits if (wi == 0 and dc == 0) else [])
    wo_ok = C["wo_ok"]
    def wo_load(e):
        for fc in range(DC):
            e.dma_start(out=wo_stg[:, fc * D:(fc + 1) * D],
                        in_=wo_d[fc * P:(fc + 1) * P, :]).then_inc(wo_ok.sem, 16)
    pl.op("gpsimd", wo_load)
    wo_ok.bump(16 * DC)

    proj_tiles = [("q", oc, tc) for oc in range(2) for tc in range(NT4)] + \
                 [("k", oc, tc) for oc in range(2) for tc in range(NT4)] + \
                 [("v", tt, 0) for tt in range(16)]
    NPJ = len(proj_tiles)

    for qt in range(4):
        for i, (kind, a, tc) in enumerate(proj_tiles):
            gi = qt * NPJ + i
            waits = [(qw, 384 * (qt + 1))] if i == 0 else []
            if i == 0 and qt >= 1:
                waits.append((atu, 16 * qt))
            if gi >= 4:
                waits.append((pjf, gi - 3))
            sl = psB[:, (gi % 4) * 512:(gi % 4) * 512 + 512]
            def mm_pj(e, kind=kind, a=a, tc=tc, sl=sl):
                for dc in range(DC):
                    if kind == "q":
                        ins = e.matmul(sl, wstg[:, dc * 256 + a * 128:dc * 256 + a * 128 + 128],
                                 h1_sb[:, dc * T + tc * 512:dc * T + tc * 512 + 512],
                                 start=(dc == 0), stop=(dc == DC - 1))
                    elif kind == "k":
                        ins = e.matmul(sl, wstg[:, 2048 + dc * 256 + a * 128:2048 + dc * 256 + a * 128 + 128],
                                 h1_sb[:, dc * T + tc * 512:dc * T + tc * 512 + 512],
                                 start=(dc == 0), stop=(dc == DC - 1))
                    else:
                        ins = e.matmul(sl[0:P, 0:256], h1_sb[:, dc * T + a * 128:dc * T + a * 128 + 128],
                                 wstg[:, 4096 + dc * 256:4096 + dc * 256 + 256],
                                 start=(dc == 0), stop=(dc == DC - 1))
                ins.then_inc(pjr.sem, 1)
            pl.op("tensor", mm_pj, waits=waits)
            pjr.bump()
            def ev_pj(e, kind=kind, a=a, tc=tc, sl=sl):
                if kind == "q":
                    ins = e.tensor_copy(out=q_sb[:, a * T + tc * 512:a * T + tc * 512 + 512], in_=sl)
                elif kind == "k":
                    ins = e.tensor_copy(out=k_sb[:, a * T + tc * 512:a * T + tc * 512 + 512], in_=sl)
                else:
                    ins = e.tensor_copy(out=v_sb[:, a * 260:a * 260 + 260].rearrange("p (h c) -> p h c", c=65)[:, :, 0:64],
                                  in_=sl[0:P, 0:256].rearrange("p (h c) -> p h c", c=64))
                dinc(e, pjf.sem)
            pl.op("vector", ev_pj, waits=[(pjr, pjr.n)])
            pjf.bump()

        for half in range(4):
            src = q_sb if half < 2 else k_sb
            oc = half % 2
            sqw = [(pjf, qt * NPJ + 4 * (half + 1))] + ([(nbm, nbm.n)] if half > 0 else [])
            if half == 0 and qt >= 1:
                sqw.append((atf, 80 * qt))
            pl.op("scalar", lambda e, src=src, oc=oc: (e.square(
                out=qsq_stg[:, :], in_=src[:, oc * T:(oc + 1) * T]), dinc(e, nbs.sem)), waits=sqw)
            nbs.bump()
            for hh in range(2):
                def mm_nb2(e, hh=hh):
                    for tc in range(NT4):
                        ins = e.matmul(psA[0:1, tc * 512:tc * 512 + 512],
                                 ones_bf[:, 1 + hh:2 + hh],
                                 qsq_stg[:, tc * 512:tc * 512 + 512],
                                 start=True, stop=True)
                    ins.then_inc(nbm.sem, 1)
                w = [(nbs, nbs.n), (nbv, nbv.n)] if hh == 0 else [(nbv, nbv.n)]
                if half == 0 and hh == 0 and qt >= 1:
                    w.append((atx, 80 * qt))
                pl.op("tensor", mm_nb2, waits=w)
                nbm.bump()
                def rd_nb(e, half=half, hh=hh):
                    col = (half * 2 + hh)
                    for tc in range(NT4):
                        e.reduce_max(out=mqrow[0:1, col * 8 + tc:col * 8 + tc + 1],
                                     in_=psA[0:1, tc * 512:tc * 512 + 512], axis=AX.X)
                    dinc(e, nbv.sem)
                pl.op("vector", rd_nb, waits=[(nbm, nbm.n)])
                nbv.bump()
        def nb_comb(e, qt=qt):
            e = DrainWrap(e)
            for hl in range(4):
                e.reduce_max(out=mhold[0:1, hl * 8:hl * 8 + 1], in_=mqrow[0:1, hl * 8:hl * 8 + 4], axis=AX.X)
                e.reduce_max(out=mhold[0:1, hl * 8 + 1:hl * 8 + 2], in_=mqrow[0:1, 32 + hl * 8:32 + hl * 8 + 4], axis=AX.X)
            e.tensor_tensor(out=mhold[0:1, :].rearrange("o (h c) -> o h c", c=8)[:, 0:4, 2:3],
                            in0=mhold[0:1, :].rearrange("o (h c) -> o h c", c=8)[:, 0:4, 0:1],
                            in1=mhold[0:1, :].rearrange("o (h c) -> o h c", c=8)[:, 0:4, 1:2],
                            op=ALU.mult)
            dinc(e, nbv.sem)
        pl.op("vector", nb_comb)
        nbv.bump()
        pl.op("scalar", lambda e: (e.activation(
            out=mhold[0:1, :].rearrange("o (h c) -> o h c", c=8)[:, 0:4, 4:5],
            in_=mhold[0:1, :].rearrange("o (h c) -> o h c", c=8)[:, 0:4, 2:3],
            func=AF.Sqrt), dinc(e, nbs.sem)), waits=[(nbv, nbv.n)])
        nbs.bump()
        def nb_scale(e, qt=qt):
            e = DrainWrap(e)
            if qt == 0 and "nbdbg" in dbg_out:
                e.tensor_copy(out=mhold2[0:1, 0:64], in_=mqrow[0:1, :])
                e.tensor_copy(out=mhold2[0:1, 64:128], in_=mhold[0:1, :])
            e.tensor_scalar(out=negcrow[0:1, qt * 4:qt * 4 + 4],
                            in0=mhold[0:1, :].rearrange("o (h c) -> o h c", c=8)[:, 0:4, 4:5],
                            scalar1=-0.125, scalar2=None, op0=ALU.mult)
            dinc(e, nbv.sem)
        pl.op("vector", nb_scale, waits=[(nbs, nbs.n)])
        nbv.bump()
        def nb_bc(e, qt=qt):
            e.matmul(psA[:, 0:512][0:P, 0:4], onesrow_f[0:1, :], negcrow[0:1, qt * 4:qt * 4 + 4],
                     start=True, stop=True).then_inc(nbm.sem, 1)
        pl.op("tensor", nb_bc, waits=[(nbv, nbv.n)])
        nbm.bump()
        def nb_ev(e, qt=qt):
            e.tensor_copy(out=negc_pp[:, qt * 4:qt * 4 + 4],
                          in_=psA[:, 0:512][0:P, 0:4])
            dinc(e, nbf.sem)
        pl.op("vector", nb_ev, waits=[(nbm, nbm.n)])
        nbf.bump()

        for pr in range(2):
            hg = 4 * qt + 2 * pr
            for qc in range(NT4):
                nkt = 4 * qc + 4
                ngrp = nkt // 2
                uev_before = atu.n

                def emit_S(g, hh, pr=pr, qc=qc):
                    gidx = ats.n
                    waits = []
                    if gidx in (qt * 80, qt * 80 + 1):
                        waits.append((nbf, qt + 1))
                    if gidx >= 2:
                        waits.append((atx, gidx - 1))
                    def mm_s(e, pr=pr, qc=qc, g=g, hh=hh):
                        for half in range(2):
                            kt = 2 * g + half
                            ins = e.matmul(psA[:, hh * 1024 + half * 512:hh * 1024 + half * 512 + 512],
                                     k_sb[hh * 64:hh * 64 + 64, pr * T + kt * 128:pr * T + kt * 128 + 128],
                                     q_sb[hh * 64:hh * 64 + 64, pr * T + qc * 512:pr * T + qc * 512 + 512],
                                     start=True, stop=True)
                        ins.then_inc(ats.sem, 1)
                    pl.op("tensor", mm_s, waits=waits)
                    ats.bump()

                emit_S(0, 0)
                emit_S(0, 1)
                for g in range(ngrp):
                    diag = (2 * g + 1) >= 4 * qc
                    mi0 = 2 * g - 4 * qc
                    for hh in range(2):
                        gidx = atx.n
                        asl = a_tiles[2 * hh + g % 2]
                        ewaits = [(ats, gidx + 1)]
                        if gidx >= 4:
                            ewaits.append((atf, gidx - 3))
                        def ex_s(e, hh=hh, asl=asl, hg=hg):
                            e.activation(out=asl[:, :], in_=psA[:, hh * 1024:hh * 1024 + 1024],
                                         func=AF.Exp, bias=negc_pp[:, hg + hh:hg + hh + 1],
                                         scale=0.125).then_inc(atx.sem, 1)
                        pl.op("scalar", ex_s, waits=ewaits)
                        atx.bump()
                        if diag:
                            def mk_s(e, asl=asl, mi0=mi0):
                                e.tensor_tensor(out=asl[:, :], in0=asl[:, :],
                                                in1=masks_sb[:, mi0 * 512:mi0 * 512 + 1024],
                                                op=ALU.mult).then_inc(ata.sem, 1)
                            pl.op("vector", mk_s, waits=[(atx, atx.n)])
                            ata.bump(1)
                            a_ready = (ata, ata.n)
                        else:
                            a_ready = (atx, atx.n)
                        avw = [a_ready]
                        if g == 0 and hh == 0:
                            avw.append((atu, uev_before))
                            avw.append((pjf, NPJ * (qt + 1)))
                            if atf.n == 0:
                                avw.append((vone, 1))
                        def mm_av(e, pr=pr, g=g, hh=hh, asl=asl,
                                  first=(g == 0), last=(g == ngrp - 1)):
                            h1l = 2 * pr
                            for half in range(2):
                                kt = 2 * g + half
                                ins = e.matmul(psB[0:65, hh * 512:hh * 512 + 512],
                                         v_sb[:, kt * 260 + (h1l + hh) * 65:kt * 260 + (h1l + hh) * 65 + 65],
                                         asl[:, half * 512:half * 512 + 512],
                                         start=(first and half == 0), stop=(last and half == 1))
                            ins.then_inc(atf.sem, 1)
                        pl.op("tensor", mm_av, waits=avw)
                        atf.bump()
                        if g + 1 < ngrp:
                            emit_S(g + 1, hh)
                glast = atf.n
                for hh in range(2):
                    hge = 4 * qt + 2 * pr + hh
                    def sums_ev(e, hh=hh):
                        e = DrainWrap(e)
                        e.tensor_copy(out=srow[0:1, :], in_=psB[64:65, hh * 512:hh * 512 + 512])
                        e.reciprocal(out=rrow[0:1, :], in_=srow[0:1, :])
                        dinc(e, atr.sem)
                    w = [(atf, glast)] if hh == 0 else [(atrb, atrb.n)]
                    pl.op("vector", sums_ev, waits=w)
                    atr.bump()
                    def rb_bc(e):
                        e.matmul(psB[0:64, 1536:2048], onesrow_f[0:1, 0:64], rrow[0:1, :],
                                 start=True, stop=True).then_inc(atb.sem, 1)
                    pl.op("tensor", rb_bc, waits=[(atr, atr.n)])
                    atb.bump()
                    def u_ev(e, hh=hh, hg=hge, qc=qc):
                        e.tensor_copy(out=rb_sb[0:64, :], in_=psB[0:64, 1536:2048])
                        dinc(e, atrb.sem)
                        rbase = (hg % 2) * 64
                        e.tensor_tensor(out=u_sb[rbase:rbase + 64, (hg // 2) * T + qc * 512:(hg // 2) * T + qc * 512 + 512],
                                        in0=psB[0:64, hh * 512:hh * 512 + 512], in1=rb_sb[0:64, :],
                                        op=ALU.mult)
                        dinc(e, atu.sem)
                    pl.op("vector", u_ev, waits=[(atb, atb.n)])
                    atrb.bump(); atu.bump()
    pl.emit(nc)

    if "nbdbg" in dbg_out:
        dump_sbuf("nbdbg", [(dbg_out["nbdbg"][0:1, 0:NH], negc_pp[0:1, :]), (dbg_out["nbdbg"][1:2, 0:NH], negc_pp[127:128, :])])
    dump_sbuf("u", [(dbg_out.get("u") and dbg_out["u"][fc * P:(fc + 1) * P, :], u_sb[:, fc * T:(fc + 1) * T])
                    for fc in range(DC)] if "u" in dbg_out else [])
    if "negc" in dbg_out:
        dump_sbuf("negc", [(dbg_out["negc"][:, :], negcrow[0:1, :])])

    pl = Plan()
    opr, opv, opf = C["opr"], C["opv"], C["opf"]
    w1mx, w1rw = C["w1mx"], C["w1rw"]
    pl.op("tensor", lambda e: None, waits=[(wo_ok, wo_ok.n)])
    for dc in range(DC):
        for tc in range(NT4):
            gi = opr.n
            waits = [(opf, gi - 1)] if gi >= 2 else []
            sl = psA[:, (gi % 2) * 512:(gi % 2) * 512 + 512]
            def mm_op(e, dc=dc, tc=tc, sl=sl):
                for fc in range(DC):
                    ins = e.matmul(sl, wo_stg[:, fc * D + dc * 128:fc * D + dc * 128 + 128],
                                   u_sb[:, fc * T + tc * 512:fc * T + tc * 512 + 512],
                                   start=(fc == 0), stop=(fc == DC - 1))
                ins.then_inc(opr.sem, 1)
            pl.op("tensor", mm_op, waits=waits)
            opr.bump()
            def ev_op(e, dc=dc, tc=tc, sl=sl, gi=gi):
                e.scalar_tensor_tensor(out=op2stg[gi % 2][:, :],
                                       in0=sl, scalar=ga_pp[:, dc:dc + 1],
                                       in1=x_sb[:, dc * T + tc * 512:dc * T + tc * 512 + 512],
                                       op0=ALU.mult, op1=ALU.add)
                dinc(e, opv.sem)
            pl.op("vector", ev_op, waits=[(opr, opr.n)] + ([(opf, gi - 1)] if gi >= 2 else []))
            opv.bump()
            def wb_op(e, dc=dc, tc=tc, gi=gi):
                e.copy(out=x_sb[:, dc * T + tc * 512:dc * T + tc * 512 + 512],
                       in_=op2stg[gi % 2][:, :])
                dinc(e, opf.sem)
            pl.op("scalar", wb_op, waits=[(opv, opv.n)])
            opf.bump()
            fcm = gi
            bw1 = [(w1mx, fcm - 1)] if fcm >= 2 else []
            lw = dma2(pl, "sync", w1nstg[fcm % 2][:, :], w1n_d[fcm * P:(fcm + 1) * P, :],
                      w1mp, fcm, waits=bw1)
            def mx_w1(e, fcm=fcm):
                e.reduce_max(out=sw1[:, fcm:fcm + 1], in_=w1nstg[fcm % 2][:, :],
                             axis=AX.X, apply_absolute_value=True)
                dinc(e, w1mx.sem)
            pl.op("vector", mx_w1, waits=[lw])
            w1mx.bump()
    def sw1f(e):
        e = DrainWrap(e)
        e.tensor_scalar(out=sw1tmp[:, :], in0=sw1[:, :], scalar1=EPS_Q, scalar2=1.0 / 127.0,
                        op0=ALU.max, op1=ALU.mult)
        e.reciprocal(out=rw1[:, :], in_=sw1tmp[:, :])
        dinc(e, w1mx.sem)
    pl.op("vector", sw1f)
    w1mx.bump()
    dma(pl, "sync", rowbuf[0:1, 8 * D:8 * D + FF].rearrange("o (a b) -> o a b", b=P).transpose([0, 2, 1]),
        rw1[:, :], w1rw, ncd=True)
    pl.ops["sync"][-1] = ((pl.ops["sync"][-1][0] + ((w1mx, w1mx.n),)), pl.ops["sync"][-1][1])
    pl.emit(nc)

    dump_sbuf("x2", [(dbg_out.get("x2") and dbg_out["x2"][dc * P:(dc + 1) * P, :], x_sb[:, dc * T:(dc + 1) * T])
                     for dc in range(DC)] if "x2" in dbg_out else [])
    if stage < 3:
        return nc, dbg_out

    pl = Plan()
    ln_stats(pl, "l2", x_sb)
    gh_bcast(pl, C["l2rw"])
    ghv2 = C["l2rw"].n
    mxv, cc1, sc1c, gh2, rqv, xo = C["mxv"], C["cc1"], C["sc1c"], C["gh2"], C["rqv"], C["xo"]
    def maxpass(e):
        for dc in range(DC):
            e.tensor_tensor(out=tmp_stg[:, :], in0=x_sb[:, dc * T:(dc + 1) * T], in1=psA[:, 0:T], op=ALU.mult)
            e.tensor_tensor(out=h2stg[:, :], in0=tmp_stg[:, :], in1=psB[:, 0:T], op=ALU.add)
            e.tensor_scalar(out=tmp_stg[:, :], in0=h2stg[:, :],
                            scalar1=cbpp_pp[:, dc:dc + 1], scalar2=sb_pp[:, dc:dc + 1],
                            op0=ALU.mult, op1=ALU.add)
            e.reduce_max(out=mx8[:, dc:dc + 1], in_=tmp_stg[:, :], axis=AX.X)
        dinc(e, mxv.sem)
    pl.op("vector", maxpass, waits=[(C["l2rw"], ghv2)])
    mxv.bump()
    dma(pl, "sync", rowbuf[0:1, 4 * D:4 * D + 1024].rearrange("o (p c) -> (o p) c", c=8), mx8[:, :], xo)
    pl.ops["sync"][-1] = ((pl.ops["sync"][-1][0] + ((mxv, mxv.n),)), pl.ops["sync"][-1][1])
    dma(pl, "sync", h2stg[0:1, 0:1024], rowbuf[0:1, 4 * D:4 * D + 1024], xo)
    pl.ops["sync"][-1] = ((pl.ops["sync"][-1][0] + ((xo, xo.n - 16),)), pl.ops["sync"][-1][1])
    xov = xo.n
    def gmax_local(e):
        e = DrainWrap(e)
        e.memset(gmaxs[0:1, :], 0.0)
        e.reduce_max(out=gmaxs[0:1, 0:1], in_=h2stg[0:1, 0:1024], axis=AX.X)
        dinc(e, mxv.sem)
    pl.op("vector", gmax_local, waits=[(xo, xov)])
    mxv.bump()
    cc1c = C["cc1c"]
    def coll1(e):
        e.dma_start(out=cc_in1.ap(), in_=gmaxs[0:1, :]).then_inc(cc1.sem, 16)
        e.wait_ge(cc1.sem, 16)
        e.collective_compute("AllReduce", ALU.max, replica_groups=[list(range(8))],
                             ins=[cc_in1.ap().opt()], outs=[cc_out1.ap().opt()]).then_inc(cc1c.sem, 1)
        e.wait_ge(cc1c.sem, 1)
        e.dma_start(out=gmaxs[0:1, :], in_=cc_out1.ap()).then_inc(cc1.sem, 16)
    pl.op("gpsimd", coll1, waits=[(mxv, mxv.n)])
    cc1.bump(32); cc1c.bump(1)
    def s1row(e):
        e = DrainWrap(e)
        e.tensor_scalar(out=tiny[0:1, 0:8], in0=gmaxs[0:1, 0:8], scalar1=EPS_Q, scalar2=1.0 / 255.0,
                        op0=ALU.max, op1=ALU.mult)
        e.reciprocal(out=mqrow[0:1, 0:8], in_=tiny[0:1, 0:8])
        dinc(e, sc1c.sem)
    pl.op("vector", s1row, waits=[(cc1, cc1.n)])
    sc1c.bump()
    def s1bc(e):
        e.matmul(psA[:, 0:512][0:P, 0:1], onesrow_f[0:1, :], tiny[0:1, 0:1], start=True, stop=True)
        e.matmul(psA[:, 512:1024][0:P, 0:1], onesrow_f[0:1, :], mqrow[0:1, 0:1],
                 start=True, stop=True).then_inc(sc1c.sem, 1)
    pl.op("tensor", s1bc, waits=[(sc1c, sc1c.n)])
    sc1c.bump()
    def s1ev(e):
        e = DrainWrap(e)
        e.tensor_copy(out=s1pp[:, 0:1], in_=psA[:, 0:512][0:P, 0:1])
        e.tensor_copy(out=r1pp[:, 0:1], in_=psA[:, 512:1024][0:P, 0:1])
        e.tensor_scalar(out=cbr1_pp[:, :], in0=cbpp_pp[:, :], scalar1=r1pp[:, 0:1], scalar2=None, op0=ALU.mult)
        e.tensor_scalar(out=sbr1_pp[:, :], in0=sb_pp[:, :], scalar1=r1pp[:, 0:1], scalar2=None,
                        op0=ALU.mult)
        dinc(e, sc1c.sem)
    pl.op("vector", s1ev, waits=[(sc1c, sc1c.n)])
    sc1c.bump()
    pl.op("tensor", lambda e: None, waits=[(sc1c, sc1c.n)])
    gh_bcast(pl, gh2)
    gh2v = gh2.n
    for dc in range(DC):
        def rq(e, dc=dc):
            e.tensor_tensor(out=tmp_stg[:, :], in0=x_sb[:, dc * T:(dc + 1) * T], in1=psA[:, 0:T], op=ALU.mult)
            e.tensor_tensor(out=h2stg[:, :], in0=tmp_stg[:, :], in1=psB[:, 0:T], op=ALU.add)
            e.tensor_scalar(out=tmp_stg[:, :], in0=h2stg[:, :],
                            scalar1=cbr1_pp[:, dc:dc + 1], scalar2=sbr1_pp[:, dc:dc + 1],
                            op0=ALU.mult, op1=ALU.add)
            for tc in range(NT4):
                e.tensor_scalar(out=i32stg[:, :], in0=tmp_stg[:, tc * 512:tc * 512 + 512],
                                scalar1=0.0, scalar2=None, op0=ALU.max)
                e.tensor_copy(out=hq_sb[:, dc * T + tc * 512:dc * T + tc * 512 + 512], in_=i32stg[:, :])
            dinc(e, rqv.sem)
        w = [(sc1c, sc1c.n), (gh2, gh2v)] if dc == 0 else []
        pl.op("vector", rq, waits=w)
        rqv.bump()
        dma(pl, "sync", xbuf[dc, :, :], x_sb[:, dc * T:(dc + 1) * T], xo)
        pl.ops["sync"][-1] = ((pl.ops["sync"][-1][0] + ((rqv, rqv.n),)), pl.ops["sync"][-1][1])
    xo_done = xo.n
    pl.op("sync", lambda e: None, waits=[(xo, xo_done)])
    pl.emit(nc)

    if "sc1" in dbg_out:
        dump_sbuf("sc1", [(dbg_out["sc1"][:, :], tiny[0:1, :])])
    if "mxdbg" in dbg_out:
        dump_sbuf("mxdbg", [(dbg_out["mxdbg"][0:1, 0:DC], mx8[0:1, :]),
                            (dbg_out["mxdbg"][2:3, 0:P], h2stg[0:1, 0:P]),
                            (dbg_out["mxdbg"][3:4, 0:8], gmaxs[0:1, :])])
    dump_sbuf("hq", [(dbg_out.get("hq") and dbg_out["hq"][dc * P:(dc + 1) * P, :], hq_sb[:, dc * T:(dc + 1) * T])
                     for dc in range(DC)] if "hq" in dbg_out else [])
    if stage < 4:
        return nc, dbg_out

    pl = Plan()
    f1r, f1s, f1v, f1d = C["f1r"], C["f1s"], C["f1v"], C["f1d"]
    w1ld, w1rw, w1bc, w1qv, w1tt, w1f = (C[k] for k in ("w1ld", "w1rw", "w1bc", "w1qv", "w1tt", "w1f"))
    w2mx, w2rw, w2bc, w2qv, w2ld = (C[k] for k in ("w2mx", "w2rw", "w2bc", "w2qv", "w2ld"))
    cc2, sc2c, b2p = C["cc2"], C["sc2c"], C["b2p"]

    v_b1 = w1ld.bump(16)
    pl.op("sync", lambda e: e.dma_start(out=b1sb[:, :], in_=b1_d[:, :]).then_inc(w1ld.sem, 16))
    def b1prep(e):
        e = DrainWrap(e)
        e.tensor_scalar(out=p1c[:, :], in0=sw1tmp[:, :], scalar1=s1pp[:, 0:1], scalar2=None, op0=ALU.mult)
        e.reciprocal(out=pb1c[:, :], in_=p1c[:, :])
        e.tensor_tensor(out=i32r[:, :], in0=b1sb[:, :], in1=pb1c[:, :], op=ALU.mult)
        e.tensor_copy(out=bq1f[:, :], in_=i32r[:, :])
        e.tensor_tensor(out=pb1c[:, :], in0=p1c[:, :], in1=bq1f[:, :], op=ALU.mult)
        dinc(e, w1f.sem)
    pl.op("vector", b1prep, waits=[(w1ld, v_b1)])
    w1f.bump()

    def emit_w1_bcast(cc):
        rlw = dma2(pl, "sync", rwstg[cc % 2][0:1, :],
                   rowbuf[0:1, 8 * D + cc * 512:8 * D + cc * 512 + 512], w1rp, cc,
                   waits=[(w1rw, w1rw.n)] if cc == 0 else ([(w1bc, cc - 1)] if cc >= 2 else []))
        def bc_w1(e, cc=cc):
            e.matmul(psA[:, (cc % 2) * 512:(cc % 2) * 512 + 512], onesrow_f[0:1, :],
                     rwstg[cc % 2][0:1, :], start=True, stop=True).then_inc(w1bc.sem, 1)
        bw = [rlw]
        if cc >= 2:
            bw.append((w1qv, (cc - 1) * DC))
        pl.op("tensor", bc_w1, waits=bw)
        w1bc.bump()

    def emit_w1_quant(cc, dc):
        gidx = cc * DC + dc
        qlw = dma2(pl, "sync", w1tstg[dc % 2][:, :],
                   w1t_d[dc * P:(dc + 1) * P, cc * 512:cc * 512 + 512], w1qp, gidx,
                   waits=[(w1tt, gidx - 1)] if gidx >= 2 else [])
        def q_w1(e, cc=cc, dc=dc, gidx=gidx):
            e.tensor_tensor(out=i32w1[gidx % 2][:, :], in0=w1tstg[dc % 2][:, :],
                            in1=psA[:, (cc % 2) * 512:(cc % 2) * 512 + 512],
                            op=ALU.mult).then_inc(w1tt.sem, 1)
        vw = [qlw, (w1bc, cc + 1)]
        if gidx >= 2:
            vw.append((w1qv, gidx - 1))
        pl.op("vector", q_w1, waits=vw)
        w1tt.bump()
        def cp_w1(e, cc=cc, dc=dc, gidx=gidx):
            e.copy(out=wq1t_sb[:, dc * FF + cc * 512:dc * FF + cc * 512 + 512],
                   in_=i32w1[gidx % 2][:, :])
            dinc(e, w1qv.sem)
        pl.op("scalar", cp_w1, waits=[(w1tt, w1tt.n)])
        w1qv.bump()

    def emit_w2_rowmax(idx):
        dc2, piece = idx // 4, idx % 4
        lw2 = dma2(pl, "sync", w2nstg[idx % 2][:, :],
                   w2n_d[dc2 * P:(dc2 + 1) * P, piece * D:piece * D + D], w2mp, idx,
                   waits=[(w2mx, idx - 1)] if idx >= 2 else [])
        def mx_w2(e, dc2=dc2, piece=piece, idx=idx):
            e = DrainWrap(e)
            e.reduce_max(out=mx8[:, piece:piece + 1], in_=w2nstg[idx % 2][:, :],
                         axis=AX.X, apply_absolute_value=True)
            if piece == 3:
                e.reduce_max(out=sw2[:, dc2:dc2 + 1], in_=mx8[:, 0:4], axis=AX.X)
            dinc(e, w2mx.sem)
        pl.op("vector", mx_w2, waits=[lw2])
        w2mx.bump()

    def emit_w2_scales():
        def sw2f(e):
            e = DrainWrap(e)
            e.tensor_scalar(out=sw2tmp[:, :], in0=sw2[:, :], scalar1=EPS_Q, scalar2=1.0 / 127.0,
                            op0=ALU.max, op1=ALU.mult)
            e.reciprocal(out=rw2[:, :], in_=sw2tmp[:, :])
            dinc(e, w2mx.sem)
        pl.op("vector", sw2f)
        w2mx.bump()
        dma(pl, "sync", rowbuf[0:1, 0:D].rearrange("o (a b) -> o a b", b=P).transpose([0, 2, 1]),
            rw2[:, :], w2rw, ncd=True)
        pl.ops["sync"][-1] = ((pl.ops["sync"][-1][0] + ((w2mx, w2mx.n),)), pl.ops["sync"][-1][1])

    def emit_w2_rwload(cc):
        return dma2(pl, "sync", w2rwstg[cc][0:1, :], rowbuf[0:1, cc * 512:cc * 512 + 512],
                    w2rp, cc, waits=[(w2rw, w2rw.n)] if cc == 0 else [])

    def emit_w2_bcast(cc, rlw2):
        def bc_w2(e, cc=cc):
            e.matmul(psA[:, 1024 + (cc % 2) * 512:1024 + (cc % 2) * 512 + 512], onesrow_f[0:1, :],
                     w2rwstg[cc][0:1, :], start=True, stop=True).then_inc(w2bc.sem, 1)
        pl.op("tensor", bc_w2, waits=[rlw2])
        w2bc.bump()

    def emit_w2_quant(cc, fc2):
        gidx2 = cc * NFC + fc2
        qlw2 = dma2(pl, "sync", w2tstg[fc2 % 2][:, :],
                    w2t_d[fc2 * P:(fc2 + 1) * P, cc * 512:cc * 512 + 512], w2qp, gidx2,
                    waits=[(w2qv, gidx2 - 1)] if gidx2 >= 2 else [])
        def q_w2(e, cc=cc, fc2=fc2):
            e.tensor_tensor(out=i32stg[:, :], in0=w2tstg[fc2 % 2][:, :],
                            in1=psA[:, 1024 + (cc % 2) * 512:1024 + (cc % 2) * 512 + 512],
                            op=ALU.mult)
            e.tensor_copy(out=wq2t_sb[:, fc2 * D + cc * 512:fc2 * D + cc * 512 + 512],
                          in_=i32stg[:, :])
            dinc(e, w2qv.sem)
        pl.op("vector", q_w2, waits=[qlw2, (w2bc, cc + 1)])
        w2qv.bump()

    emit_w1_bcast(0)
    for dc in range(DC):
        emit_w1_quant(0, dc)
    pend_w1 = []
    pend_w2q = [(cc, f) for cc in range(2) for f in range(NFC)]
    w2mx_next = 0
    w2rl = [None, None]
    ti = 0
    NT_F1 = NT4 * NFC
    for fc in range(NFC):
        cc_next = fc // 4 + 1
        if fc % 4 == 0 and cc_next < DC:
            emit_w1_bcast(cc_next)
            pend_w1 = [(cc_next, dc) for dc in range(DC)]
        for tc in range(NT4):
            gi = ti; ti += 1
            waits = [(f1s, gi - 3)] if gi >= 4 else []
            if tc == 0:
                waits.append((w1qv, DC * (fc // 4 + 1)))
            sl = psB[:, (gi % 4) * 512:(gi % 4) * 512 + 512]
            def mm_f1(e, tc=tc, fc=fc, sl=sl):
                for dc in range(DC):
                    ins = e.matmul(sl, wq1t_sb[:, dc * FF + fc * 128:dc * FF + fc * 128 + 128],
                                   hq_sb[:, dc * T + tc * 512:dc * T + tc * 512 + 512],
                                   start=(dc == 0), stop=(dc == DC - 1))
                ins.then_inc(f1r.sem, 1)
            pl.op("tensor", mm_f1, waits=waits)
            f1r.bump()
            sw = [(f1r, f1r.n)]
            if gi == 0:
                sw.append((w1f, 1))
            if gi >= 2:
                sw.append((f1d, gi - 1))
            def ev_f1(e, gi=gi, fc=fc, sl=sl):
                e.activation(out=h3stg[gi % 2][:, :], in_=sl, func=AF.Relu,
                             scale=p1c[:, fc:fc + 1], bias=pb1c[:, fc:fc + 1])
                dinc(e, f1s.sem)
            pl.op("scalar", ev_f1, waits=sw)
            f1s.bump()
            def mxf1(e, gi=gi):
                e.reduce_max(out=mxf[:, gi:gi + 1], in_=h3stg[gi % 2][:, :], axis=AX.X)
                dinc(e, f1v.sem)
            pl.op("vector", mxf1, waits=[(f1s, f1s.n)])
            f1v.bump()
            stw = dma2(pl, "sync", h3buf[tc, fc, :, :], h3stg[gi % 2][:, :], f1hp, gi,
                       waits=[(f1s, f1s.n)])
            def fd_mark(e, gi=gi):
                e.nop().then_inc(f1d.sem, 1)
            pl.op("sync", fd_mark, waits=[stw, (f1v, f1v.n)])
            f1d.bump()
            if gi % 2 == 0 and pend_w1:
                emit_w1_quant(*pend_w1.pop(0))
            if gi % 2 == 1 and gi >= 4 and w2mx_next < 32:
                emit_w2_rowmax(w2mx_next); w2mx_next += 1
            if gi == 73:
                emit_w2_scales()
            if gi == 75:
                w2rl[0] = emit_w2_rwload(0)
            if gi == 77:
                emit_w2_bcast(0, w2rl[0])
            if gi == 81:
                w2rl[1] = emit_w2_rwload(1)
            if gi == 83:
                emit_w2_bcast(1, w2rl[1])
            if gi >= 80 and pend_w2q:
                emit_w2_quant(*pend_w2q.pop(0))

    def mxfin(e):
        e = DrainWrap(e)
        for c in range(8):
            e.reduce_max(out=mx8[:, c:c + 1], in_=mxf[:, c * 16:(c + 1) * 16], axis=AX.X)
        dinc(e, f1v.sem)
    pl.op("vector", mxfin, waits=[(f1v, NT_F1)])
    f1v.bump()
    v = C["xo"].bump(16)
    pl.op("sync", lambda e: e.dma_start(
        out=rowbuf[0:1, 4 * D:4 * D + 1024].rearrange("o (p c) -> (o p) c", c=8),
        in_=mx8[:, :]).then_inc(C["xo"].sem, 16),
        waits=[(f1v, NT_F1 + 1)])
    v2 = C["xo"].bump(16)
    pl.op("sync", lambda e: e.dma_start(out=h2stg[0:1, 0:1024],
                                        in_=rowbuf[0:1, 4 * D:4 * D + 1024]).then_inc(C["xo"].sem, 16),
          waits=[(C["xo"], v)])
    for _ in range(min(4, len(pend_w2q))):
        emit_w2_quant(*pend_w2q.pop(0))
    def gmax2_local(e):
        e = DrainWrap(e)
        e.memset(gmaxs[0:1, :], 0.0)
        e.reduce_max(out=gmaxs[0:1, 0:1], in_=h2stg[0:1, 0:1024], axis=AX.X)
        dinc(e, f1v.sem)
    pl.op("vector", gmax2_local, waits=[(C["xo"], v2)])
    f1v.bump(1)
    cc2c = C["cc2c"]
    def coll2(e):
        e.dma_start(out=cc_in2.ap(), in_=gmaxs[0:1, :]).then_inc(cc2.sem, 16)
        e.wait_ge(cc2.sem, 16)
        e.collective_compute("AllReduce", ALU.max, replica_groups=[list(range(8))],
                             ins=[cc_in2.ap().opt()], outs=[cc_out2.ap().opt()]).then_inc(cc2c.sem, 1)
        e.wait_ge(cc2c.sem, 1)
        e.dma_start(out=gmaxs[0:1, :], in_=cc_out2.ap()).then_inc(cc2.sem, 16)
    pl.op("gpsimd", coll2, waits=[(f1v, f1v.n)])
    cc2.bump(32); cc2c.bump(1)
    while pend_w2q:
        emit_w2_quant(*pend_w2q.pop(0))
    def s2row(e):
        e = DrainWrap(e)
        e.tensor_scalar(out=tiny[0:1, 0:8], in0=gmaxs[0:1, 0:8], scalar1=EPS_Q, scalar2=1.0 / 255.0,
                        op0=ALU.max, op1=ALU.mult)
        e.reciprocal(out=mqrow[0:1, 0:8], in_=tiny[0:1, 0:8])
        dinc(e, sc2c.sem)
    pl.op("vector", s2row, waits=[(cc2, cc2.n)])
    sc2c.bump()
    def s2bc(e):
        e.matmul(psB[:, 0:512][0:P, 0:1], onesrow_f[0:1, :], tiny[0:1, 0:1], start=True, stop=True)
        e.matmul(psB[:, 512:1024][0:P, 0:1], onesrow_f[0:1, :], mqrow[0:1, 0:1],
                 start=True, stop=True).then_inc(sc2c.sem, 1)
    pl.op("tensor", s2bc, waits=[(sc2c, sc2c.n), (f1s, NT_F1)])
    sc2c.bump()
    v = w2ld.bump(16)
    pl.op("sync", lambda e: e.dma_start(out=b2sb[:, :], in_=b2_d[:, :]).then_inc(w2ld.sem, 16))
    def b2prep(e):
        e = DrainWrap(e)
        e.tensor_copy(out=r2pp[:, 0:1], in_=psB[:, 512:1024][0:P, 0:1])
        e.tensor_copy(out=s2pp[:, 0:1], in_=psB[:, 0:512][0:P, 0:1])
        e.tensor_scalar(out=p2c[:, :], in0=sw2tmp[:, :], scalar1=s2pp[:, 0:1], scalar2=None, op0=ALU.mult)
        e.reciprocal(out=p2gbb[:, :], in_=p2c[:, :])
        e.tensor_tensor(out=i32r2[:, :], in0=b2sb[:, :], in1=p2gbb[:, :], op=ALU.mult)
        e.tensor_copy(out=bq2f[:, :], in_=i32r2[:, :])
        e.tensor_tensor(out=p2gb[:, :], in0=p2c[:, :], in1=gb_pp[:, :], op=ALU.mult)
        e.tensor_tensor(out=p2gbb[:, :], in0=p2gb[:, :], in1=bq2f[:, :], op=ALU.mult)
        dinc(e, b2p.sem)
    pl.op("vector", b2prep, waits=[(sc2c, sc2c.n), (w2ld, v)])
    b2p.bump()
    pl.emit(nc)

    if "h3dbg" in dbg_out:
        plh = Plan()
        c = C["dbgc"]
        dma(plh, "sync", dbg_out["h3dbg"][:, :, :, :], h3buf[:, :, :, :], c)
        tgt = c.n
        plh.op("sync", lambda e: None, waits=[(c, tgt)])
        plh.emit(nc)
    if "sc2" in dbg_out:
        dump_sbuf("sc2", [(dbg_out["sc2"][:, :], tiny[0:1, :])])
    if "wq2dbg" in dbg_out:
        dump_sbuf("wq2dbg", [(dbg_out["wq2dbg"][:, :], wq2t_sb[:, :])])
    if "b2dbg" in dbg_out:
        dump_sbuf("b2dbg", [(dbg_out["b2dbg"][0:1, 0:DC], p2gb[0:1, :]),
                            (dbg_out["b2dbg"][1:2, 0:DC], p2gbb[0:1, :]),
                            (dbg_out["b2dbg"][2:3, 0:DC], bq2f[0:1, :]),
                            (dbg_out["b2dbg"][3:4, 0:DC], sw2[0:1, :]),
                            (dbg_out["b2dbg"][4:5, 0:DC], sw2tmp[0:1, :]),
                            (dbg_out["b2dbg"][5:6, 0:DC], gb_pp[0:1, :])])
    if "mx2dbg" in dbg_out:
        dump_sbuf("mx2dbg", [(dbg_out["mx2dbg"][0:1, 0:P], mxf[0:1, :]),
                             (dbg_out["mx2dbg"][1:2, 0:8], mx8[0:1, :]),
                             (dbg_out["mx2dbg"][2:3, 0:P], h2stg[0:1, 0:P]),
                             (dbg_out["mx2dbg"][3:4, 0:8], gmaxs[0:1, :])])
    if stage < 5:
        return nc, dbg_out

    pl = Plan()
    rbq, f2r, f2s, f2v, f2x, yo = (C[k] for k in ("rbq", "f2r", "f2s", "f2v", "f2x", "yo"))
    for tc in range(NT4):
        for fc in range(NFC):
            qi = tc * NFC + fc
            w = [(rbq, qi - 5)] if qi >= 6 else []
            if tc >= 2 and fc == 0:
                w.append((f2r, (tc - 1) * DC))
            c6 = rbl6[qi % 6]
            v6 = c6.bump(16)
            pl.op("sync", lambda e, tc=tc, fc=fc, qi=qi, c6=c6: e.dma_start(
                out=rbstg[qi % 6][:, :], in_=h3buf[tc, fc, :, :]).then_inc(c6.sem, 16), waits=w)
            rlw3 = (c6, v6)
            def q_rb(e, tc=tc, fc=fc, qi=qi):
                e.tensor_scalar(out=i32b_stg[:, :], in0=rbstg[qi % 6][:, :],
                                scalar1=r2pp[:, 0:1], scalar2=None, op0=ALU.mult)
                e.tensor_copy(out=h3q_sb[tc % 2][:, fc * 512:fc * 512 + 512],
                              in_=i32b_stg[:, :])
                dinc(e, rbq.sem)
            qrw = [rlw3]
            if qi == 0:
                qrw.append((C["b2p"], 1))
            pl.op("vector", q_rb, waits=qrw)
            rbq.bump()
        for dc in range(DC):
            gi = tc * DC + dc
            waits = [(rbq, (tc + 1) * NFC)]
            if gi >= 2:
                waits.append((f2s, gi - 1))
            sl = psA[:, (gi % 2) * 512:(gi % 2) * 512 + 512]
            def mm_f2(e, tc=tc, dc=dc, sl=sl):
                for fc in range(NFC):
                    ins = e.matmul(sl, wq2t_sb[:, fc * D + dc * 128:fc * D + dc * 128 + 128],
                                   h3q_sb[tc % 2][:, fc * 512:fc * 512 + 512],
                                   start=(fc == 0), stop=(fc == NFC - 1))
                ins.then_inc(f2r.sem, 1)
            pl.op("tensor", mm_f2, waits=waits)
            f2r.bump()
            xlw = dma2(pl, "sync", xinstg[gi % 2][:, :], xbuf[dc, :, tc * 512:tc * 512 + 512],
                       f2xp, gi, waits=[(f2v, gi - 1)] if gi >= 2 else [])
            def ev_f2s(e, dc=dc, sl=sl, gi=gi):
                e.activation(out=o1stg[gi % 2][:, :], in_=sl, func=AF.Identity,
                             scale=p2gb[:, dc:dc + 1], bias=p2gbb[:, dc:dc + 1])
                dinc(e, f2s.sem)
            sw2_ = [(f2r, f2r.n)]
            if gi >= 2:
                sw2_.append((f2v, gi - 1))
            pl.op("scalar", ev_f2s, waits=sw2_)
            f2s.bump()
            def ev_f2v(e, gi=gi):
                e.tensor_tensor(out=o2stg[gi % 2][:, :], in0=o1stg[gi % 2][:, :], in1=xinstg[gi % 2][:, :],
                                op=ALU.add)
                dinc(e, f2v.sem)
            evw = [(f2s, f2s.n), xlw]
            if gi >= 2:
                evw.append((yop[gi % 2], 16 * (gi // 2)))
            pl.op("vector", ev_f2v, waits=evw)
            f2v.bump()
            dma2(pl, "sync", y_out[dc * P:(dc + 1) * P, tc * 512:tc * 512 + 512],
                 o2stg[gi % 2][:, :], yop, gi, waits=[(f2v, f2v.n)])
    pl.op("sync", lambda e: None, waits=[(yop[0], yop[0].n), (yop[1], yop[1].n)])
    pl.emit(nc)

    return nc, dbg_out


def make_host_masks():
    m = np.zeros((4, P, 512), dtype=BF)
    for kt in range(4):
        for r in range(P):
            m[kt, r, kt * P + r:] = 1.0
    return m


def prep_inputs(inputs):
    x = inputs["L0_attn_x"]
    w1 = np.ascontiguousarray(inputs["L0_ffn_w1"])
    w2 = np.ascontiguousarray(inputs["L0_ffn_w2"])
    shared = {
        "wq": np.ascontiguousarray(inputs["L0_attn_WQ"]),
        "wk": np.ascontiguousarray(inputs["L0_attn_WK"]),
        "wv": np.ascontiguousarray(inputs["L0_attn_WV"]),
        "wo": np.ascontiguousarray(inputs["L0_attn_WO"]),
        "wmod": np.ascontiguousarray(inputs["L0_wmod"]),
        "w1n": w1,
        "w1t": np.ascontiguousarray(w1.T),
        "w2n": w2,
        "w2t": np.ascontiguousarray(w2.T),
        "b1c": np.ascontiguousarray(inputs["L0_ffn_b1"].reshape(NFC, P).T),
        "b2c": np.ascontiguousarray(inputs["L0_ffn_b2"].reshape(DC, P).T),
        "masks": make_host_masks(),
    }
    cores = []
    for b in range(8):
        d = dict(shared)
        d["x_fm"] = np.ascontiguousarray(x[b].T)
        d["crow"] = np.ascontiguousarray(inputs["L0_c"][b:b + 1, :])
        cores.append(d)
    return cores


def kernel(**inputs):
    cores = prep_inputs(inputs)
    nc, _ = build_nc(stage=5)
    res = run_bass_kernel_spmd(nc, cores, core_ids=list(range(8)))
    out = np.stack([r["y_fm"].T for r in res.results], axis=0)
    return np.ascontiguousarray(out, dtype=np.float32)


# version 03
# speedup vs baseline: 1.1223x; 1.0267x over previous
import numpy as np
import ml_dtypes
import concourse.bass as bass
import concourse.mybir as mybir
from concourse.bass_utils import run_bass_kernel_spmd

F32 = mybir.dt.float32
BF16 = mybir.dt.bfloat16
I32 = mybir.dt.int32
ALU = mybir.AluOpType
AF = mybir.ActivationFunctionType
AX = mybir.AxisListType
BF = ml_dtypes.bfloat16

P = 128
D = 1024; DC = 8
T = 2048; NT4 = 4
NH = 16
FF = 4096; NFC = 32
EPS_LN = 1e-5
EPS_Q = 1e-8

OFF_X = 17408
OFF_WQ2T = 17408
OFF_B = 82944
OFF_QSB = OFF_B
OFF_KSB = OFF_B + 8192
OFF_VSB = OFF_B + 16384
OFF_USB = OFF_B + 24704
OFF_MASKS = OFF_B + 57472
OFF_WQ1T = OFF_B
OFF_H3Q0 = OFF_B
OFF_H3Q1 = OFF_B + 32768
OFF_SXROW = OFF_B
OFF_SXXROW = OFF_B + 8192
OFF_XBF = OFF_B + 24704
OFF_MH2 = OFF_B + 41088
OFF_C = 148480
OFF_D = 181248
OFF_ROWS = 214016
OFF_E = 226304
OFF_MODROW = OFF_B

ENGS = ("sync", "gpsimd", "tensor", "vector", "scalar")


class Ctr:
    __slots__ = ("sem", "n")
    def __init__(self, sem):
        self.sem = sem
        self.n = 0
    def bump(self, k=1):
        self.n += k
        return self.n


SIM_DRAINS = [False]


def dinc(e, sem, k=1):
    e.drain().then_inc(sem, k)


class DrainWrap:
    __slots__ = ("_e",)
    def __init__(self, e):
        object.__setattr__(self, "_e", e)
    def __getattr__(self, n):
        a = getattr(self._e, n)
        if callable(a) and n not in ("wait_ge", "drain", "nop", "sem_inc"):
            def w(*args, **kw):
                self._e.drain()
                return a(*args, **kw)
            return w
        return a


class Plan:
    def __init__(self):
        self.ops = {e: [] for e in ENGS}
    def op(self, eng, fn, waits=()):
        self.ops[eng].append((tuple(waits), fn))
    def emit(self, nc):
        with nc.Block() as b:
            for eng in ENGS:
                ops = self.ops[eng]
                if not ops:
                    continue
                def body(e, ops=ops):
                    if SIM_DRAINS[0]:
                        e = DrainWrap(e)
                    for waits, fn in ops:
                        for (c, v) in waits:
                            e.wait_ge(c.sem, v)
                        fn(e)
                getattr(b, eng)(body)


def build_nc(stage=5, debug=(), sim=False, skip_mod=False):
    SIM_DRAINS[0] = bool(sim)
    nc = bass.Bass("TRN2", target_bir_lowering=False, debug=False, dynamic_dma_scratch_size=16384,
                   detect_race_conditions=bool(sim))
    dbg = set(debug)

    x_in = nc.dram_tensor("x_fm", [D, T], F32, kind="ExternalInput").ap()
    wq_d = nc.dram_tensor("wq", [D, D], F32, kind="ExternalInput").ap()
    wk_d = nc.dram_tensor("wk", [D, D], F32, kind="ExternalInput").ap()
    wv_d = nc.dram_tensor("wv", [D, D], F32, kind="ExternalInput").ap()
    wo_d = nc.dram_tensor("wo", [D, D], F32, kind="ExternalInput").ap()
    wmod_d = nc.dram_tensor("wmod", [D, 6 * D], F32, kind="ExternalInput").ap()
    crow_d = nc.dram_tensor("crow", [1, D], F32, kind="ExternalInput").ap()
    w1n_d = nc.dram_tensor("w1n", [FF, D], F32, kind="ExternalInput").ap()
    w1t_d = nc.dram_tensor("w1t", [D, FF], F32, kind="ExternalInput").ap()
    w2n_d = nc.dram_tensor("w2n", [D, FF], F32, kind="ExternalInput").ap()
    w2t_d = nc.dram_tensor("w2t", [FF, D], F32, kind="ExternalInput").ap()
    b1_d = nc.dram_tensor("b1c", [P, NFC], F32, kind="ExternalInput").ap()
    b2_d = nc.dram_tensor("b2c", [P, DC], F32, kind="ExternalInput").ap()
    mask_d = nc.dram_tensor("masks", [4, P, 512], BF16, kind="ExternalInput").ap()
    y_out = nc.dram_tensor("y_fm", [D, T], F32, kind="ExternalOutput").ap()

    cc_in1 = nc.dram_tensor("cc_in1", [1, 8], F32)
    cc_out1 = nc.dram_tensor("cc_out1", [1, 8], F32)
    cc_in2 = nc.dram_tensor("cc_in2", [1, 8], F32)
    cc_out2 = nc.dram_tensor("cc_out2", [1, 8], F32)
    h3buf = nc.dram_tensor("h3buf", [NT4, NFC, P, 512], F32).ap()
    xbuf = nc.dram_tensor("xbuf", [DC, P, T], F32).ap()
    rowbuf = nc.dram_tensor("rowbuf", [1, 16 * D], F32).ap()

    AL = lambda name, shape, dt, off: nc.alloc_sbuf_tensor_at(name, shape, dt, offset=off)
    x_sb = AL("x_sb", [P, DC * T], F32, OFF_X)
    q_sb = AL("q_sb", [P, 2 * T], BF16, OFF_QSB)
    k_sb = AL("k_sb", [P, 2 * T], BF16, OFF_KSB)
    v_sb = AL("v_sb", [P, 16 * 260], BF16, OFF_VSB)
    u_sb = AL("u_sb", [P, 8 * T], BF16, OFF_USB)
    wq1t_sb = AL("wq1t_sb", [P, DC * FF], BF16, OFF_WQ1T)
    wq2t_sb = AL("wq2t_sb", [P, NFC * D], BF16, OFF_WQ2T)
    h3q_sb = [AL("h3q0_sb", [P, NFC * 512], BF16, OFF_H3Q0),
              AL("h3q1_sb", [P, NFC * 512], BF16, OFF_H3Q1)]
    h1_sb = AL("h1_sb", [P, DC * T], BF16, OFF_C)
    hq_sb = AL("hq_sb", [P, DC * T], BF16, OFF_C)
    modrow = AL("modrow", [1, 6 * D], F32, OFF_MODROW)

    xbf_stg = [AL("xbf_stg0", [P, T], BF16, OFF_XBF), AL("xbf_stg1", [P, T], BF16, OFF_XBF + 4096)]
    xsq_stg = [AL("xsq_stg0", [P, T], BF16, OFF_XBF + 8192), AL("xsq_stg1", [P, T], BF16, OFF_XBF + 12288)]
    wmod_stg = [AL("wmod_stg0", [P, DC * 512], BF16, OFF_D),
                AL("wmod_stg1", [P, DC * 512], BF16, OFF_D + 8192)]
    wmod32_stg = AL("wmod32_stg", [P, DC * 512], F32, OFF_D + 16384)
    wstg = AL("wstg", [P, 3 * DC * 256], BF16, OFF_D)
    qsq_stg = AL("qsq_stg", [P, T], BF16, OFF_D + 28672)
    wo_stg = AL("wo_stg", [P, DC * D], BF16, OFF_D + 12288)
    tmp_stg = AL("tmp_stg", [P, T], F32, OFF_D)
    h2stg = AL("h2stg", [P, T], F32, OFF_D + 8192)
    w1tstg = [AL("w1tstg0", [P, 512], F32, OFF_D + 16384), AL("w1tstg1", [P, 512], F32, OFF_D + 18432)]
    i32stg = AL("i32stg", [P, 512], I32, OFF_D + 20480)
    rwstg = [AL("rwstg0", [1, 512], F32, OFF_D + 22528), AL("rwstg1", [1, 512], F32, OFF_ROWS)]
    h3stg = [AL("h3stg0", [P, 512], F32, OFF_D + 24576), AL("h3stg1", [P, 512], F32, OFF_D + 26624)]
    w2nstg = [AL("w2nstg0", [P, D], F32, OFF_D), AL("w2nstg1", [P, D], F32, OFF_D + 4096)]
    w1nstg = [AL("w1nstg0", [P, D], F32, OFF_D), AL("w1nstg1", [P, D], F32, OFF_D + 8192)]
    i32w1 = [AL("i32w1_0", [P, 512], I32, OFF_D + 12288), AL("i32w1_1", [P, 512], I32, OFF_D + 14336)]
    w2rwstg = [AL("w2rwstg0", [1, 512], F32, OFF_ROWS + 4096), AL("w2rwstg1", [1, 512], F32, OFF_ROWS + 6144)]
    w2tstg = [AL("w2tstg0", [P, 512], F32, OFF_D + 28672), AL("w2tstg1", [P, 512], F32, OFF_D + 30720)]
    rbstg = [AL("rbstg%d" % i, [P, 512], F32, OFF_D + off) for i, off in
             enumerate((0, 2048, 4096, 20480, 22528, 24576))]
    op1stg = [AL("op1stg0", [P, 512], F32, OFF_D), AL("op1stg1", [P, 512], F32, OFF_D + 2048)]
    op2stg = [AL("op2stg0", [P, 512], F32, OFF_D + 4096), AL("op2stg1", [P, 512], F32, OFF_D + 6144)]
    i32b_stg = AL("i32b_stg", [P, 512], I32, OFF_D + 6144)
    o1stg = [AL("o1stg0", [P, 512], F32, OFF_D + 8192), AL("o1stg1", [P, 512], F32, OFF_D + 18432)]
    o2stg = [AL("o2stg0", [P, 512], F32, OFF_D + 10240), AL("o2stg1", [P, 512], F32, OFF_D + 12288)]
    xinstg = [AL("xinstg0", [P, 512], F32, OFF_D + 14336), AL("xinstg1", [P, 512], F32, OFF_D + 16384)]

    sxrow = AL("sxrow", [1, T], F32, OFF_SXROW)
    sxxrow = AL("sxxrow", [1, T], F32, OFF_SXXROW)
    rrow2 = AL("rrow2", [1, 1024], F32, OFF_ROWS)
    a_tiles = [AL("a_tile0", [P, 1024], BF16, OFF_ROWS + 4096),
               AL("a_tile1", [P, 1024], BF16, OFF_ROWS + 6144),
               AL("a_tile2", [P, 1024], BF16, OFF_D + 28672),
               AL("a_tile3", [P, 1024], BF16, OFF_D + 30720)]
    rb2_sb = AL("rb2_sb", [64, 1024], F32, OFF_ROWS + 8192)
    mqrow = AL("mqrow", [1, 64], F32, OFF_ROWS + 2048)
    mhold = AL("mhold", [1, 64], F32, OFF_ROWS + 2304)
    negcrow = AL("negcrow", [1, NH], F32, OFF_ROWS + 2560)
    mhold2 = AL("mhold2", [1, 2 * T], F32, OFF_MH2)

    e = OFF_E
    def EA(name, shape, dt):
        nonlocal e
        t = AL(name, shape, dt, e)
        itm = {BF16: 2, F32: 4, I32: 4}[dt]
        nbytes = int(np.prod(shape[1:])) * itm
        e += -(-nbytes // 32) * 32
        return t
    masks_sb = AL("masks_sb", [P, 4 * 512], BF16, OFF_MASKS)
    sa_pp = EA("sa_pp", [P, DC], F32)
    sb_pp = EA("sb_pp", [P, DC], F32)
    ga_pp = EA("ga_pp", [P, DC], F32)
    gb_pp = EA("gb_pp", [P, DC], F32)
    capp_pp = EA("capp_pp", [P, DC], F32)
    cbpp_pp = EA("cbpp_pp", [P, DC], F32)
    cbr1_pp = EA("cbr1_pp", [P, DC], F32)
    sbr1_pp = EA("sbr1_pp", [P, DC], F32)
    ccol = EA("ccol", [P, DC], F32)
    scol = EA("scol", [P, DC], BF16)
    scol32 = EA("scol32", [P, DC], F32)
    ones_bf = EA("ones_bf", [P, 8], BF16)
    onesrow_f = EA("onesrow_f", [1, P], F32)
    negc_pp = EA("negc_pp", [P, NH], F32)
    sw1 = EA("sw1", [P, NFC], F32)
    sw1tmp = EA("sw1tmp", [P, NFC], F32)
    sw2tmp = EA("sw2tmp", [P, DC], F32)
    rw1 = EA("rw1", [P, NFC], F32)
    p1c = EA("p1c", [P, NFC], F32)
    pb1c = EA("pb1c", [P, NFC], F32)
    b1sb = EA("b1sb", [P, NFC], F32)
    bq1f = EA("bq1f", [P, NFC], F32)
    i32r = EA("i32r", [P, NFC], I32)
    sw2 = EA("sw2", [P, DC], F32)
    rw2 = EA("rw2", [P, DC], F32)
    p2c = EA("p2c", [P, DC], F32)
    p2gb = EA("p2gb", [P, DC], F32)
    p2gbb = EA("p2gbb", [P, DC], F32)
    b2sb = EA("b2sb", [P, DC], F32)
    bq2f = EA("bq2f", [P, DC], F32)
    i32r2 = EA("i32r2", [P, DC], I32)
    s1pp = EA("s1pp", [P, 1], F32)
    s2pp = EA("s2pp", [P, 1], F32)
    r1pp = EA("r1pp", [P, 1], F32)
    r2pp = EA("r2pp", [P, 1], F32)
    mx8 = EA("mx8", [P, DC], F32)
    mxf = EA("mxf", [P, NT4 * NFC], F32)
    gmaxs = EA("gmaxs", [1, 8], F32)
    tiny = EA("tiny", [1, 8], F32)
    assert e <= 224 * 1024 - 32, e

    psA = nc.alloc_psum_tensor("psA", [P, T], F32)
    psB = nc.alloc_psum_tensor("psB", [P, T], F32)

    dbg_out = {}
    for name, shp in (("mod", [1, 6 * D]), ("h1", [D, T]), ("rows", [2, T]), ("raw", [2, T]), ("u", [D, T]),
                      ("negc", [1, NH]), ("x2", [D, T]), ("hq", [D, T]),
                      ("sc1", [1, 8]), ("mxdbg", [4, P]), ("nbdbg", [2, 64]), ("mx2dbg", [4, P]), ("wq1dbg", [P, DC * FF]),
                      ("h3dbg", [NT4, NFC, P, 512]), ("sc2", [1, 8]), ("wq2dbg", [P, NFC * D]), ("b2dbg", [6, P])):
        if name in dbg:
            dbg_out[name] = nc.dram_tensor("dbg_" + name, shp, F32, kind="ExternalOutput").ap()

    C = {}
    def ctr(name):
        C[name] = Ctr(nc.alloc_semaphore(name))
        return C[name]
    for n in ("ld0", "silu", "m32", "mw", "mwf", "mr", "me", "pp", "vone",
              "l1sv", "l1ss", "l1sf", "l1mm", "l1rw",
              "qw", "qwf", "pjr", "pjf", "nbs", "nbm", "nbv", "nbf",
              "ats", "atx", "ata", "atf", "atr", "atb", "atrb", "atu",
              "opr", "opv", "opf", "wo_ok",
              "l2sv", "l2ss", "l2sf", "l2mm", "l2rw",
              "mxv", "cc1", "cc1c", "sc1c", "gh2", "rqv", "xo",
              "w1ld", "w1mx", "w1rw", "w1bc", "w1qv", "w1tt", "w1f",
              "f1r", "f1s", "f1v", "f1d",
              "w2ld", "w2mx", "w2rw", "w2bc", "w2qv",
              "cc2", "cc2c", "sc2c", "b2p",
              "rbq", "f2r", "f2s", "f2v", "f2x", "yo", "dbgc"):
        ctr(n)

    def pair(name):
        return (ctr(name + "_a"), ctr(name + "_b"))
    mwp = pair("mwp"); w1mp = pair("w1mp"); w1qp = pair("w1qp"); w1rp = pair("w1rp")
    f1hp = pair("f1hp"); w2mp = pair("w2mp"); w2qp = pair("w2qp"); w2rp = pair("w2rp")
    rblp = pair("rblp"); f2xp = pair("f2xp"); yop = pair("yop")
    rbl6 = [ctr("rbl6_%d" % i) for i in range(6)]

    def dma2(pl, eng, out, in_, pr, idx, waits=(), ncd=False):
        c = pr[idx % 2]
        v = c.bump(16)
        pl.op(eng, lambda e, o=out, i=in_, cc=c, ncd=ncd: e.dma_start(
            out=o, in_=i, allow_slow_non_contiguous=ncd).then_inc(cc.sem, 16), waits=waits)
        return (c, v)

    def dma(pl, eng, out, in_, c, ncd=False):
        v = c.bump(16)
        pl.op(eng, lambda e, o=out, i=in_, cc=c, ncd=ncd: e.dma_start(
            out=o, in_=i, allow_slow_non_contiguous=ncd).then_inc(cc.sem, 16))
        return v

    def dump_sbuf(name, pieces):
        if name not in dbg_out:
            return
        pl = Plan()
        c = C["dbgc"]
        for dr, sb in pieces:
            dma(pl, "gpsimd", dr, sb, c)
        tgt = c.n
        pl.op("gpsimd", lambda e, t=tgt: None, waits=[(c, tgt)])
        pl.emit(nc)

    pl = Plan()
    ld0, silu_c, mw, mwf, mr, me = C["ld0"], C["silu"], C["mw"], C["mwf"], C["mr"], C["me"]
    for dc in range(DC):
        dma(pl, "sync", x_sb[:, dc * T:(dc + 1) * T], x_in[dc * P:(dc + 1) * P, :], ld0)
    dma(pl, "sync", masks_sb[:].rearrange("p (m c) -> p m c", c=512), mask_d[:, :, :].transpose([1, 0, 2]), ld0)
    dma(pl, "sync", ccol[:, :], crow_d[0:1, :].rearrange("o (c p) -> p (o c)", p=P), ld0, ncd=True)
    ld0_all = ld0.n
    pl.op("gpsimd", lambda e: e.memset(ones_bf[:], 1.0))
    pl.op("gpsimd", lambda e: e.memset(ones_bf[:, 1:3], 0.0))
    pl.op("gpsimd", lambda e: e.memset(ones_bf[0:64, 1:2], 1.0))
    pl.op("gpsimd", lambda e: e.memset(ones_bf[64:128, 2:3], 1.0))
    pl.op("gpsimd", lambda e: e.memset(onesrow_f[:], 1.0))
    pl.op("gpsimd", lambda e: e.memset(tiny[:], 0.0))
    mw_waits = []
    J32 = (4, 5, 10, 11)
    for j in range(0 if skip_mod == "all" else 12):
        waits = ([(mr, j - 1)] if j >= 2 else []) if not skip_mod else []
        if j in J32:
            c = C["m32"]
            w32 = ([(mr, j)] if j >= 1 else []) if not skip_mod else []
            for dc in range(DC):
                v = c.bump(16)
                pl.op("sync", lambda e, j=j, dc=dc, cc=c: e.dma_start(
                    out=wmod32_stg[:, dc * 512:(dc + 1) * 512],
                    in_=wmod_d[dc * P:(dc + 1) * P, j * 512:(j + 1) * 512]).then_inc(cc.sem, 16),
                    waits=w32 if dc == 0 else [])
            mw_waits.append((c, c.n))
        else:
            c = mwp[j % 2]
            for dc in range(DC):
                v = c.bump(16)
                pl.op("gpsimd", lambda e, j=j, dc=dc, cc=c: e.dma_start(
                    out=wmod_stg[j % 2][:, dc * 512:(dc + 1) * 512],
                    in_=wmod_d[dc * P:(dc + 1) * P, j * 512:(j + 1) * 512]).then_inc(cc.sem, 16),
                    waits=waits if dc == 0 else [])
            mw_waits.append((c, c.n))
    pl.op("scalar", lambda e: (e.activation(out=scol32[:, :], in_=ccol[:, :], func=AF.Silu),
                               dinc(e, silu_c.sem)), waits=[(ld0, ld0_all)])
    silu_c.bump()
    pl.op("vector", lambda e: (e.tensor_copy(out=scol[:, :], in_=scol32[:, :]),
                               dinc(e, silu_c.sem)), waits=[(silu_c, silu_c.n)])
    silu_c.bump()
    for j in range(12 if not skip_mod else 0):
        waits = [mw_waits[j]]
        if j <= 1:
            waits.append((silu_c, 2))
        if j >= 4:
            waits.append((me, j - 3))
        def mm_mod(e, j=j):
            if j in J32:
                for dc in range(DC):
                    ins = e.matmul(psA[0:1, (j % 4) * 512:(j % 4) * 512 + 512],
                                   scol32[:, dc:dc + 1],
                                   wmod32_stg[:, dc * 512:(dc + 1) * 512],
                                   start=(dc == 0), stop=(dc == DC - 1))
            else:
                for dc in range(DC):
                    ins = e.matmul(psA[0:1, (j % 4) * 512:(j % 4) * 512 + 512],
                                   scol[:, dc:dc + 1],
                                   wmod_stg[j % 2][:, dc * 512:(dc + 1) * 512],
                                   start=(dc == 0), stop=(dc == DC - 1))
            ins.then_inc(mr.sem, 1)
        pl.op("tensor", mm_mod, waits=waits)
        mr.bump()
        def ev_mod(e, j=j):
            e.tensor_copy(out=modrow[0:1, j * 512:(j + 1) * 512],
                          in_=psA[0:1, (j % 4) * 512:(j % 4) * 512 + 512])
            dinc(e, me.sem)
        pl.op("vector", ev_mod, waits=[(mr, j + 1)])
        me.bump()
    if skip_mod:
        pl.op("vector", lambda e: e.memset(modrow[:, :], 0.01))
    pl.emit(nc)

    pl = Plan()
    pp = C["pp"]
    def mod_post(e):
        e = DrainWrap(e)
        S = h2stg[0:1, 0:D]
        S2 = tmp_stg[0:1, 0:D]
        S3 = h2stg[0:1, D:2 * D]
        e.tensor_scalar(out=S2, in0=modrow[0:1, D:2 * D], scalar1=1.0, scalar2=None, op0=ALU.add)
        e.tensor_copy(out=modrow[0:1, D:2 * D], in_=S2)
        e.tensor_scalar(out=S2, in0=modrow[0:1, 4 * D:5 * D], scalar1=1.0, scalar2=None, op0=ALU.add)
        e.tensor_copy(out=modrow[0:1, 4 * D:5 * D], in_=S2)
        for off in (2 * D, 5 * D):
            gsl = modrow[0:1, off:off + D]
            e.tensor_scalar(out=S, in0=gsl, scalar1=0.0, scalar2=None, op0=ALU.is_ge)
            e.tensor_scalar(out=S2, in0=S, scalar1=2.0, scalar2=-1.0, op0=ALU.mult, op1=ALU.add)
            e.tensor_scalar(out=S, in0=gsl, scalar1=-1.0, scalar2=None, op0=ALU.mult)
            e.tensor_tensor(out=S3, in0=gsl, in1=S, op=ALU.max)
            e.tensor_scalar(out=S, in0=S3, scalar1=0.25, scalar2=None, op0=ALU.max)
            e.tensor_tensor(out=gsl, in0=S2, in1=S, op=ALU.mult)
        dinc(e, pp.sem)
    pl.op("vector", mod_post)
    pp.bump()
    order = [(0, sa_pp), (3 * D, sb_pp), (2 * D, ga_pp), (5 * D, gb_pp), (D, capp_pp), (4 * D, cbpp_pp)]
    w0 = pp.n
    for i, (off, _) in enumerate(order):
        dma(pl, "sync", rowbuf[0:1, i * D:(i + 1) * D], modrow[0:1, off:off + D], pp)
        if i == 0:
            pl.ops["sync"][-1] = ((pl.ops["sync"][-1][0] + ((pp, w0),)), pl.ops["sync"][-1][1])
    wmid = pp.n
    pl.op("sync", lambda e: None, waits=[(pp, wmid)])
    for i, (_, dst) in enumerate(order):
        dma(pl, "sync", dst[:, :], rowbuf[0:1, i * D:(i + 1) * D].rearrange("o (c p) -> p (o c)", p=P), pp, ncd=True)
    wend = pp.n
    pl.op("sync", lambda e: None, waits=[(pp, wend)])
    pl.emit(nc)

    if "mod" in dbg_out:
        dump_sbuf("mod", [(dbg_out["mod"][:, :], modrow[:, :])])

    def ln_stats(pl, pref, xsrc):
        sv, ss, sf, smm, srw = C[pref + "sv"], C[pref + "ss"], C[pref + "sf"], C[pref + "mm"], C[pref + "rw"]
        base_sf = sf.n
        for dc in range(DC):
            waits = [(sf, base_sf + dc - 1)] if dc >= 2 else []
            pl.op("vector", lambda e, dc=dc: (e.tensor_copy(
                out=xbf_stg[dc % 2][:, :], in_=xsrc[:, dc * T:(dc + 1) * T]), dinc(e, sv.sem)), waits=waits)
            sv.bump()
            pl.op("scalar", lambda e, dc=dc: (e.square(
                out=xsq_stg[dc % 2][:, :], in_=xsrc[:, dc * T:(dc + 1) * T]), dinc(e, ss.sem)), waits=waits)
            ss.bump()
            def mm_st(e, dc=dc):
                for tc in range(NT4):
                    e.matmul(psA[0:1, tc * 512:tc * 512 + 512], ones_bf[:, 0:1],
                             xbf_stg[dc % 2][:, tc * 512:tc * 512 + 512], start=(dc == 0), stop=(dc == DC - 1))
                    ins = e.matmul(psB[0:1, tc * 512:tc * 512 + 512], ones_bf[:, 0:1],
                                   xsq_stg[dc % 2][:, tc * 512:tc * 512 + 512], start=(dc == 0), stop=(dc == DC - 1))
                ins.then_inc(sf.sem, 1)
            pl.op("tensor", mm_st, waits=[(sv, sv.n), (ss, ss.n)])
            sf.bump()
        def rows1(e):
            e = DrainWrap(e)
            T4 = tmp_stg[0:1, 0:T]
            S = h2stg[0:1, 0:T]
            e.tensor_copy(out=sxrow[0:1, :], in_=psA[0:1, 0:T])
            e.tensor_copy(out=sxxrow[0:1, :], in_=psB[0:1, 0:T])
            if "raw" in dbg_out and pref == "l1":
                e.tensor_copy(out=mhold2[0:1, 0:T], in_=psA[0:1, 0:T])
                e.tensor_copy(out=mhold2[0:1, T:2 * T], in_=psB[0:1, 0:T])
            
            e.tensor_scalar(out=T4, in0=sxrow[0:1, :], scalar1=-1.0 / D, scalar2=None, op0=ALU.mult)
            e.tensor_scalar(out=S, in0=sxxrow[0:1, :], scalar1=1.0 / D, scalar2=None, op0=ALU.mult)
            e.tensor_tensor(out=sxrow[0:1, :], in0=T4, in1=T4, op=ALU.mult)
            e.tensor_tensor(out=sxxrow[0:1, :], in0=S, in1=sxrow[0:1, :], op=ALU.subtract)
            e.tensor_scalar(out=S, in0=sxxrow[0:1, :], scalar1=EPS_LN, scalar2=None, op0=ALU.add)
            e.reciprocal(out=sxxrow[0:1, :], in_=S)
            dinc(e, srw.sem)
        pl.op("vector", rows1, waits=[(sf, base_sf + DC)])
        srw.bump()
        pl.op("scalar", lambda e: (e.activation(out=h2stg[0:1, 0:T], in_=sxxrow[0:1, :],
                                                func=AF.Sqrt), dinc(e, srw.sem)), waits=[(srw, srw.n)])
        srw.bump()
        def rows2(e):
            e = DrainWrap(e)
            T4 = tmp_stg[0:1, 0:T]
            S = h2stg[0:1, 0:T]
            e.tensor_tensor(out=sxrow[0:1, :], in0=T4, in1=S, op=ALU.mult)
            e.tensor_copy(out=sxxrow[0:1, :], in_=S)
            dinc(e, srw.sem)
        pl.op("vector", rows2, waits=[(srw, srw.n)])
        srw.bump()

    def gh_bcast(pl, srw_like):
        def ghm(e):
            for tc in range(NT4):
                e.matmul(psA[:, tc * 512:tc * 512 + 512], onesrow_f[0:1, :],
                         sxxrow[0:1, tc * 512:tc * 512 + 512], start=True, stop=True)
                ins = e.matmul(psB[:, tc * 512:tc * 512 + 512], onesrow_f[0:1, :],
                               sxrow[0:1, tc * 512:tc * 512 + 512], start=True, stop=True)
            ins.then_inc(srw_like.sem, 1)
        pl.op("tensor", ghm, waits=[(srw_like, srw_like.n)])
        srw_like.bump()

    pl = Plan()
    ln_stats(pl, "l1", x_sb)
    gh_bcast(pl, C["l1rw"])
    ghv = C["l1rw"].n
    pl.emit(nc)
    if "rows" in dbg_out:
        dump_sbuf("rows", [(dbg_out["rows"][0:1, :], sxxrow[0:1, :]), (dbg_out["rows"][1:2, :], sxrow[0:1, :])])
    if "raw" in dbg_out:
        dump_sbuf("raw", [(dbg_out["raw"][0:1, :], mhold2[0:1, 0:T]), (dbg_out["raw"][1:2, :], mhold2[0:1, T:2 * T])])
    pl = Plan()
    def apply1(e):
        for dc in range(DC):
            e.tensor_tensor(out=tmp_stg[:, :], in0=x_sb[:, dc * T:(dc + 1) * T], in1=psA[:, 0:T], op=ALU.mult)
            e.tensor_tensor(out=h2stg[:, :], in0=tmp_stg[:, :], in1=psB[:, 0:T], op=ALU.add)
            e.tensor_scalar(out=h1_sb[:, dc * T:(dc + 1) * T], in0=h2stg[:, :],
                            scalar1=capp_pp[:, dc:dc + 1], scalar2=sa_pp[:, dc:dc + 1],
                            op0=ALU.mult, op1=ALU.add)
    pl.op("vector", apply1, waits=[(C["l1rw"], ghv)])
    pl.emit(nc)

    dump_sbuf("h1", [(dbg_out.get("h1") and dbg_out["h1"][dc * P:(dc + 1) * P, :], h1_sb[:, dc * T:(dc + 1) * T])
                     for dc in range(DC)] if "h1" in dbg_out else [])
    if stage < 2:
        return nc, dbg_out

    pl = Plan()
    qw, qwf = C["qw"], C["qwf"]
    vone = C["vone"]
    pl.op("gpsimd", lambda e: (e.memset(
        v_sb[:].rearrange("p (t h c) -> p t h c", h=4, c=65)[:, :, :, 64:65], 1.0), dinc(e, vone.sem)))
    vone.bump()
    pjr, pjf = C["pjr"], C["pjf"]
    nbs, nbm, nbv, nbf = C["nbs"], C["nbm"], C["nbv"], C["nbf"]
    ats, atx, ata, atf = C["ats"], C["atx"], C["ata"], C["atf"]
    atr, atb, atrb, atu = C["atr"], C["atb"], C["atrb"], C["atu"]

    wo_ok = C["wo_ok"]
    NPJ = 32
    pj_done = {}
    qw_done = {}

    def emit_wloads(qt):
        c0 = qt * 256
        first = True
        for wi, wd in enumerate((wq_d, wk_d, wv_d)):
            for dc in range(DC):
                qw.bump(16)
                w = [(pjf, pj_done[qt - 1])] if (first and qt >= 1) else []
                pl.op("gpsimd", lambda e, wd=wd, wi=wi, dc=dc, c0=c0: e.dma_start(
                    out=wstg[:, wi * 2048 + dc * 256:wi * 2048 + (dc + 1) * 256],
                    in_=wd[dc * P:(dc + 1) * P, c0:c0 + 256]).then_inc(qw.sem, 16),
                    waits=w)
                first = False
        qw_done[qt] = qw.n

    def emit_proj_tile(qt, kind, a, tc):
        gi = pjr.n
        waits = [(qw, qw_done[qt]), (atrb, atrb.n)]
        if gi >= 2:
            waits.append((pjf, gi - 1))
        sl = psB[:, 1024 + (gi % 2) * 512:1024 + (gi % 2) * 512 + 512]
        def mm_pj(e, kind=kind, a=a, tc=tc, sl=sl):
            for dc in range(DC):
                if kind == "q":
                    ins = e.matmul(sl, wstg[:, dc * 256 + a * 128:dc * 256 + a * 128 + 128],
                             h1_sb[:, dc * T + tc * 512:dc * T + tc * 512 + 512],
                             start=(dc == 0), stop=(dc == DC - 1))
                elif kind == "k":
                    ins = e.matmul(sl, wstg[:, 2048 + dc * 256 + a * 128:2048 + dc * 256 + a * 128 + 128],
                             h1_sb[:, dc * T + tc * 512:dc * T + tc * 512 + 512],
                             start=(dc == 0), stop=(dc == DC - 1))
                else:
                    ins = e.matmul(sl[0:P, 0:256], h1_sb[:, dc * T + a * 128:dc * T + a * 128 + 128],
                             wstg[:, 4096 + dc * 256:4096 + dc * 256 + 256],
                             start=(dc == 0), stop=(dc == DC - 1))
            ins.then_inc(pjr.sem, 1)
        pl.op("tensor", mm_pj, waits=waits)
        pjr.bump()
        def ev_pj(e, kind=kind, a=a, tc=tc, sl=sl):
            if kind == "q":
                e.tensor_copy(out=q_sb[:, a * T + tc * 512:a * T + tc * 512 + 512], in_=sl)
            elif kind == "k":
                e.tensor_copy(out=k_sb[:, a * T + tc * 512:a * T + tc * 512 + 512], in_=sl)
            else:
                e.tensor_copy(out=v_sb[:, a * 260:a * 260 + 260].rearrange("p (h c) -> p h c", c=65)[:, :, 0:64],
                              in_=sl[0:P, 0:256].rearrange("p (h c) -> p h c", c=64))
            dinc(e, pjf.sem)
        pl.op("vector", ev_pj, waits=[(pjr, pjr.n)])
        pjf.bump()

    def proj_batch(qt, qc):
        for oc in range(2):
            emit_proj_tile(qt, "q", oc, qc)
        for oc in range(2):
            emit_proj_tile(qt, "k", oc, qc)
        for j in range(4):
            emit_proj_tile(qt, "v", 4 * qc + j, 0)

    emit_wloads(0)
    def wo_load(e):
        for fc in range(DC):
            e.dma_start(out=wo_stg[:, fc * D:(fc + 1) * D],
                        in_=wo_d[fc * P:(fc + 1) * P, :]).then_inc(wo_ok.sem, 16)
    pl.op("gpsimd", wo_load)
    wo_ok.bump(16 * DC)

    for qt in range(4):
        if qt == 0:
            for qc in range(NT4 - 1, -1, -1):
                proj_batch(0, qc)
            pj_done[0] = pjf.n
            emit_wloads(1)
        elif qt < 3:
            emit_wloads(qt + 1)

        for half in range(4):
            src = q_sb if half < 2 else k_sb
            oc = half % 2
            sqw = [(pjf, pj_done[qt])] + ([(nbm, nbm.n)] if half > 0 else [])
            if half == 0 and qt >= 1:
                sqw.append((atf, 80 * qt))
            pl.op("scalar", lambda e, src=src, oc=oc: (e.square(
                out=qsq_stg[:, :], in_=src[:, oc * T:(oc + 1) * T]), dinc(e, nbs.sem)), waits=sqw)
            nbs.bump()
            for hh in range(2):
                def mm_nb2(e, hh=hh):
                    for tc in range(NT4):
                        ins = e.matmul(psA[0:1, tc * 512:tc * 512 + 512],
                                 ones_bf[:, 1 + hh:2 + hh],
                                 qsq_stg[:, tc * 512:tc * 512 + 512],
                                 start=True, stop=True)
                    ins.then_inc(nbm.sem, 1)
                w = [(nbs, nbs.n), (nbv, nbv.n)] if hh == 0 else [(nbv, nbv.n)]
                if half == 0 and hh == 0 and qt >= 1:
                    w.append((atx, 80 * qt))
                pl.op("tensor", mm_nb2, waits=w)
                nbm.bump()
                def rd_nb(e, half=half, hh=hh):
                    col = (half * 2 + hh)
                    for tc in range(NT4):
                        e.reduce_max(out=mqrow[0:1, col * 8 + tc:col * 8 + tc + 1],
                                     in_=psA[0:1, tc * 512:tc * 512 + 512], axis=AX.X)
                    dinc(e, nbv.sem)
                pl.op("vector", rd_nb, waits=[(nbm, nbm.n)])
                nbv.bump()
        def nb_comb(e, qt=qt):
            e = DrainWrap(e)
            for hl in range(4):
                e.reduce_max(out=mhold[0:1, hl * 8:hl * 8 + 1], in_=mqrow[0:1, hl * 8:hl * 8 + 4], axis=AX.X)
                e.reduce_max(out=mhold[0:1, hl * 8 + 1:hl * 8 + 2], in_=mqrow[0:1, 32 + hl * 8:32 + hl * 8 + 4], axis=AX.X)
            e.tensor_tensor(out=mhold[0:1, :].rearrange("o (h c) -> o h c", c=8)[:, 0:4, 2:3],
                            in0=mhold[0:1, :].rearrange("o (h c) -> o h c", c=8)[:, 0:4, 0:1],
                            in1=mhold[0:1, :].rearrange("o (h c) -> o h c", c=8)[:, 0:4, 1:2],
                            op=ALU.mult)
            dinc(e, nbv.sem)
        pl.op("vector", nb_comb)
        nbv.bump()
        pl.op("scalar", lambda e: (e.activation(
            out=mhold[0:1, :].rearrange("o (h c) -> o h c", c=8)[:, 0:4, 4:5],
            in_=mhold[0:1, :].rearrange("o (h c) -> o h c", c=8)[:, 0:4, 2:3],
            func=AF.Sqrt), dinc(e, nbs.sem)), waits=[(nbv, nbv.n)])
        nbs.bump()
        def nb_scale(e, qt=qt):
            e = DrainWrap(e)
            if qt == 0 and "nbdbg" in dbg_out:
                e.tensor_copy(out=mhold2[0:1, 0:64], in_=mqrow[0:1, :])
                e.tensor_copy(out=mhold2[0:1, 64:128], in_=mhold[0:1, :])
            e.tensor_scalar(out=negcrow[0:1, qt * 4:qt * 4 + 4],
                            in0=mhold[0:1, :].rearrange("o (h c) -> o h c", c=8)[:, 0:4, 4:5],
                            scalar1=-0.125, scalar2=None, op0=ALU.mult)
            dinc(e, nbv.sem)
        pl.op("vector", nb_scale, waits=[(nbs, nbs.n)])
        nbv.bump()
        def nb_bc(e, qt=qt):
            e.matmul(psA[:, 0:512][0:P, 0:4], onesrow_f[0:1, :], negcrow[0:1, qt * 4:qt * 4 + 4],
                     start=True, stop=True).then_inc(nbm.sem, 1)
        pl.op("tensor", nb_bc, waits=[(nbv, nbv.n)])
        nbm.bump()
        def nb_ev(e, qt=qt):
            e.tensor_copy(out=negc_pp[:, qt * 4:qt * 4 + 4],
                          in_=psA[:, 0:512][0:P, 0:4])
            dinc(e, nbf.sem)
        pl.op("vector", nb_ev, waits=[(nbm, nbm.n)])
        nbf.bump()

        for qc in range(NT4 - 1, -1, -1):
            for pr in range(2):
                hg = 4 * qt + 2 * pr
                nkt = 4 * qc + 4
                ngrp = nkt // 2
                uev_before = atu.n

                def emit_S(g, hh, pr=pr, qc=qc):
                    gidx = ats.n
                    waits = []
                    if gidx in (qt * 80, qt * 80 + 1):
                        waits.append((nbf, qt + 1))
                    if gidx >= 2:
                        waits.append((atx, gidx - 1))
                    def mm_s(e, pr=pr, qc=qc, g=g, hh=hh):
                        for half in range(2):
                            kt = 2 * g + half
                            ins = e.matmul(psA[:, hh * 1024 + half * 512:hh * 1024 + half * 512 + 512],
                                     k_sb[hh * 64:hh * 64 + 64, pr * T + kt * 128:pr * T + kt * 128 + 128],
                                     q_sb[hh * 64:hh * 64 + 64, pr * T + qc * 512:pr * T + qc * 512 + 512],
                                     start=True, stop=True)
                        ins.then_inc(ats.sem, 1)
                    pl.op("tensor", mm_s, waits=waits)
                    ats.bump()

                emit_S(0, 0)
                emit_S(0, 1)
                for g in range(ngrp):
                    diag = (2 * g + 1) >= 4 * qc
                    mi0 = 2 * g - 4 * qc
                    for hh in range(2):
                        gidx = atx.n
                        asl = a_tiles[2 * hh + g % 2]
                        ewaits = [(ats, gidx + 1)]
                        if gidx >= 4:
                            ewaits.append((atf, gidx - 3))
                        def ex_s(e, hh=hh, asl=asl, hg=hg):
                            e.activation(out=asl[:, :], in_=psA[:, hh * 1024:hh * 1024 + 1024],
                                         func=AF.Exp, bias=negc_pp[:, hg + hh:hg + hh + 1],
                                         scale=0.125).then_inc(atx.sem, 1)
                        pl.op("scalar", ex_s, waits=ewaits)
                        atx.bump()
                        if diag:
                            def mk_s(e, asl=asl, mi0=mi0):
                                e.tensor_tensor(out=asl[:, :], in0=asl[:, :],
                                                in1=masks_sb[:, mi0 * 512:mi0 * 512 + 1024],
                                                op=ALU.mult).then_inc(ata.sem, 1)
                            pl.op("vector", mk_s, waits=[(atx, atx.n)])
                            ata.bump(1)
                            a_ready = (ata, ata.n)
                        else:
                            a_ready = (atx, atx.n)
                        avw = [a_ready]
                        if g == 0 and hh == 0:
                            avw.append((atu, uev_before))
                            avw.append((pjf, pj_done[qt]))
                            if atf.n == 0:
                                avw.append((vone, 1))
                        def mm_av(e, pr=pr, g=g, hh=hh, asl=asl,
                                  first=(g == 0), last=(g == ngrp - 1)):
                            h1l = 2 * pr
                            for half in range(2):
                                kt = 2 * g + half
                                ins = e.matmul(psB[0:65, hh * 512:hh * 512 + 512],
                                         v_sb[:, kt * 260 + (h1l + hh) * 65:kt * 260 + (h1l + hh) * 65 + 65],
                                         asl[:, half * 512:half * 512 + 512],
                                         start=(first and half == 0), stop=(last and half == 1))
                            ins.then_inc(atf.sem, 1)
                        pl.op("tensor", mm_av, waits=avw)
                        atf.bump()
                        if g + 1 < ngrp:
                            emit_S(g + 1, hh)
                glast = atf.n
                def sums_ev(e):
                    e.reciprocal(out=rrow2[0:1, 0:1024], in_=psB[64:65, 0:1024])
                    dinc(e, atr.sem)
                pl.op("vector", sums_ev, waits=[(atf, glast)])
                atr.bump()
                def rb_bc(e):
                    e.matmul(psB[0:64, 1024:1536], onesrow_f[0:1, 0:64], rrow2[0:1, 512:1024],
                             start=True, stop=True)
                    e.matmul(psB[0:64, 1536:2048], onesrow_f[0:1, 0:64], rrow2[0:1, 0:512],
                             start=True, stop=True).then_inc(atb.sem, 1)
                pl.op("tensor", rb_bc, waits=[(atr, atr.n), (pjf, pjf.n)])
                atb.bump()
                def u_ev(e, hg=hg, qc=qc):
                    e.tensor_copy(out=rb2_sb[0:64, :], in_=psB[0:64, 1024:2048])
                    dinc(e, atrb.sem)
                    e.tensor_tensor(out=u_sb[0:64, (hg // 2) * T + qc * 512:(hg // 2) * T + qc * 512 + 512],
                                    in0=psB[0:64, 0:512], in1=rb2_sb[0:64, 512:1024],
                                    op=ALU.mult)
                    e.tensor_tensor(out=u_sb[64:128, (hg // 2) * T + qc * 512:(hg // 2) * T + qc * 512 + 512],
                                    in0=psB[0:64, 512:1024], in1=rb2_sb[0:64, 0:512],
                                    op=ALU.mult)
                    dinc(e, atu.sem)
                pl.op("vector", u_ev, waits=[(atb, atb.n)])
                atrb.bump(); atu.bump()
            if qt + 1 < 4:
                proj_batch(qt + 1, qc)
        if qt + 1 < 4:
            pj_done[qt + 1] = pjf.n
    pl.emit(nc)


    if "nbdbg" in dbg_out:
        dump_sbuf("nbdbg", [(dbg_out["nbdbg"][0:1, 0:NH], negc_pp[0:1, :]), (dbg_out["nbdbg"][1:2, 0:NH], negc_pp[127:128, :])])
    dump_sbuf("u", [(dbg_out.get("u") and dbg_out["u"][fc * P:(fc + 1) * P, :], u_sb[:, fc * T:(fc + 1) * T])
                    for fc in range(DC)] if "u" in dbg_out else [])
    if "negc" in dbg_out:
        dump_sbuf("negc", [(dbg_out["negc"][:, :], negcrow[0:1, :])])

    pl = Plan()
    opr, opv, opf = C["opr"], C["opv"], C["opf"]
    w1mx, w1rw = C["w1mx"], C["w1rw"]
    pl.op("tensor", lambda e: None, waits=[(wo_ok, wo_ok.n)])
    for dc in range(DC):
        for tc in range(NT4):
            gi = opr.n
            waits = [(opf, gi - 1)] if gi >= 2 else []
            sl = psA[:, (gi % 2) * 512:(gi % 2) * 512 + 512]
            def mm_op(e, dc=dc, tc=tc, sl=sl):
                for fc in range(DC):
                    ins = e.matmul(sl, wo_stg[:, fc * D + dc * 128:fc * D + dc * 128 + 128],
                                   u_sb[:, fc * T + tc * 512:fc * T + tc * 512 + 512],
                                   start=(fc == 0), stop=(fc == DC - 1))
                ins.then_inc(opr.sem, 1)
            pl.op("tensor", mm_op, waits=waits)
            opr.bump()
            def ev_op(e, dc=dc, tc=tc, sl=sl, gi=gi):
                e.scalar_tensor_tensor(out=op2stg[gi % 2][:, :],
                                       in0=sl, scalar=ga_pp[:, dc:dc + 1],
                                       in1=x_sb[:, dc * T + tc * 512:dc * T + tc * 512 + 512],
                                       op0=ALU.mult, op1=ALU.add)
                dinc(e, opv.sem)
            pl.op("vector", ev_op, waits=[(opr, opr.n)] + ([(opf, gi - 1)] if gi >= 2 else []))
            opv.bump()
            def wb_op(e, dc=dc, tc=tc, gi=gi):
                e.copy(out=x_sb[:, dc * T + tc * 512:dc * T + tc * 512 + 512],
                       in_=op2stg[gi % 2][:, :])
                dinc(e, opf.sem)
            pl.op("scalar", wb_op, waits=[(opv, opv.n)])
            opf.bump()
            fcm = gi
            bw1 = [(w1mx, fcm - 1)] if fcm >= 2 else []
            lw = dma2(pl, "sync", w1nstg[fcm % 2][:, :], w1n_d[fcm * P:(fcm + 1) * P, :],
                      w1mp, fcm, waits=bw1)
            def mx_w1(e, fcm=fcm):
                e.reduce_max(out=sw1[:, fcm:fcm + 1], in_=w1nstg[fcm % 2][:, :],
                             axis=AX.X, apply_absolute_value=True)
                dinc(e, w1mx.sem)
            pl.op("vector", mx_w1, waits=[lw])
            w1mx.bump()
    def sw1f(e):
        e = DrainWrap(e)
        e.tensor_scalar(out=sw1tmp[:, :], in0=sw1[:, :], scalar1=EPS_Q, scalar2=1.0 / 127.0,
                        op0=ALU.max, op1=ALU.mult)
        e.reciprocal(out=rw1[:, :], in_=sw1tmp[:, :])
        dinc(e, w1mx.sem)
    pl.op("vector", sw1f)
    w1mx.bump()
    dma(pl, "sync", rowbuf[0:1, 8 * D:8 * D + FF].rearrange("o (a b) -> o a b", b=P).transpose([0, 2, 1]),
        rw1[:, :], w1rw, ncd=True)
    pl.ops["sync"][-1] = ((pl.ops["sync"][-1][0] + ((w1mx, w1mx.n),)), pl.ops["sync"][-1][1])
    pl.emit(nc)

    dump_sbuf("x2", [(dbg_out.get("x2") and dbg_out["x2"][dc * P:(dc + 1) * P, :], x_sb[:, dc * T:(dc + 1) * T])
                     for dc in range(DC)] if "x2" in dbg_out else [])
    if stage < 3:
        return nc, dbg_out

    pl = Plan()
    ln_stats(pl, "l2", x_sb)
    gh_bcast(pl, C["l2rw"])
    ghv2 = C["l2rw"].n
    mxv, cc1, sc1c, gh2, rqv, xo = C["mxv"], C["cc1"], C["sc1c"], C["gh2"], C["rqv"], C["xo"]
    def maxpass(e):
        for dc in range(DC):
            e.tensor_tensor(out=tmp_stg[:, :], in0=x_sb[:, dc * T:(dc + 1) * T], in1=psA[:, 0:T], op=ALU.mult)
            e.tensor_tensor(out=h2stg[:, :], in0=tmp_stg[:, :], in1=psB[:, 0:T], op=ALU.add)
            e.tensor_scalar(out=tmp_stg[:, :], in0=h2stg[:, :],
                            scalar1=cbpp_pp[:, dc:dc + 1], scalar2=sb_pp[:, dc:dc + 1],
                            op0=ALU.mult, op1=ALU.add)
            e.reduce_max(out=mx8[:, dc:dc + 1], in_=tmp_stg[:, :], axis=AX.X)
        dinc(e, mxv.sem)
    pl.op("vector", maxpass, waits=[(C["l2rw"], ghv2)])
    mxv.bump()
    dma(pl, "sync", rowbuf[0:1, 4 * D:4 * D + 1024].rearrange("o (p c) -> (o p) c", c=8), mx8[:, :], xo)
    pl.ops["sync"][-1] = ((pl.ops["sync"][-1][0] + ((mxv, mxv.n),)), pl.ops["sync"][-1][1])
    dma(pl, "sync", h2stg[0:1, 0:1024], rowbuf[0:1, 4 * D:4 * D + 1024], xo)
    pl.ops["sync"][-1] = ((pl.ops["sync"][-1][0] + ((xo, xo.n - 16),)), pl.ops["sync"][-1][1])
    xov = xo.n
    def gmax_local(e):
        e = DrainWrap(e)
        e.memset(gmaxs[0:1, :], 0.0)
        e.reduce_max(out=gmaxs[0:1, 0:1], in_=h2stg[0:1, 0:1024], axis=AX.X)
        dinc(e, mxv.sem)
    pl.op("vector", gmax_local, waits=[(xo, xov)])
    mxv.bump()
    cc1c = C["cc1c"]
    def coll1(e):
        e.dma_start(out=cc_in1.ap(), in_=gmaxs[0:1, :]).then_inc(cc1.sem, 16)
        e.wait_ge(cc1.sem, 16)
        e.collective_compute("AllReduce", ALU.max, replica_groups=[list(range(8))],
                             ins=[cc_in1.ap().opt()], outs=[cc_out1.ap().opt()]).then_inc(cc1c.sem, 1)
        e.wait_ge(cc1c.sem, 1)
        e.dma_start(out=gmaxs[0:1, :], in_=cc_out1.ap()).then_inc(cc1.sem, 16)
    pl.op("gpsimd", coll1, waits=[(mxv, mxv.n)])
    cc1.bump(32); cc1c.bump(1)
    def s1row(e):
        e = DrainWrap(e)
        e.tensor_scalar(out=tiny[0:1, 0:8], in0=gmaxs[0:1, 0:8], scalar1=EPS_Q, scalar2=1.0 / 255.0,
                        op0=ALU.max, op1=ALU.mult)
        e.reciprocal(out=mqrow[0:1, 0:8], in_=tiny[0:1, 0:8])
        dinc(e, sc1c.sem)
    pl.op("vector", s1row, waits=[(cc1, cc1.n)])
    sc1c.bump()
    def s1bc(e):
        e.matmul(psA[:, 0:512][0:P, 0:1], onesrow_f[0:1, :], tiny[0:1, 0:1], start=True, stop=True)
        e.matmul(psA[:, 512:1024][0:P, 0:1], onesrow_f[0:1, :], mqrow[0:1, 0:1],
                 start=True, stop=True).then_inc(sc1c.sem, 1)
    pl.op("tensor", s1bc, waits=[(sc1c, sc1c.n)])
    sc1c.bump()
    def s1ev(e):
        e = DrainWrap(e)
        e.tensor_copy(out=s1pp[:, 0:1], in_=psA[:, 0:512][0:P, 0:1])
        e.tensor_copy(out=r1pp[:, 0:1], in_=psA[:, 512:1024][0:P, 0:1])
        e.tensor_scalar(out=cbr1_pp[:, :], in0=cbpp_pp[:, :], scalar1=r1pp[:, 0:1], scalar2=None, op0=ALU.mult)
        e.tensor_scalar(out=sbr1_pp[:, :], in0=sb_pp[:, :], scalar1=r1pp[:, 0:1], scalar2=None,
                        op0=ALU.mult)
        dinc(e, sc1c.sem)
    pl.op("vector", s1ev, waits=[(sc1c, sc1c.n)])
    sc1c.bump()
    pl.op("tensor", lambda e: None, waits=[(sc1c, sc1c.n)])
    gh_bcast(pl, gh2)
    gh2v = gh2.n
    for dc in range(DC):
        def rq(e, dc=dc):
            e.tensor_tensor(out=tmp_stg[:, :], in0=x_sb[:, dc * T:(dc + 1) * T], in1=psA[:, 0:T], op=ALU.mult)
            e.tensor_tensor(out=h2stg[:, :], in0=tmp_stg[:, :], in1=psB[:, 0:T], op=ALU.add)
            e.tensor_scalar(out=tmp_stg[:, :], in0=h2stg[:, :],
                            scalar1=cbr1_pp[:, dc:dc + 1], scalar2=sbr1_pp[:, dc:dc + 1],
                            op0=ALU.mult, op1=ALU.add)
            for tc in range(NT4):
                e.tensor_scalar(out=i32stg[:, :], in0=tmp_stg[:, tc * 512:tc * 512 + 512],
                                scalar1=0.0, scalar2=None, op0=ALU.max)
                e.tensor_copy(out=hq_sb[:, dc * T + tc * 512:dc * T + tc * 512 + 512], in_=i32stg[:, :])
            dinc(e, rqv.sem)
        w = [(sc1c, sc1c.n), (gh2, gh2v)] if dc == 0 else []
        pl.op("vector", rq, waits=w)
        rqv.bump()
        dma(pl, "sync", xbuf[dc, :, :], x_sb[:, dc * T:(dc + 1) * T], xo)
        pl.ops["sync"][-1] = ((pl.ops["sync"][-1][0] + ((rqv, rqv.n),)), pl.ops["sync"][-1][1])
    xo_done = xo.n
    pl.op("sync", lambda e: None, waits=[(xo, xo_done)])
    pl.emit(nc)

    if "sc1" in dbg_out:
        dump_sbuf("sc1", [(dbg_out["sc1"][:, :], tiny[0:1, :])])
    if "mxdbg" in dbg_out:
        dump_sbuf("mxdbg", [(dbg_out["mxdbg"][0:1, 0:DC], mx8[0:1, :]),
                            (dbg_out["mxdbg"][2:3, 0:P], h2stg[0:1, 0:P]),
                            (dbg_out["mxdbg"][3:4, 0:8], gmaxs[0:1, :])])
    dump_sbuf("hq", [(dbg_out.get("hq") and dbg_out["hq"][dc * P:(dc + 1) * P, :], hq_sb[:, dc * T:(dc + 1) * T])
                     for dc in range(DC)] if "hq" in dbg_out else [])
    if stage < 4:
        return nc, dbg_out

    pl = Plan()
    f1r, f1s, f1v, f1d = C["f1r"], C["f1s"], C["f1v"], C["f1d"]
    w1ld, w1rw, w1bc, w1qv, w1tt, w1f = (C[k] for k in ("w1ld", "w1rw", "w1bc", "w1qv", "w1tt", "w1f"))
    w2mx, w2rw, w2bc, w2qv, w2ld = (C[k] for k in ("w2mx", "w2rw", "w2bc", "w2qv", "w2ld"))
    cc2, sc2c, b2p = C["cc2"], C["sc2c"], C["b2p"]

    v_b1 = w1ld.bump(16)
    pl.op("sync", lambda e: e.dma_start(out=b1sb[:, :], in_=b1_d[:, :]).then_inc(w1ld.sem, 16))
    def b1prep(e):
        e = DrainWrap(e)
        e.tensor_scalar(out=p1c[:, :], in0=sw1tmp[:, :], scalar1=s1pp[:, 0:1], scalar2=None, op0=ALU.mult)
        e.reciprocal(out=pb1c[:, :], in_=p1c[:, :])
        e.tensor_tensor(out=i32r[:, :], in0=b1sb[:, :], in1=pb1c[:, :], op=ALU.mult)
        e.tensor_copy(out=bq1f[:, :], in_=i32r[:, :])
        e.tensor_tensor(out=pb1c[:, :], in0=p1c[:, :], in1=bq1f[:, :], op=ALU.mult)
        dinc(e, w1f.sem)
    pl.op("vector", b1prep, waits=[(w1ld, v_b1)])
    w1f.bump()

    def emit_w1_bcast(cc):
        rlw = dma2(pl, "sync", rwstg[cc % 2][0:1, :],
                   rowbuf[0:1, 8 * D + cc * 512:8 * D + cc * 512 + 512], w1rp, cc,
                   waits=[(w1rw, w1rw.n)] if cc == 0 else ([(w1bc, cc - 1)] if cc >= 2 else []))
        def bc_w1(e, cc=cc):
            e.matmul(psA[:, (cc % 2) * 512:(cc % 2) * 512 + 512], onesrow_f[0:1, :],
                     rwstg[cc % 2][0:1, :], start=True, stop=True).then_inc(w1bc.sem, 1)
        bw = [rlw]
        if cc >= 2:
            bw.append((w1qv, (cc - 1) * DC))
        pl.op("tensor", bc_w1, waits=bw)
        w1bc.bump()

    def emit_w1_quant(cc, dc):
        gidx = cc * DC + dc
        qlw = dma2(pl, "sync", w1tstg[dc % 2][:, :],
                   w1t_d[dc * P:(dc + 1) * P, cc * 512:cc * 512 + 512], w1qp, gidx,
                   waits=[(w1tt, gidx - 1)] if gidx >= 2 else [])
        def q_w1(e, cc=cc, dc=dc, gidx=gidx):
            e.tensor_tensor(out=i32w1[gidx % 2][:, :], in0=w1tstg[dc % 2][:, :],
                            in1=psA[:, (cc % 2) * 512:(cc % 2) * 512 + 512],
                            op=ALU.mult).then_inc(w1tt.sem, 1)
        vw = [qlw, (w1bc, cc + 1)]
        if gidx >= 2:
            vw.append((w1qv, gidx - 1))
        pl.op("vector", q_w1, waits=vw)
        w1tt.bump()
        def cp_w1(e, cc=cc, dc=dc, gidx=gidx):
            e.copy(out=wq1t_sb[:, dc * FF + cc * 512:dc * FF + cc * 512 + 512],
                   in_=i32w1[gidx % 2][:, :])
            dinc(e, w1qv.sem)
        pl.op("scalar", cp_w1, waits=[(w1tt, w1tt.n)])
        w1qv.bump()

    def emit_w2_rowmax(idx):
        dc2, piece = idx // 4, idx % 4
        lw2 = dma2(pl, "sync", w2nstg[idx % 2][:, :],
                   w2n_d[dc2 * P:(dc2 + 1) * P, piece * D:piece * D + D], w2mp, idx,
                   waits=[(w2mx, idx - 1)] if idx >= 2 else [])
        def mx_w2(e, dc2=dc2, piece=piece, idx=idx):
            e = DrainWrap(e)
            e.reduce_max(out=mx8[:, piece:piece + 1], in_=w2nstg[idx % 2][:, :],
                         axis=AX.X, apply_absolute_value=True)
            if piece == 3:
                e.reduce_max(out=sw2[:, dc2:dc2 + 1], in_=mx8[:, 0:4], axis=AX.X)
            dinc(e, w2mx.sem)
        pl.op("vector", mx_w2, waits=[lw2])
        w2mx.bump()

    def emit_w2_scales():
        def sw2f(e):
            e = DrainWrap(e)
            e.tensor_scalar(out=sw2tmp[:, :], in0=sw2[:, :], scalar1=EPS_Q, scalar2=1.0 / 127.0,
                            op0=ALU.max, op1=ALU.mult)
            e.reciprocal(out=rw2[:, :], in_=sw2tmp[:, :])
            dinc(e, w2mx.sem)
        pl.op("vector", sw2f)
        w2mx.bump()
        dma(pl, "sync", rowbuf[0:1, 0:D].rearrange("o (a b) -> o a b", b=P).transpose([0, 2, 1]),
            rw2[:, :], w2rw, ncd=True)
        pl.ops["sync"][-1] = ((pl.ops["sync"][-1][0] + ((w2mx, w2mx.n),)), pl.ops["sync"][-1][1])

    def emit_w2_rwload(cc):
        return dma2(pl, "sync", w2rwstg[cc][0:1, :], rowbuf[0:1, cc * 512:cc * 512 + 512],
                    w2rp, cc, waits=[(w2rw, w2rw.n)] if cc == 0 else [])

    def emit_w2_bcast(cc, rlw2):
        def bc_w2(e, cc=cc):
            e.matmul(psA[:, 1024 + (cc % 2) * 512:1024 + (cc % 2) * 512 + 512], onesrow_f[0:1, :],
                     w2rwstg[cc][0:1, :], start=True, stop=True).then_inc(w2bc.sem, 1)
        pl.op("tensor", bc_w2, waits=[rlw2])
        w2bc.bump()

    def emit_w2_quant(cc, fc2):
        gidx2 = cc * NFC + fc2
        qlw2 = dma2(pl, "sync", w2tstg[fc2 % 2][:, :],
                    w2t_d[fc2 * P:(fc2 + 1) * P, cc * 512:cc * 512 + 512], w2qp, gidx2,
                    waits=[(w2qv, gidx2 - 1)] if gidx2 >= 2 else [])
        def q_w2(e, cc=cc, fc2=fc2):
            e.tensor_tensor(out=i32stg[:, :], in0=w2tstg[fc2 % 2][:, :],
                            in1=psA[:, 1024 + (cc % 2) * 512:1024 + (cc % 2) * 512 + 512],
                            op=ALU.mult)
            e.tensor_copy(out=wq2t_sb[:, fc2 * D + cc * 512:fc2 * D + cc * 512 + 512],
                          in_=i32stg[:, :])
            dinc(e, w2qv.sem)
        pl.op("vector", q_w2, waits=[qlw2, (w2bc, cc + 1)])
        w2qv.bump()

    emit_w1_bcast(0)
    for dc in range(DC):
        emit_w1_quant(0, dc)
    pend_w1 = []
    pend_w2q = [(cc, f) for cc in range(2) for f in range(NFC)]
    w2mx_next = 0
    w2rl = [None, None]
    ti = 0
    NT_F1 = NT4 * NFC
    for fc in range(NFC):
        cc_next = fc // 4 + 1
        if fc % 4 == 0 and cc_next < DC:
            emit_w1_bcast(cc_next)
            pend_w1 = [(cc_next, dc) for dc in range(DC)]
        for tc in range(NT4):
            gi = ti; ti += 1
            waits = [(f1s, gi - 3)] if gi >= 4 else []
            if tc == 0:
                waits.append((w1qv, DC * (fc // 4 + 1)))
            sl = psB[:, (gi % 4) * 512:(gi % 4) * 512 + 512]
            def mm_f1(e, tc=tc, fc=fc, sl=sl):
                for dc in range(DC):
                    ins = e.matmul(sl, wq1t_sb[:, dc * FF + fc * 128:dc * FF + fc * 128 + 128],
                                   hq_sb[:, dc * T + tc * 512:dc * T + tc * 512 + 512],
                                   start=(dc == 0), stop=(dc == DC - 1))
                ins.then_inc(f1r.sem, 1)
            pl.op("tensor", mm_f1, waits=waits)
            f1r.bump()
            sw = [(f1r, f1r.n)]
            if gi == 0:
                sw.append((w1f, 1))
            if gi >= 2:
                sw.append((f1d, gi - 1))
            def ev_f1(e, gi=gi, fc=fc, sl=sl):
                e.activation(out=h3stg[gi % 2][:, :], in_=sl, func=AF.Relu,
                             scale=p1c[:, fc:fc + 1], bias=pb1c[:, fc:fc + 1])
                dinc(e, f1s.sem)
            pl.op("scalar", ev_f1, waits=sw)
            f1s.bump()
            def mxf1(e, gi=gi):
                e.reduce_max(out=mxf[:, gi:gi + 1], in_=h3stg[gi % 2][:, :], axis=AX.X)
                dinc(e, f1v.sem)
            pl.op("vector", mxf1, waits=[(f1s, f1s.n)])
            f1v.bump()
            stw = dma2(pl, "sync", h3buf[tc, fc, :, :], h3stg[gi % 2][:, :], f1hp, gi,
                       waits=[(f1s, f1s.n)])
            def fd_mark(e, gi=gi):
                e.nop().then_inc(f1d.sem, 1)
            pl.op("sync", fd_mark, waits=[stw, (f1v, f1v.n)])
            f1d.bump()
            if gi % 2 == 0 and pend_w1:
                emit_w1_quant(*pend_w1.pop(0))
            if gi % 2 == 1 and gi >= 4 and w2mx_next < 32:
                emit_w2_rowmax(w2mx_next); w2mx_next += 1
            if gi == 73:
                emit_w2_scales()
            if gi == 75:
                w2rl[0] = emit_w2_rwload(0)
            if gi == 77:
                emit_w2_bcast(0, w2rl[0])
            if gi == 81:
                w2rl[1] = emit_w2_rwload(1)
            if gi == 83:
                emit_w2_bcast(1, w2rl[1])
            if gi >= 80 and pend_w2q:
                emit_w2_quant(*pend_w2q.pop(0))

    def mxfin(e):
        e = DrainWrap(e)
        for c in range(8):
            e.reduce_max(out=mx8[:, c:c + 1], in_=mxf[:, c * 16:(c + 1) * 16], axis=AX.X)
        dinc(e, f1v.sem)
    pl.op("vector", mxfin, waits=[(f1v, NT_F1)])
    f1v.bump()
    v = C["xo"].bump(16)
    pl.op("sync", lambda e: e.dma_start(
        out=rowbuf[0:1, 4 * D:4 * D + 1024].rearrange("o (p c) -> (o p) c", c=8),
        in_=mx8[:, :]).then_inc(C["xo"].sem, 16),
        waits=[(f1v, NT_F1 + 1)])
    v2 = C["xo"].bump(16)
    pl.op("sync", lambda e: e.dma_start(out=h2stg[0:1, 0:1024],
                                        in_=rowbuf[0:1, 4 * D:4 * D + 1024]).then_inc(C["xo"].sem, 16),
          waits=[(C["xo"], v)])
    for _ in range(min(4, len(pend_w2q))):
        emit_w2_quant(*pend_w2q.pop(0))
    def gmax2_local(e):
        e = DrainWrap(e)
        e.memset(gmaxs[0:1, :], 0.0)
        e.reduce_max(out=gmaxs[0:1, 0:1], in_=h2stg[0:1, 0:1024], axis=AX.X)
        dinc(e, f1v.sem)
    pl.op("vector", gmax2_local, waits=[(C["xo"], v2)])
    f1v.bump(1)
    cc2c = C["cc2c"]
    def coll2(e):
        e.dma_start(out=cc_in2.ap(), in_=gmaxs[0:1, :]).then_inc(cc2.sem, 16)
        e.wait_ge(cc2.sem, 16)
        e.collective_compute("AllReduce", ALU.max, replica_groups=[list(range(8))],
                             ins=[cc_in2.ap().opt()], outs=[cc_out2.ap().opt()]).then_inc(cc2c.sem, 1)
        e.wait_ge(cc2c.sem, 1)
        e.dma_start(out=gmaxs[0:1, :], in_=cc_out2.ap()).then_inc(cc2.sem, 16)
    pl.op("gpsimd", coll2, waits=[(f1v, f1v.n)])
    cc2.bump(32); cc2c.bump(1)
    while pend_w2q:
        emit_w2_quant(*pend_w2q.pop(0))
    def s2row(e):
        e = DrainWrap(e)
        e.tensor_scalar(out=tiny[0:1, 0:8], in0=gmaxs[0:1, 0:8], scalar1=EPS_Q, scalar2=1.0 / 255.0,
                        op0=ALU.max, op1=ALU.mult)
        e.reciprocal(out=mqrow[0:1, 0:8], in_=tiny[0:1, 0:8])
        dinc(e, sc2c.sem)
    pl.op("vector", s2row, waits=[(cc2, cc2.n)])
    sc2c.bump()
    def s2bc(e):
        e.matmul(psB[:, 0:512][0:P, 0:1], onesrow_f[0:1, :], tiny[0:1, 0:1], start=True, stop=True)
        e.matmul(psB[:, 512:1024][0:P, 0:1], onesrow_f[0:1, :], mqrow[0:1, 0:1],
                 start=True, stop=True).then_inc(sc2c.sem, 1)
    pl.op("tensor", s2bc, waits=[(sc2c, sc2c.n), (f1s, NT_F1)])
    sc2c.bump()
    v = w2ld.bump(16)
    pl.op("sync", lambda e: e.dma_start(out=b2sb[:, :], in_=b2_d[:, :]).then_inc(w2ld.sem, 16))
    def b2prep(e):
        e = DrainWrap(e)
        e.tensor_copy(out=r2pp[:, 0:1], in_=psB[:, 512:1024][0:P, 0:1])
        e.tensor_copy(out=s2pp[:, 0:1], in_=psB[:, 0:512][0:P, 0:1])
        e.tensor_scalar(out=p2c[:, :], in0=sw2tmp[:, :], scalar1=s2pp[:, 0:1], scalar2=None, op0=ALU.mult)
        e.reciprocal(out=p2gbb[:, :], in_=p2c[:, :])
        e.tensor_tensor(out=i32r2[:, :], in0=b2sb[:, :], in1=p2gbb[:, :], op=ALU.mult)
        e.tensor_copy(out=bq2f[:, :], in_=i32r2[:, :])
        e.tensor_tensor(out=p2gb[:, :], in0=p2c[:, :], in1=gb_pp[:, :], op=ALU.mult)
        e.tensor_tensor(out=p2gbb[:, :], in0=p2gb[:, :], in1=bq2f[:, :], op=ALU.mult)
        dinc(e, b2p.sem)
    pl.op("vector", b2prep, waits=[(sc2c, sc2c.n), (w2ld, v)])
    b2p.bump()
    pl.emit(nc)

    if "h3dbg" in dbg_out:
        plh = Plan()
        c = C["dbgc"]
        dma(plh, "sync", dbg_out["h3dbg"][:, :, :, :], h3buf[:, :, :, :], c)
        tgt = c.n
        plh.op("sync", lambda e: None, waits=[(c, tgt)])
        plh.emit(nc)
    if "sc2" in dbg_out:
        dump_sbuf("sc2", [(dbg_out["sc2"][:, :], tiny[0:1, :])])
    if "wq2dbg" in dbg_out:
        dump_sbuf("wq2dbg", [(dbg_out["wq2dbg"][:, :], wq2t_sb[:, :])])
    if "b2dbg" in dbg_out:
        dump_sbuf("b2dbg", [(dbg_out["b2dbg"][0:1, 0:DC], p2gb[0:1, :]),
                            (dbg_out["b2dbg"][1:2, 0:DC], p2gbb[0:1, :]),
                            (dbg_out["b2dbg"][2:3, 0:DC], bq2f[0:1, :]),
                            (dbg_out["b2dbg"][3:4, 0:DC], sw2[0:1, :]),
                            (dbg_out["b2dbg"][4:5, 0:DC], sw2tmp[0:1, :]),
                            (dbg_out["b2dbg"][5:6, 0:DC], gb_pp[0:1, :])])
    if "mx2dbg" in dbg_out:
        dump_sbuf("mx2dbg", [(dbg_out["mx2dbg"][0:1, 0:P], mxf[0:1, :]),
                             (dbg_out["mx2dbg"][1:2, 0:8], mx8[0:1, :]),
                             (dbg_out["mx2dbg"][2:3, 0:P], h2stg[0:1, 0:P]),
                             (dbg_out["mx2dbg"][3:4, 0:8], gmaxs[0:1, :])])
    if stage < 5:
        return nc, dbg_out

    pl = Plan()
    rbq, f2r, f2s, f2v, f2x, yo = (C[k] for k in ("rbq", "f2r", "f2s", "f2v", "f2x", "yo"))
    for tc in range(NT4):
        for fc in range(NFC):
            qi = tc * NFC + fc
            w = [(rbq, qi - 5)] if qi >= 6 else []
            if tc >= 2 and fc == 0:
                w.append((f2r, (tc - 1) * DC))
            c6 = rbl6[qi % 6]
            v6 = c6.bump(16)
            pl.op("sync", lambda e, tc=tc, fc=fc, qi=qi, c6=c6: e.dma_start(
                out=rbstg[qi % 6][:, :], in_=h3buf[tc, fc, :, :]).then_inc(c6.sem, 16), waits=w)
            rlw3 = (c6, v6)
            def q_rb(e, tc=tc, fc=fc, qi=qi):
                e.tensor_scalar(out=i32b_stg[:, :], in0=rbstg[qi % 6][:, :],
                                scalar1=r2pp[:, 0:1], scalar2=None, op0=ALU.mult)
                e.tensor_copy(out=h3q_sb[tc % 2][:, fc * 512:fc * 512 + 512],
                              in_=i32b_stg[:, :])
                dinc(e, rbq.sem)
            qrw = [rlw3]
            if qi == 0:
                qrw.append((C["b2p"], 1))
            pl.op("vector", q_rb, waits=qrw)
            rbq.bump()
        for dc in range(DC):
            gi = tc * DC + dc
            waits = [(rbq, (tc + 1) * NFC)]
            if gi >= 2:
                waits.append((f2s, gi - 1))
            sl = psA[:, (gi % 2) * 512:(gi % 2) * 512 + 512]
            def mm_f2(e, tc=tc, dc=dc, sl=sl):
                for fc in range(NFC):
                    ins = e.matmul(sl, wq2t_sb[:, fc * D + dc * 128:fc * D + dc * 128 + 128],
                                   h3q_sb[tc % 2][:, fc * 512:fc * 512 + 512],
                                   start=(fc == 0), stop=(fc == NFC - 1))
                ins.then_inc(f2r.sem, 1)
            pl.op("tensor", mm_f2, waits=waits)
            f2r.bump()
            xlw = dma2(pl, "sync", xinstg[gi % 2][:, :], xbuf[dc, :, tc * 512:tc * 512 + 512],
                       f2xp, gi, waits=[(f2v, gi - 1)] if gi >= 2 else [])
            def ev_f2s(e, dc=dc, sl=sl, gi=gi):
                e.activation(out=o1stg[gi % 2][:, :], in_=sl, func=AF.Identity,
                             scale=p2gb[:, dc:dc + 1], bias=p2gbb[:, dc:dc + 1])
                dinc(e, f2s.sem)
            sw2_ = [(f2r, f2r.n)]
            if gi >= 2:
                sw2_.append((f2v, gi - 1))
            pl.op("scalar", ev_f2s, waits=sw2_)
            f2s.bump()
            def ev_f2v(e, gi=gi):
                e.tensor_tensor(out=o2stg[gi % 2][:, :], in0=o1stg[gi % 2][:, :], in1=xinstg[gi % 2][:, :],
                                op=ALU.add)
                dinc(e, f2v.sem)
            evw = [(f2s, f2s.n), xlw]
            if gi >= 2:
                evw.append((yop[gi % 2], 16 * (gi // 2)))
            pl.op("vector", ev_f2v, waits=evw)
            f2v.bump()
            dma2(pl, "sync", y_out[dc * P:(dc + 1) * P, tc * 512:tc * 512 + 512],
                 o2stg[gi % 2][:, :], yop, gi, waits=[(f2v, f2v.n)])
    pl.op("sync", lambda e: None, waits=[(yop[0], yop[0].n), (yop[1], yop[1].n)])
    pl.emit(nc)

    return nc, dbg_out


def make_host_masks():
    m = np.zeros((4, P, 512), dtype=BF)
    for kt in range(4):
        for r in range(P):
            m[kt, r, kt * P + r:] = 1.0
    return m


def prep_inputs(inputs):
    x = inputs["L0_attn_x"]
    w1 = np.ascontiguousarray(inputs["L0_ffn_w1"])
    w2 = np.ascontiguousarray(inputs["L0_ffn_w2"])
    shared = {
        "wq": np.ascontiguousarray(inputs["L0_attn_WQ"]),
        "wk": np.ascontiguousarray(inputs["L0_attn_WK"]),
        "wv": np.ascontiguousarray(inputs["L0_attn_WV"]),
        "wo": np.ascontiguousarray(inputs["L0_attn_WO"]),
        "wmod": np.ascontiguousarray(inputs["L0_wmod"]),
        "w1n": w1,
        "w1t": np.ascontiguousarray(w1.T),
        "w2n": w2,
        "w2t": np.ascontiguousarray(w2.T),
        "b1c": np.ascontiguousarray(inputs["L0_ffn_b1"].reshape(NFC, P).T),
        "b2c": np.ascontiguousarray(inputs["L0_ffn_b2"].reshape(DC, P).T),
        "masks": make_host_masks(),
    }
    cores = []
    for b in range(8):
        d = dict(shared)
        d["x_fm"] = np.ascontiguousarray(x[b].T)
        d["crow"] = np.ascontiguousarray(inputs["L0_c"][b:b + 1, :])
        cores.append(d)
    return cores


def kernel(**inputs):
    cores = prep_inputs(inputs)
    nc, _ = build_nc(stage=5)
    res = run_bass_kernel_spmd(nc, cores, core_ids=list(range(8)))
    out = np.stack([r["y_fm"].T for r in res.results], axis=0)
    return np.ascontiguousarray(out, dtype=np.float32)


# version 04 (final)
# speedup vs baseline: 1.1803x; 1.0517x over previous
import numpy as np
import ml_dtypes
import concourse.bass as bass
import concourse.mybir as mybir
from concourse.bass_utils import run_bass_kernel_spmd

F32 = mybir.dt.float32
BF16 = mybir.dt.bfloat16
I32 = mybir.dt.int32
ALU = mybir.AluOpType
AF = mybir.ActivationFunctionType
AX = mybir.AxisListType
BF = ml_dtypes.bfloat16

P = 128
D = 1024; DC = 8
T = 2048; NT4 = 4
NH = 16
FF = 4096; NFC = 32
EPS_LN = 1e-5
EPS_Q = 1e-8

OFF_X = 17408
OFF_WQ2T = 17408
OFF_B = 82944
OFF_QSB = OFF_B
OFF_KSB = OFF_B + 8192
OFF_VSB = OFF_B + 16384
OFF_USB = OFF_B + 24704
OFF_MASKS = OFF_B + 57472
OFF_WQ1T = OFF_B
OFF_H3Q0 = OFF_B
OFF_H3Q1 = OFF_B + 32768
OFF_SXROW = OFF_B
OFF_SXXROW = OFF_B + 8192
OFF_XBF = OFF_B + 24704
OFF_MH2 = OFF_B + 41088
OFF_C = 148480
OFF_D = 181248
OFF_ROWS = 214016
OFF_E = 226304
OFF_MODROW = OFF_B

ENGS = ("sync", "gpsimd", "tensor", "vector", "scalar")


class Ctr:
    __slots__ = ("sem", "n")
    def __init__(self, sem):
        self.sem = sem
        self.n = 0
    def bump(self, k=1):
        self.n += k
        return self.n


SIM_DRAINS = [False]


def dinc(e, sem, k=1):
    e.drain().then_inc(sem, k)


class DrainWrap:
    __slots__ = ("_e",)
    def __init__(self, e):
        object.__setattr__(self, "_e", e)
    def __getattr__(self, n):
        a = getattr(self._e, n)
        if callable(a) and n not in ("wait_ge", "drain", "nop", "sem_inc"):
            def w(*args, **kw):
                self._e.drain()
                return a(*args, **kw)
            return w
        return a


class Plan:
    def __init__(self):
        self.ops = {e: [] for e in ENGS}
    def op(self, eng, fn, waits=()):
        self.ops[eng].append((tuple(waits), fn))
    def emit(self, nc):
        with nc.Block() as b:
            for eng in ENGS:
                ops = self.ops[eng]
                if not ops:
                    continue
                def body(e, ops=ops):
                    if SIM_DRAINS[0]:
                        e = DrainWrap(e)
                    for waits, fn in ops:
                        for (c, v) in waits:
                            e.wait_ge(c.sem, v)
                        fn(e)
                getattr(b, eng)(body)


def build_nc(stage=5, debug=(), sim=False, skip_mod=False):
    SIM_DRAINS[0] = bool(sim)
    nc = bass.Bass("TRN2", target_bir_lowering=False, debug=False, dynamic_dma_scratch_size=16384,
                   detect_race_conditions=bool(sim))
    dbg = set(debug)

    x_in = nc.dram_tensor("x_fm", [D, T], F32, kind="ExternalInput").ap()
    wq_d = nc.dram_tensor("wq", [D, D], F32, kind="ExternalInput").ap()
    wk_d = nc.dram_tensor("wk", [D, D], F32, kind="ExternalInput").ap()
    wv_d = nc.dram_tensor("wv", [D, D], F32, kind="ExternalInput").ap()
    wo_d = nc.dram_tensor("wo", [D, D], F32, kind="ExternalInput").ap()
    wmod_d = nc.dram_tensor("wmod", [D, 6 * D], F32, kind="ExternalInput").ap()
    crow_d = nc.dram_tensor("crow", [1, D], F32, kind="ExternalInput").ap()
    w1n_d = nc.dram_tensor("w1n", [FF, D], F32, kind="ExternalInput").ap()
    w1t_d = nc.dram_tensor("w1t", [D, FF], F32, kind="ExternalInput").ap()
    w2n_d = nc.dram_tensor("w2n", [D, FF], F32, kind="ExternalInput").ap()
    w2t_d = nc.dram_tensor("w2t", [FF, D], F32, kind="ExternalInput").ap()
    b1_d = nc.dram_tensor("b1c", [P, NFC], F32, kind="ExternalInput").ap()
    b2_d = nc.dram_tensor("b2c", [P, DC], F32, kind="ExternalInput").ap()
    mask_d = nc.dram_tensor("masks", [4, P, 512], BF16, kind="ExternalInput").ap()
    y_out = nc.dram_tensor("y_fm", [D, T], F32, kind="ExternalOutput").ap()

    cc_in1 = nc.dram_tensor("cc_in1", [1, 8], F32)
    cc_out1 = nc.dram_tensor("cc_out1", [1, 8], F32)
    cc_in2 = nc.dram_tensor("cc_in2", [1, 8], F32)
    cc_out2 = nc.dram_tensor("cc_out2", [1, 8], F32)
    h3buf = nc.dram_tensor("h3buf", [NT4, NFC, P, 512], F32).ap()
    xbuf = nc.dram_tensor("xbuf", [DC, P, T], F32).ap()
    rowbuf = nc.dram_tensor("rowbuf", [1, 16 * D], F32).ap()

    AL = lambda name, shape, dt, off: nc.alloc_sbuf_tensor_at(name, shape, dt, offset=off)
    x_sb = AL("x_sb", [P, DC * T], F32, OFF_X)
    q_sb = AL("q_sb", [P, 2 * T], BF16, OFF_QSB)
    k_sb = AL("k_sb", [P, 2 * T], BF16, OFF_KSB)
    v_sb = AL("v_sb", [P, 16 * 260], BF16, OFF_VSB)
    u_sb = AL("u_sb", [P, 8 * T], BF16, OFF_USB)
    wq1t_sb = AL("wq1t_sb", [P, DC * FF], BF16, OFF_WQ1T)
    wq2t_sb = AL("wq2t_sb", [P, NFC * D], BF16, OFF_WQ2T)
    h3q_sb = [AL("h3q0_sb", [P, NFC * 512], BF16, OFF_H3Q0),
              AL("h3q1_sb", [P, NFC * 512], BF16, OFF_H3Q1)]
    h1_sb = AL("h1_sb", [P, DC * T], BF16, OFF_C)
    hq_sb = AL("hq_sb", [P, DC * T], BF16, OFF_C)
    modrow = AL("modrow", [1, 6 * D], F32, OFF_MODROW)

    xbf_stg = [AL("xbf_stg0", [P, T], BF16, OFF_XBF), AL("xbf_stg1", [P, T], BF16, OFF_XBF + 4096)]
    xsq_stg = [AL("xsq_stg0", [P, T], BF16, OFF_XBF + 8192), AL("xsq_stg1", [P, T], BF16, OFF_XBF + 12288)]
    wmod_stg = [AL("wmod_stg0", [P, DC * 512], BF16, OFF_D),
                AL("wmod_stg1", [P, DC * 512], BF16, OFF_D + 8192)]
    wmod32_stg = AL("wmod32_stg", [P, DC * 512], F32, OFF_D + 16384)
    wstg = AL("wstg", [P, 3 * DC * 256], BF16, OFF_D)
    qsq_stg = AL("qsq_stg", [P, T], BF16, OFF_D + 28672)
    wo_stg = AL("wo_stg", [P, DC * D], BF16, OFF_D + 12288)
    tmp_stg = AL("tmp_stg", [P, T], F32, OFF_D)
    h2stg = AL("h2stg", [P, T], F32, OFF_D + 8192)
    w1tstg = [AL("w1tstg0", [P, 512], F32, OFF_D + 16384), AL("w1tstg1", [P, 512], F32, OFF_D + 18432)]
    i32stg = AL("i32stg", [P, 512], I32, OFF_D + 20480)
    rwstg = [AL("rwstg0", [1, 512], F32, OFF_D + 22528), AL("rwstg1", [1, 512], F32, OFF_ROWS)]
    h3stg = [AL("h3stg0", [P, 512], F32, OFF_D + 24576), AL("h3stg1", [P, 512], F32, OFF_D + 26624)]
    w2nstg = [AL("w2nstg0", [P, D], F32, OFF_D), AL("w2nstg1", [P, D], F32, OFF_D + 4096)]
    w1nstg = [AL("w1nstg0", [P, D], F32, OFF_D), AL("w1nstg1", [P, D], F32, OFF_D + 8192)]
    i32w1 = [AL("i32w1_0", [P, 512], I32, OFF_D + 12288), AL("i32w1_1", [P, 512], I32, OFF_D + 14336)]
    w2rwstg = [AL("w2rwstg0", [1, 512], F32, OFF_ROWS + 4096), AL("w2rwstg1", [1, 512], F32, OFF_ROWS + 6144)]
    w2tstg = [AL("w2tstg0", [P, 512], F32, OFF_D + 28672), AL("w2tstg1", [P, 512], F32, OFF_D + 30720)]
    rbstg = [AL("rbstg%d" % i, [P, 512], F32, OFF_D + off) for i, off in
             enumerate((0, 2048, 4096, 20480, 22528, 24576))]
    op1stg = [AL("op1stg0", [P, 512], F32, OFF_D), AL("op1stg1", [P, 512], F32, OFF_D + 2048)]
    op2stg = [AL("op2stg0", [P, 512], F32, OFF_D + 4096), AL("op2stg1", [P, 512], F32, OFF_D + 6144)]
    i32b_stg = AL("i32b_stg", [P, 512], I32, OFF_D + 6144)
    o1stg = [AL("o1stg0", [P, 512], F32, OFF_D + 8192), AL("o1stg1", [P, 512], F32, OFF_D + 18432)]
    o2stg = [AL("o2stg0", [P, 512], F32, OFF_D + 10240), AL("o2stg1", [P, 512], F32, OFF_D + 12288)]
    xinstg = [AL("xinstg0", [P, 512], F32, OFF_D + 14336), AL("xinstg1", [P, 512], F32, OFF_D + 16384)]

    sxrow = AL("sxrow", [1, T], F32, OFF_SXROW)
    sxxrow = AL("sxxrow", [1, T], F32, OFF_SXXROW)
    rrow2 = AL("rrow2", [1, 1024], F32, OFF_ROWS)
    a_tiles = [AL("a_tile0", [P, 1024], BF16, OFF_ROWS + 4096),
               AL("a_tile1", [P, 1024], BF16, OFF_ROWS + 6144),
               AL("a_tile2", [P, 1024], BF16, OFF_D + 28672),
               AL("a_tile3", [P, 1024], BF16, OFF_D + 30720)]
    rb2_sb = AL("rb2_sb", [64, 1024], F32, OFF_ROWS + 8192)
    mqrow = AL("mqrow", [1, 64], F32, OFF_ROWS + 2048)
    mhold = AL("mhold", [1, 64], F32, OFF_ROWS + 2304)
    negcrow = AL("negcrow", [1, NH], F32, OFF_ROWS + 2560)
    mhold2 = AL("mhold2", [1, 2 * T], F32, OFF_MH2)

    e = OFF_E
    def EA(name, shape, dt):
        nonlocal e
        t = AL(name, shape, dt, e)
        itm = {BF16: 2, F32: 4, I32: 4}[dt]
        nbytes = int(np.prod(shape[1:])) * itm
        e += -(-nbytes // 32) * 32
        return t
    masks_sb = AL("masks_sb", [P, 4 * 512], BF16, OFF_MASKS)
    sa_pp = EA("sa_pp", [P, DC], F32)
    sb_pp = EA("sb_pp", [P, DC], F32)
    ga_pp = EA("ga_pp", [P, DC], F32)
    gb_pp = EA("gb_pp", [P, DC], F32)
    capp_pp = EA("capp_pp", [P, DC], F32)
    cbpp_pp = EA("cbpp_pp", [P, DC], F32)
    cbr1_pp = EA("cbr1_pp", [P, DC], F32)
    sbr1_pp = EA("sbr1_pp", [P, DC], F32)
    ccol = EA("ccol", [P, DC], F32)
    scol = EA("scol", [P, DC], BF16)
    scol32 = EA("scol32", [P, DC], F32)
    ones_bf = EA("ones_bf", [P, 8], BF16)
    onesrow_f = EA("onesrow_f", [1, P], F32)
    negc_pp = EA("negc_pp", [P, NH], F32)
    sw1 = EA("sw1", [P, NFC], F32)
    sw1tmp = EA("sw1tmp", [P, NFC], F32)
    sw2tmp = EA("sw2tmp", [P, DC], F32)
    rw1 = EA("rw1", [P, NFC], F32)
    p1c = EA("p1c", [P, NFC], F32)
    pb1c = EA("pb1c", [P, NFC], F32)
    b1sb = EA("b1sb", [P, NFC], F32)
    bq1f = EA("bq1f", [P, NFC], F32)
    i32r = EA("i32r", [P, NFC], I32)
    sw2 = EA("sw2", [P, DC], F32)
    rw2 = EA("rw2", [P, DC], F32)
    p2c = EA("p2c", [P, DC], F32)
    p2gb = EA("p2gb", [P, DC], F32)
    p2gbb = EA("p2gbb", [P, DC], F32)
    b2sb = EA("b2sb", [P, DC], F32)
    bq2f = EA("bq2f", [P, DC], F32)
    i32r2 = EA("i32r2", [P, DC], I32)
    s1pp = EA("s1pp", [P, 1], F32)
    s2pp = EA("s2pp", [P, 1], F32)
    r1pp = EA("r1pp", [P, 1], F32)
    r2pp = EA("r2pp", [P, 1], F32)
    mx8 = EA("mx8", [P, DC], F32)
    mxf = EA("mxf", [P, NT4 * NFC], F32)
    gmaxs = EA("gmaxs", [1, 8], F32)
    tiny = EA("tiny", [1, 8], F32)
    assert e <= 224 * 1024 - 32, e

    psA = nc.alloc_psum_tensor("psA", [P, T], F32)
    psB = nc.alloc_psum_tensor("psB", [P, T], F32)

    dbg_out = {}
    for name, shp in (("mod", [1, 6 * D]), ("h1", [D, T]), ("rows", [2, T]), ("raw", [2, T]), ("u", [D, T]),
                      ("negc", [1, NH]), ("x2", [D, T]), ("hq", [D, T]),
                      ("sc1", [1, 8]), ("mxdbg", [4, P]), ("nbdbg", [2, 64]), ("mx2dbg", [4, P]), ("wq1dbg", [P, DC * FF]),
                      ("h3dbg", [NT4, NFC, P, 512]), ("sc2", [1, 8]), ("wq2dbg", [P, NFC * D]), ("b2dbg", [6, P])):
        if name in dbg:
            dbg_out[name] = nc.dram_tensor("dbg_" + name, shp, F32, kind="ExternalOutput").ap()

    C = {}
    def ctr(name):
        C[name] = Ctr(nc.alloc_semaphore(name))
        return C[name]
    for n in ("ld0", "silu", "m32", "mw", "mwf", "mr", "me", "pp", "vone",
              "l1sv", "l1ss", "l1sf", "l1mm", "l1rw",
              "qw", "qwf", "pjr", "pjf", "nbs", "nbm", "nbv", "nbf",
              "ats", "atx", "ata", "atf", "atr", "atb", "atrb", "atu",
              "opr", "opv", "opf", "wo_ok",
              "l2sv", "l2ss", "l2sf", "l2mm", "l2rw",
              "mxv", "cc1", "cc1c", "sc1c", "gh2", "rqv", "xo",
              "w1ld", "w1mx", "w1rw", "w1bc", "w1qv", "w1tt", "w1f",
              "f1r", "f1s", "f1v", "f1d",
              "w2ld", "w2mx", "w2rw", "w2bc", "w2qv",
              "cc2", "cc2c", "sc2c", "b2p",
              "rbq", "f2r", "f2s", "f2v", "f2x", "yo", "dbgc"):
        ctr(n)

    def pair(name):
        return (ctr(name + "_a"), ctr(name + "_b"))
    mwp = pair("mwp"); w1mp = pair("w1mp"); w1qp = pair("w1qp"); w1rp = pair("w1rp")
    f1hp = pair("f1hp"); w2mp = pair("w2mp"); w2qp = pair("w2qp"); w2rp = pair("w2rp")
    rblp = pair("rblp"); f2xp = pair("f2xp"); yop = pair("yop")
    rbl6 = [ctr("rbl6_%d" % i) for i in range(6)]

    def dma2(pl, eng, out, in_, pr, idx, waits=(), ncd=False):
        c = pr[idx % 2]
        v = c.bump(16)
        pl.op(eng, lambda e, o=out, i=in_, cc=c, ncd=ncd: e.dma_start(
            out=o, in_=i, allow_slow_non_contiguous=ncd).then_inc(cc.sem, 16), waits=waits)
        return (c, v)

    def dma(pl, eng, out, in_, c, ncd=False):
        v = c.bump(16)
        pl.op(eng, lambda e, o=out, i=in_, cc=c, ncd=ncd: e.dma_start(
            out=o, in_=i, allow_slow_non_contiguous=ncd).then_inc(cc.sem, 16))
        return v

    def dump_sbuf(name, pieces):
        if name not in dbg_out:
            return
        pl = Plan()
        c = C["dbgc"]
        for dr, sb in pieces:
            dma(pl, "gpsimd", dr, sb, c)
        tgt = c.n
        pl.op("gpsimd", lambda e, t=tgt: None, waits=[(c, tgt)])
        pl.emit(nc)

    pl = Plan()
    ld0, silu_c, mw, mwf, mr, me = C["ld0"], C["silu"], C["mw"], C["mwf"], C["mr"], C["me"]
    for dc in range(DC):
        dma(pl, "sync", x_sb[:, dc * T:(dc + 1) * T], x_in[dc * P:(dc + 1) * P, :], ld0)
    dma(pl, "sync", masks_sb[:].rearrange("p (m c) -> p m c", c=512), mask_d[:, :, :].transpose([1, 0, 2]), ld0)
    dma(pl, "sync", ccol[:, :], crow_d[0:1, :].rearrange("o (c p) -> p (o c)", p=P), ld0, ncd=True)
    ld0_all = ld0.n
    pl.op("gpsimd", lambda e: e.memset(ones_bf[:], 1.0))
    pl.op("gpsimd", lambda e: e.memset(ones_bf[:, 1:3], 0.0))
    pl.op("gpsimd", lambda e: e.memset(ones_bf[0:64, 1:2], 1.0))
    pl.op("gpsimd", lambda e: e.memset(ones_bf[64:128, 2:3], 1.0))
    pl.op("gpsimd", lambda e: e.memset(onesrow_f[:], 1.0))
    pl.op("gpsimd", lambda e: e.memset(tiny[:], 0.0))
    mw_waits = []
    J32 = (4, 5, 10, 11)
    for j in range(0 if skip_mod == "all" else 12):
        waits = ([(mr, j - 1)] if j >= 2 else []) if not skip_mod else []
        if j in J32:
            c = C["m32"]
            w32 = ([(mr, j)] if j >= 1 else []) if not skip_mod else []
            for dc in range(DC):
                v = c.bump(16)
                pl.op("sync", lambda e, j=j, dc=dc, cc=c: e.dma_start(
                    out=wmod32_stg[:, dc * 512:(dc + 1) * 512],
                    in_=wmod_d[dc * P:(dc + 1) * P, j * 512:(j + 1) * 512]).then_inc(cc.sem, 16),
                    waits=w32 if dc == 0 else [])
            mw_waits.append((c, c.n))
        else:
            c = mwp[j % 2]
            for dc in range(DC):
                v = c.bump(16)
                pl.op("gpsimd", lambda e, j=j, dc=dc, cc=c: e.dma_start(
                    out=wmod_stg[j % 2][:, dc * 512:(dc + 1) * 512],
                    in_=wmod_d[dc * P:(dc + 1) * P, j * 512:(j + 1) * 512]).then_inc(cc.sem, 16),
                    waits=waits if dc == 0 else [])
            mw_waits.append((c, c.n))
    pl.op("scalar", lambda e: (e.activation(out=scol32[:, :], in_=ccol[:, :], func=AF.Silu),
                               dinc(e, silu_c.sem)), waits=[(ld0, ld0_all)])
    silu_c.bump()
    pl.op("vector", lambda e: (e.tensor_copy(out=scol[:, :], in_=scol32[:, :]),
                               dinc(e, silu_c.sem)), waits=[(silu_c, silu_c.n)])
    silu_c.bump()
    for j in range(12 if not skip_mod else 0):
        waits = [mw_waits[j]]
        if j <= 1:
            waits.append((silu_c, 2))
        if j >= 4:
            waits.append((me, j - 3))
        def mm_mod(e, j=j):
            if j in J32:
                for dc in range(DC):
                    ins = e.matmul(psA[0:1, (j % 4) * 512:(j % 4) * 512 + 512],
                                   scol32[:, dc:dc + 1],
                                   wmod32_stg[:, dc * 512:(dc + 1) * 512],
                                   start=(dc == 0), stop=(dc == DC - 1))
            else:
                for dc in range(DC):
                    ins = e.matmul(psA[0:1, (j % 4) * 512:(j % 4) * 512 + 512],
                                   scol[:, dc:dc + 1],
                                   wmod_stg[j % 2][:, dc * 512:(dc + 1) * 512],
                                   start=(dc == 0), stop=(dc == DC - 1))
            ins.then_inc(mr.sem, 1)
        pl.op("tensor", mm_mod, waits=waits)
        mr.bump()
        def ev_mod(e, j=j):
            e.tensor_copy(out=modrow[0:1, j * 512:(j + 1) * 512],
                          in_=psA[0:1, (j % 4) * 512:(j % 4) * 512 + 512])
            dinc(e, me.sem)
        pl.op("vector", ev_mod, waits=[(mr, j + 1)])
        me.bump()
    if skip_mod:
        pl.op("vector", lambda e: e.memset(modrow[:, :], 0.01))
    pl.emit(nc)

    pl = Plan()
    pp = C["pp"]
    def mod_post(e):
        e = DrainWrap(e)
        S = h2stg[0:1, 0:D]
        S2 = tmp_stg[0:1, 0:D]
        S3 = h2stg[0:1, D:2 * D]
        e.tensor_scalar(out=S2, in0=modrow[0:1, D:2 * D], scalar1=1.0, scalar2=None, op0=ALU.add)
        e.tensor_copy(out=modrow[0:1, D:2 * D], in_=S2)
        e.tensor_scalar(out=S2, in0=modrow[0:1, 4 * D:5 * D], scalar1=1.0, scalar2=None, op0=ALU.add)
        e.tensor_copy(out=modrow[0:1, 4 * D:5 * D], in_=S2)
        for off in (2 * D, 5 * D):
            gsl = modrow[0:1, off:off + D]
            e.tensor_scalar(out=S, in0=gsl, scalar1=0.0, scalar2=None, op0=ALU.is_ge)
            e.tensor_scalar(out=S2, in0=S, scalar1=2.0, scalar2=-1.0, op0=ALU.mult, op1=ALU.add)
            e.tensor_scalar(out=S, in0=gsl, scalar1=-1.0, scalar2=None, op0=ALU.mult)
            e.tensor_tensor(out=S3, in0=gsl, in1=S, op=ALU.max)
            e.tensor_scalar(out=S, in0=S3, scalar1=0.25, scalar2=None, op0=ALU.max)
            e.tensor_tensor(out=gsl, in0=S2, in1=S, op=ALU.mult)
        dinc(e, pp.sem)
    pl.op("vector", mod_post)
    pp.bump()
    order = [(0, sa_pp), (3 * D, sb_pp), (2 * D, ga_pp), (5 * D, gb_pp), (D, capp_pp), (4 * D, cbpp_pp)]
    w0 = pp.n
    for i, (off, _) in enumerate(order):
        dma(pl, "sync", rowbuf[0:1, i * D:(i + 1) * D], modrow[0:1, off:off + D], pp)
        if i == 0:
            pl.ops["sync"][-1] = ((pl.ops["sync"][-1][0] + ((pp, w0),)), pl.ops["sync"][-1][1])
    wmid = pp.n
    pl.op("sync", lambda e: None, waits=[(pp, wmid)])
    for i, (_, dst) in enumerate(order):
        dma(pl, "sync", dst[:, :], rowbuf[0:1, i * D:(i + 1) * D].rearrange("o (c p) -> p (o c)", p=P), pp, ncd=True)
    wend = pp.n
    pl.op("sync", lambda e: None, waits=[(pp, wend)])
    pl.emit(nc)

    if "mod" in dbg_out:
        dump_sbuf("mod", [(dbg_out["mod"][:, :], modrow[:, :])])

    def ln_stats(pl, pref, xsrc):
        sv, ss, sf, smm, srw = C[pref + "sv"], C[pref + "ss"], C[pref + "sf"], C[pref + "mm"], C[pref + "rw"]
        base_sf = sf.n
        for dc in range(DC):
            waits = [(sf, base_sf + dc - 1)] if dc >= 2 else []
            pl.op("vector", lambda e, dc=dc: (e.tensor_copy(
                out=xbf_stg[dc % 2][:, :], in_=xsrc[:, dc * T:(dc + 1) * T]), dinc(e, sv.sem)), waits=waits)
            sv.bump()
            pl.op("scalar", lambda e, dc=dc: (e.square(
                out=xsq_stg[dc % 2][:, :], in_=xsrc[:, dc * T:(dc + 1) * T]), dinc(e, ss.sem)), waits=waits)
            ss.bump()
            def mm_st(e, dc=dc):
                for tc in range(NT4):
                    e.matmul(psA[0:1, tc * 512:tc * 512 + 512], ones_bf[:, 0:1],
                             xbf_stg[dc % 2][:, tc * 512:tc * 512 + 512], start=(dc == 0), stop=(dc == DC - 1))
                    ins = e.matmul(psB[0:1, tc * 512:tc * 512 + 512], ones_bf[:, 0:1],
                                   xsq_stg[dc % 2][:, tc * 512:tc * 512 + 512], start=(dc == 0), stop=(dc == DC - 1))
                ins.then_inc(sf.sem, 1)
            pl.op("tensor", mm_st, waits=[(sv, sv.n), (ss, ss.n)])
            sf.bump()
        def rows1(e):
            e = DrainWrap(e)
            T4 = tmp_stg[0:1, 0:T]
            S = h2stg[0:1, 0:T]
            e.tensor_copy(out=sxrow[0:1, :], in_=psA[0:1, 0:T])
            e.tensor_copy(out=sxxrow[0:1, :], in_=psB[0:1, 0:T])
            if "raw" in dbg_out and pref == "l1":
                e.tensor_copy(out=mhold2[0:1, 0:T], in_=psA[0:1, 0:T])
                e.tensor_copy(out=mhold2[0:1, T:2 * T], in_=psB[0:1, 0:T])
            
            e.tensor_scalar(out=T4, in0=sxrow[0:1, :], scalar1=-1.0 / D, scalar2=None, op0=ALU.mult)
            e.tensor_scalar(out=S, in0=sxxrow[0:1, :], scalar1=1.0 / D, scalar2=None, op0=ALU.mult)
            e.tensor_tensor(out=sxrow[0:1, :], in0=T4, in1=T4, op=ALU.mult)
            e.tensor_tensor(out=sxxrow[0:1, :], in0=S, in1=sxrow[0:1, :], op=ALU.subtract)
            e.tensor_scalar(out=S, in0=sxxrow[0:1, :], scalar1=EPS_LN, scalar2=None, op0=ALU.add)
            e.reciprocal(out=sxxrow[0:1, :], in_=S)
            dinc(e, srw.sem)
        pl.op("vector", rows1, waits=[(sf, base_sf + DC)])
        srw.bump()
        pl.op("scalar", lambda e: (e.activation(out=h2stg[0:1, 0:T], in_=sxxrow[0:1, :],
                                                func=AF.Sqrt), dinc(e, srw.sem)), waits=[(srw, srw.n)])
        srw.bump()
        def rows2(e):
            e = DrainWrap(e)
            T4 = tmp_stg[0:1, 0:T]
            S = h2stg[0:1, 0:T]
            e.tensor_tensor(out=sxrow[0:1, :], in0=T4, in1=S, op=ALU.mult)
            e.tensor_copy(out=sxxrow[0:1, :], in_=S)
            dinc(e, srw.sem)
        pl.op("vector", rows2, waits=[(srw, srw.n)])
        srw.bump()

    def gh_bcast(pl, srw_like):
        def ghm(e):
            for tc in range(NT4):
                e.matmul(psA[:, tc * 512:tc * 512 + 512], onesrow_f[0:1, :],
                         sxxrow[0:1, tc * 512:tc * 512 + 512], start=True, stop=True)
                ins = e.matmul(psB[:, tc * 512:tc * 512 + 512], onesrow_f[0:1, :],
                               sxrow[0:1, tc * 512:tc * 512 + 512], start=True, stop=True)
            ins.then_inc(srw_like.sem, 1)
        pl.op("tensor", ghm, waits=[(srw_like, srw_like.n)])
        srw_like.bump()

    pl = Plan()
    ln_stats(pl, "l1", x_sb)
    gh_bcast(pl, C["l1rw"])
    ghv = C["l1rw"].n
    pl.emit(nc)
    if "rows" in dbg_out:
        dump_sbuf("rows", [(dbg_out["rows"][0:1, :], sxxrow[0:1, :]), (dbg_out["rows"][1:2, :], sxrow[0:1, :])])
    if "raw" in dbg_out:
        dump_sbuf("raw", [(dbg_out["raw"][0:1, :], mhold2[0:1, 0:T]), (dbg_out["raw"][1:2, :], mhold2[0:1, T:2 * T])])
    pl = Plan()
    def apply1(e):
        for dc in range(DC):
            e.tensor_tensor(out=tmp_stg[:, :], in0=x_sb[:, dc * T:(dc + 1) * T], in1=psA[:, 0:T], op=ALU.mult)
            e.tensor_tensor(out=h2stg[:, :], in0=tmp_stg[:, :], in1=psB[:, 0:T], op=ALU.add)
            e.tensor_scalar(out=h1_sb[:, dc * T:(dc + 1) * T], in0=h2stg[:, :],
                            scalar1=capp_pp[:, dc:dc + 1], scalar2=sa_pp[:, dc:dc + 1],
                            op0=ALU.mult, op1=ALU.add)
    pl.op("vector", apply1, waits=[(C["l1rw"], ghv)])
    pl.emit(nc)

    dump_sbuf("h1", [(dbg_out.get("h1") and dbg_out["h1"][dc * P:(dc + 1) * P, :], h1_sb[:, dc * T:(dc + 1) * T])
                     for dc in range(DC)] if "h1" in dbg_out else [])
    if stage < 2:
        return nc, dbg_out

    pl = Plan()
    qw, qwf = C["qw"], C["qwf"]
    vone = C["vone"]
    pl.op("gpsimd", lambda e: (e.memset(
        v_sb[:].rearrange("p (t h c) -> p t h c", h=4, c=65)[:, :, :, 64:65], 1.0), dinc(e, vone.sem)))
    vone.bump()
    pjr, pjf = C["pjr"], C["pjf"]
    nbs, nbm, nbv, nbf = C["nbs"], C["nbm"], C["nbv"], C["nbf"]
    ats, atx, ata, atf = C["ats"], C["atx"], C["ata"], C["atf"]
    atr, atb, atrb, atu = C["atr"], C["atb"], C["atrb"], C["atu"]

    wo_ok = C["wo_ok"]
    NPJ = 32
    pj_done = {}
    qw_done = {}

    def emit_wloads(qt):
        c0 = qt * 256
        first = True
        for wi, wd in enumerate((wq_d, wk_d, wv_d)):
            for dc in range(DC):
                qw.bump(16)
                w = [(pjf, pj_done[qt - 1])] if (first and qt >= 1) else []
                pl.op("gpsimd", lambda e, wd=wd, wi=wi, dc=dc, c0=c0: e.dma_start(
                    out=wstg[:, wi * 2048 + dc * 256:wi * 2048 + (dc + 1) * 256],
                    in_=wd[dc * P:(dc + 1) * P, c0:c0 + 256]).then_inc(qw.sem, 16),
                    waits=w)
                first = False
        qw_done[qt] = qw.n

    def emit_proj_tile(qt, kind, a, tc):
        gi = pjr.n
        waits = [(qw, qw_done[qt]), (atrb, atrb.n)]
        if gi >= 2:
            waits.append((pjf, gi - 1))
        sl = psB[:, 1024 + (gi % 2) * 512:1024 + (gi % 2) * 512 + 512]
        def mm_pj(e, kind=kind, a=a, tc=tc, sl=sl):
            for dc in range(DC):
                if kind == "q":
                    ins = e.matmul(sl, wstg[:, dc * 256 + a * 128:dc * 256 + a * 128 + 128],
                             h1_sb[:, dc * T + tc * 512:dc * T + tc * 512 + 512],
                             start=(dc == 0), stop=(dc == DC - 1))
                elif kind == "k":
                    ins = e.matmul(sl, wstg[:, 2048 + dc * 256 + a * 128:2048 + dc * 256 + a * 128 + 128],
                             h1_sb[:, dc * T + tc * 512:dc * T + tc * 512 + 512],
                             start=(dc == 0), stop=(dc == DC - 1))
                else:
                    ins = e.matmul(sl[0:P, 0:256], h1_sb[:, dc * T + a * 128:dc * T + a * 128 + 128],
                             wstg[:, 4096 + dc * 256:4096 + dc * 256 + 256],
                             start=(dc == 0), stop=(dc == DC - 1))
            ins.then_inc(pjr.sem, 1)
        pl.op("tensor", mm_pj, waits=waits)
        pjr.bump()
        def ev_pj(e, kind=kind, a=a, tc=tc, sl=sl):
            if kind == "q":
                e.tensor_copy(out=q_sb[:, a * T + tc * 512:a * T + tc * 512 + 512], in_=sl)
            elif kind == "k":
                e.tensor_copy(out=k_sb[:, a * T + tc * 512:a * T + tc * 512 + 512], in_=sl)
            else:
                e.tensor_copy(out=v_sb[:, a * 260:a * 260 + 260].rearrange("p (h c) -> p h c", c=65)[:, :, 0:64],
                              in_=sl[0:P, 0:256].rearrange("p (h c) -> p h c", c=64))
            dinc(e, pjf.sem)
        pl.op("vector", ev_pj, waits=[(pjr, pjr.n)])
        pjf.bump()

    def proj_batch(qt, qc):
        for oc in range(2):
            emit_proj_tile(qt, "q", oc, qc)
        for oc in range(2):
            emit_proj_tile(qt, "k", oc, qc)
        for j in range(4):
            emit_proj_tile(qt, "v", 4 * qc + j, 0)

    pend_proj = []
    PACE_PROJ = globals().get("PACE", True)

    emit_wloads(0)
    def wo_load(e):
        for fc in range(DC):
            e.dma_start(out=wo_stg[:, fc * D:(fc + 1) * D],
                        in_=wo_d[fc * P:(fc + 1) * P, :]).then_inc(wo_ok.sem, 16)
    pl.op("gpsimd", wo_load)
    wo_ok.bump(16 * DC)

    for qt in range(4):
        if qt == 0:
            for qc in range(NT4 - 1, -1, -1):
                proj_batch(0, qc)
            pj_done[0] = pjf.n
            emit_wloads(1)
        elif qt < 3:
            emit_wloads(qt + 1)

        for half in range(4):
            src = q_sb if half < 2 else k_sb
            oc = half % 2
            sqw = [(pjf, pj_done[qt])] + ([(nbm, nbm.n)] if half > 0 else [])
            if half == 0 and qt >= 1:
                sqw.append((atf, 80 * qt))
            pl.op("scalar", lambda e, src=src, oc=oc: (e.square(
                out=qsq_stg[:, :], in_=src[:, oc * T:(oc + 1) * T]), dinc(e, nbs.sem)), waits=sqw)
            nbs.bump()
            for hh in range(2):
                def mm_nb2(e, hh=hh):
                    for tc in range(NT4):
                        ins = e.matmul(psA[0:1, tc * 512:tc * 512 + 512],
                                 ones_bf[:, 1 + hh:2 + hh],
                                 qsq_stg[:, tc * 512:tc * 512 + 512],
                                 start=True, stop=True)
                    ins.then_inc(nbm.sem, 1)
                w = [(nbs, nbs.n), (nbv, nbv.n)] if hh == 0 else [(nbv, nbv.n)]
                if half == 0 and hh == 0 and qt >= 1:
                    w.append((atx, 80 * qt))
                pl.op("tensor", mm_nb2, waits=w)
                nbm.bump()
                def rd_nb(e, half=half, hh=hh):
                    col = (half * 2 + hh)
                    for tc in range(NT4):
                        e.reduce_max(out=mqrow[0:1, col * 8 + tc:col * 8 + tc + 1],
                                     in_=psA[0:1, tc * 512:tc * 512 + 512], axis=AX.X)
                    dinc(e, nbv.sem)
                pl.op("vector", rd_nb, waits=[(nbm, nbm.n)])
                nbv.bump()
        def nb_comb(e, qt=qt):
            e = DrainWrap(e)
            for hl in range(4):
                e.reduce_max(out=mhold[0:1, hl * 8:hl * 8 + 1], in_=mqrow[0:1, hl * 8:hl * 8 + 4], axis=AX.X)
                e.reduce_max(out=mhold[0:1, hl * 8 + 1:hl * 8 + 2], in_=mqrow[0:1, 32 + hl * 8:32 + hl * 8 + 4], axis=AX.X)
            e.tensor_tensor(out=mhold[0:1, :].rearrange("o (h c) -> o h c", c=8)[:, 0:4, 2:3],
                            in0=mhold[0:1, :].rearrange("o (h c) -> o h c", c=8)[:, 0:4, 0:1],
                            in1=mhold[0:1, :].rearrange("o (h c) -> o h c", c=8)[:, 0:4, 1:2],
                            op=ALU.mult)
            dinc(e, nbv.sem)
        pl.op("vector", nb_comb)
        nbv.bump()
        pl.op("scalar", lambda e: (e.activation(
            out=mhold[0:1, :].rearrange("o (h c) -> o h c", c=8)[:, 0:4, 4:5],
            in_=mhold[0:1, :].rearrange("o (h c) -> o h c", c=8)[:, 0:4, 2:3],
            func=AF.Sqrt), dinc(e, nbs.sem)), waits=[(nbv, nbv.n)])
        nbs.bump()
        def nb_scale(e, qt=qt):
            e = DrainWrap(e)
            if qt == 0 and "nbdbg" in dbg_out:
                e.tensor_copy(out=mhold2[0:1, 0:64], in_=mqrow[0:1, :])
                e.tensor_copy(out=mhold2[0:1, 64:128], in_=mhold[0:1, :])
            e.tensor_scalar(out=negcrow[0:1, qt * 4:qt * 4 + 4],
                            in0=mhold[0:1, :].rearrange("o (h c) -> o h c", c=8)[:, 0:4, 4:5],
                            scalar1=-0.125, scalar2=None, op0=ALU.mult)
            dinc(e, nbv.sem)
        pl.op("vector", nb_scale, waits=[(nbs, nbs.n)])
        nbv.bump()
        def nb_bc(e, qt=qt):
            e.matmul(psA[:, 0:512][0:P, 0:4], onesrow_f[0:1, :], negcrow[0:1, qt * 4:qt * 4 + 4],
                     start=True, stop=True).then_inc(nbm.sem, 1)
        pl.op("tensor", nb_bc, waits=[(nbv, nbv.n)])
        nbm.bump()
        def nb_ev(e, qt=qt):
            e.tensor_copy(out=negc_pp[:, qt * 4:qt * 4 + 4],
                          in_=psA[:, 0:512][0:P, 0:4])
            dinc(e, nbf.sem)
        pl.op("vector", nb_ev, waits=[(nbm, nbm.n)])
        nbf.bump()

        for qc in range(NT4 - 1, -1, -1):
            for pr in range(2):
                hg = 4 * qt + 2 * pr
                nkt = 4 * qc + 4
                ngrp = nkt // 2
                uev_before = atu.n

                def emit_S(g, hh, pr=pr, qc=qc):
                    gidx = ats.n
                    waits = []
                    if gidx in (qt * 80, qt * 80 + 1):
                        waits.append((nbf, qt + 1))
                    if gidx >= 2:
                        waits.append((atx, gidx - 1))
                    def mm_s(e, pr=pr, qc=qc, g=g, hh=hh):
                        for half in range(2):
                            kt = 2 * g + half
                            ins = e.matmul(psA[:, hh * 1024 + half * 512:hh * 1024 + half * 512 + 512],
                                     k_sb[hh * 64:hh * 64 + 64, pr * T + kt * 128:pr * T + kt * 128 + 128],
                                     q_sb[hh * 64:hh * 64 + 64, pr * T + qc * 512:pr * T + qc * 512 + 512],
                                     start=True, stop=True)
                        ins.then_inc(ats.sem, 1)
                    pl.op("tensor", mm_s, waits=waits)
                    ats.bump()

                emit_S(0, 0)
                emit_S(0, 1)
                for g in range(ngrp):
                    diag = (2 * g + 1) >= 4 * qc
                    mi0 = 2 * g - 4 * qc
                    for hh in range(2):
                        gidx = atx.n
                        asl = a_tiles[2 * hh + g % 2]
                        ewaits = [(ats, gidx + 1)]
                        if gidx >= 4:
                            ewaits.append((atf, gidx - 3))
                        def ex_s(e, hh=hh, asl=asl, hg=hg):
                            e.activation(out=asl[:, :], in_=psA[:, hh * 1024:hh * 1024 + 1024],
                                         func=AF.Exp, bias=negc_pp[:, hg + hh:hg + hh + 1],
                                         scale=0.125).then_inc(atx.sem, 1)
                        pl.op("scalar", ex_s, waits=ewaits)
                        atx.bump()
                        if diag:
                            def mk_s(e, asl=asl, mi0=mi0):
                                e.tensor_tensor(out=asl[:, :], in0=asl[:, :],
                                                in1=masks_sb[:, mi0 * 512:mi0 * 512 + 1024],
                                                op=ALU.mult).then_inc(ata.sem, 1)
                            pl.op("vector", mk_s, waits=[(atx, atx.n)])
                            ata.bump(1)
                            a_ready = (ata, ata.n)
                        else:
                            a_ready = (atx, atx.n)
                        avw = [a_ready]
                        if g == 0 and hh == 0:
                            avw.append((atu, uev_before))
                            avw.append((pjf, pj_done[qt]))
                            if atf.n == 0:
                                avw.append((vone, 1))
                        def mm_av(e, pr=pr, g=g, hh=hh, asl=asl,
                                  first=(g == 0), last=(g == ngrp - 1)):
                            h1l = 2 * pr
                            for half in range(2):
                                kt = 2 * g + half
                                ins = e.matmul(psB[0:65, hh * 512:hh * 512 + 512],
                                         v_sb[:, kt * 260 + (h1l + hh) * 65:kt * 260 + (h1l + hh) * 65 + 65],
                                         asl[:, half * 512:half * 512 + 512],
                                         start=(first and half == 0), stop=(last and half == 1))
                            ins.then_inc(atf.sem, 1)
                        pl.op("tensor", mm_av, waits=avw)
                        atf.bump()
                        if g + 1 < ngrp:
                            emit_S(g + 1, hh)
                        if atf.n % 2 == 0 and pend_proj:
                            emit_proj_tile(*pend_proj.pop(0))
                glast = atf.n
                def sums_ev(e):
                    e.activation(out=psB[64:65, 1024:2048], in_=psB[64:65, 0:1024], func=AF.Ln)
                    e.activation(out=rrow2[0:1, 0:1024], in_=psB[64:65, 1024:2048], func=AF.Exp,
                                 scale=-1.0)
                    dinc(e, atr.sem)
                pl.op("scalar", sums_ev, waits=[(atf, glast), (pjr, pjr.n)])
                atr.bump()
                def rb_bc(e):
                    e.matmul(psB[0:64, 1024:1536], onesrow_f[0:1, 0:64], rrow2[0:1, 512:1024],
                             start=True, stop=True)
                    e.matmul(psB[0:64, 1536:2048], onesrow_f[0:1, 0:64], rrow2[0:1, 0:512],
                             start=True, stop=True).then_inc(atb.sem, 1)
                pl.op("tensor", rb_bc, waits=[(atr, atr.n), (pjf, pjf.n)])
                atb.bump()
                def u_ev(e, hg=hg, qc=qc):
                    e.tensor_copy(out=rb2_sb[0:64, :], in_=psB[0:64, 1024:2048])
                    dinc(e, atrb.sem)
                    e.tensor_tensor(out=u_sb[0:64, (hg // 2) * T + qc * 512:(hg // 2) * T + qc * 512 + 512],
                                    in0=psB[0:64, 0:512], in1=rb2_sb[0:64, 512:1024],
                                    op=ALU.mult)
                    e.tensor_tensor(out=u_sb[64:128, (hg // 2) * T + qc * 512:(hg // 2) * T + qc * 512 + 512],
                                    in0=psB[0:64, 512:1024], in1=rb2_sb[0:64, 0:512],
                                    op=ALU.mult)
                    dinc(e, atu.sem)
                pl.op("vector", u_ev, waits=[(atb, atb.n)])
                atrb.bump(); atu.bump()
            if qt + 1 < 4:
                if qc > 0 and PACE_PROJ:
                    pend_proj.extend(
                        [(qt + 1, "q", oc, qc) for oc in range(2)]
                        + [(qt + 1, "k", oc, qc) for oc in range(2)]
                        + [(qt + 1, "v", 4 * qc + j, 0) for j in range(4)])
                elif qc > 0:
                    proj_batch(qt + 1, qc)
                else:
                    while pend_proj:
                        emit_proj_tile(*pend_proj.pop(0))
                    proj_batch(qt + 1, 0)
        if qt + 1 < 4:
            pj_done[qt + 1] = pjf.n
    pl.emit(nc)


    if "nbdbg" in dbg_out:
        dump_sbuf("nbdbg", [(dbg_out["nbdbg"][0:1, 0:NH], negc_pp[0:1, :]), (dbg_out["nbdbg"][1:2, 0:NH], negc_pp[127:128, :])])
    dump_sbuf("u", [(dbg_out.get("u") and dbg_out["u"][fc * P:(fc + 1) * P, :], u_sb[:, fc * T:(fc + 1) * T])
                    for fc in range(DC)] if "u" in dbg_out else [])
    if "negc" in dbg_out:
        dump_sbuf("negc", [(dbg_out["negc"][:, :], negcrow[0:1, :])])

    pl = Plan()
    opr, opv, opf = C["opr"], C["opv"], C["opf"]
    w1mx, w1rw = C["w1mx"], C["w1rw"]
    pl.op("tensor", lambda e: None, waits=[(wo_ok, wo_ok.n)])
    for dc in range(DC):
        for tc in range(NT4):
            gi = opr.n
            waits = [(opf, gi - 1)] if gi >= 2 else []
            sl = psA[:, (gi % 2) * 512:(gi % 2) * 512 + 512]
            def mm_op(e, dc=dc, tc=tc, sl=sl):
                for fc in range(DC):
                    ins = e.matmul(sl, wo_stg[:, fc * D + dc * 128:fc * D + dc * 128 + 128],
                                   u_sb[:, fc * T + tc * 512:fc * T + tc * 512 + 512],
                                   start=(fc == 0), stop=(fc == DC - 1))
                ins.then_inc(opr.sem, 1)
            pl.op("tensor", mm_op, waits=waits)
            opr.bump()
            def ev_op(e, dc=dc, tc=tc, sl=sl, gi=gi):
                e.scalar_tensor_tensor(out=op2stg[gi % 2][:, :],
                                       in0=sl, scalar=ga_pp[:, dc:dc + 1],
                                       in1=x_sb[:, dc * T + tc * 512:dc * T + tc * 512 + 512],
                                       op0=ALU.mult, op1=ALU.add)
                dinc(e, opv.sem)
            pl.op("vector", ev_op, waits=[(opr, opr.n)] + ([(opf, gi - 1)] if gi >= 2 else []))
            opv.bump()
            def wb_op(e, dc=dc, tc=tc, gi=gi):
                e.copy(out=x_sb[:, dc * T + tc * 512:dc * T + tc * 512 + 512],
                       in_=op2stg[gi % 2][:, :])
                dinc(e, opf.sem)
            pl.op("scalar", wb_op, waits=[(opv, opv.n)])
            opf.bump()
            fcm = gi
            bw1 = [(w1mx, fcm - 1)] if fcm >= 2 else []
            lw = dma2(pl, "sync", w1nstg[fcm % 2][:, :], w1n_d[fcm * P:(fcm + 1) * P, :],
                      w1mp, fcm, waits=bw1)
            def mx_w1(e, fcm=fcm):
                e.reduce_max(out=sw1[:, fcm:fcm + 1], in_=w1nstg[fcm % 2][:, :],
                             axis=AX.X, apply_absolute_value=True)
                dinc(e, w1mx.sem)
            pl.op("vector", mx_w1, waits=[lw])
            w1mx.bump()
    def sw1f(e):
        e = DrainWrap(e)
        e.tensor_scalar(out=sw1tmp[:, :], in0=sw1[:, :], scalar1=EPS_Q, scalar2=1.0 / 127.0,
                        op0=ALU.max, op1=ALU.mult)
        e.reciprocal(out=rw1[:, :], in_=sw1tmp[:, :])
        dinc(e, w1mx.sem)
    pl.op("vector", sw1f)
    w1mx.bump()
    dma(pl, "sync", rowbuf[0:1, 8 * D:8 * D + FF].rearrange("o (a b) -> o a b", b=P).transpose([0, 2, 1]),
        rw1[:, :], w1rw, ncd=True)
    pl.ops["sync"][-1] = ((pl.ops["sync"][-1][0] + ((w1mx, w1mx.n),)), pl.ops["sync"][-1][1])
    pl.emit(nc)

    dump_sbuf("x2", [(dbg_out.get("x2") and dbg_out["x2"][dc * P:(dc + 1) * P, :], x_sb[:, dc * T:(dc + 1) * T])
                     for dc in range(DC)] if "x2" in dbg_out else [])
    if stage < 3:
        return nc, dbg_out

    pl = Plan()
    ln_stats(pl, "l2", x_sb)
    gh_bcast(pl, C["l2rw"])
    ghv2 = C["l2rw"].n
    mxv, cc1, sc1c, gh2, rqv, xo = C["mxv"], C["cc1"], C["sc1c"], C["gh2"], C["rqv"], C["xo"]
    def maxpass(e):
        for dc in range(DC):
            e.tensor_tensor(out=tmp_stg[:, :], in0=x_sb[:, dc * T:(dc + 1) * T], in1=psA[:, 0:T], op=ALU.mult)
            e.tensor_tensor(out=h2stg[:, :], in0=tmp_stg[:, :], in1=psB[:, 0:T], op=ALU.add)
            e.tensor_scalar(out=tmp_stg[:, :], in0=h2stg[:, :],
                            scalar1=cbpp_pp[:, dc:dc + 1], scalar2=sb_pp[:, dc:dc + 1],
                            op0=ALU.mult, op1=ALU.add)
            e.reduce_max(out=mx8[:, dc:dc + 1], in_=tmp_stg[:, :], axis=AX.X)
        dinc(e, mxv.sem)
    pl.op("vector", maxpass, waits=[(C["l2rw"], ghv2)])
    mxv.bump()
    dma(pl, "sync", rowbuf[0:1, 4 * D:4 * D + 1024].rearrange("o (p c) -> (o p) c", c=8), mx8[:, :], xo)
    pl.ops["sync"][-1] = ((pl.ops["sync"][-1][0] + ((mxv, mxv.n),)), pl.ops["sync"][-1][1])
    dma(pl, "sync", h2stg[0:1, 0:1024], rowbuf[0:1, 4 * D:4 * D + 1024], xo)
    pl.ops["sync"][-1] = ((pl.ops["sync"][-1][0] + ((xo, xo.n - 16),)), pl.ops["sync"][-1][1])
    xov = xo.n
    def gmax_local(e):
        e = DrainWrap(e)
        e.memset(gmaxs[0:1, :], 0.0)
        e.reduce_max(out=gmaxs[0:1, 0:1], in_=h2stg[0:1, 0:1024], axis=AX.X)
        dinc(e, mxv.sem)
    pl.op("vector", gmax_local, waits=[(xo, xov)])
    mxv.bump()
    cc1c = C["cc1c"]
    def coll1(e):
        e.dma_start(out=cc_in1.ap(), in_=gmaxs[0:1, :]).then_inc(cc1.sem, 16)
        e.wait_ge(cc1.sem, 16)
        e.collective_compute("AllReduce", ALU.max, replica_groups=[list(range(8))],
                             ins=[cc_in1.ap().opt()], outs=[cc_out1.ap().opt()]).then_inc(cc1c.sem, 1)
        e.wait_ge(cc1c.sem, 1)
        e.dma_start(out=gmaxs[0:1, :], in_=cc_out1.ap()).then_inc(cc1.sem, 16)
    pl.op("gpsimd", coll1, waits=[(mxv, mxv.n)])
    cc1.bump(32); cc1c.bump(1)
    def s1row(e):
        e = DrainWrap(e)
        e.tensor_scalar(out=tiny[0:1, 0:8], in0=gmaxs[0:1, 0:8], scalar1=EPS_Q, scalar2=1.0 / 255.0,
                        op0=ALU.max, op1=ALU.mult)
        e.reciprocal(out=mqrow[0:1, 0:8], in_=tiny[0:1, 0:8])
        dinc(e, sc1c.sem)
    pl.op("vector", s1row, waits=[(cc1, cc1.n)])
    sc1c.bump()
    def s1bc(e):
        e.matmul(psA[:, 0:512][0:P, 0:1], onesrow_f[0:1, :], tiny[0:1, 0:1], start=True, stop=True)
        e.matmul(psA[:, 512:1024][0:P, 0:1], onesrow_f[0:1, :], mqrow[0:1, 0:1],
                 start=True, stop=True).then_inc(sc1c.sem, 1)
    pl.op("tensor", s1bc, waits=[(sc1c, sc1c.n)])
    sc1c.bump()
    def s1ev(e):
        e = DrainWrap(e)
        e.tensor_copy(out=s1pp[:, 0:1], in_=psA[:, 0:512][0:P, 0:1])
        e.tensor_copy(out=r1pp[:, 0:1], in_=psA[:, 512:1024][0:P, 0:1])
        e.tensor_scalar(out=cbr1_pp[:, :], in0=cbpp_pp[:, :], scalar1=r1pp[:, 0:1], scalar2=None, op0=ALU.mult)
        e.tensor_scalar(out=sbr1_pp[:, :], in0=sb_pp[:, :], scalar1=r1pp[:, 0:1], scalar2=None,
                        op0=ALU.mult)
        dinc(e, sc1c.sem)
    pl.op("vector", s1ev, waits=[(sc1c, sc1c.n)])
    sc1c.bump()
    pl.op("tensor", lambda e: None, waits=[(sc1c, sc1c.n)])
    gh_bcast(pl, gh2)
    gh2v = gh2.n
    for dc in range(DC):
        def rq(e, dc=dc):
            e.tensor_tensor(out=tmp_stg[:, :], in0=x_sb[:, dc * T:(dc + 1) * T], in1=psA[:, 0:T], op=ALU.mult)
            e.tensor_tensor(out=h2stg[:, :], in0=tmp_stg[:, :], in1=psB[:, 0:T], op=ALU.add)
            e.tensor_scalar(out=tmp_stg[:, :], in0=h2stg[:, :],
                            scalar1=cbr1_pp[:, dc:dc + 1], scalar2=sbr1_pp[:, dc:dc + 1],
                            op0=ALU.mult, op1=ALU.add)
            for tc in range(NT4):
                e.tensor_scalar(out=i32stg[:, :], in0=tmp_stg[:, tc * 512:tc * 512 + 512],
                                scalar1=0.0, scalar2=None, op0=ALU.max)
                e.tensor_copy(out=hq_sb[:, dc * T + tc * 512:dc * T + tc * 512 + 512], in_=i32stg[:, :])
            dinc(e, rqv.sem)
        w = [(sc1c, sc1c.n), (gh2, gh2v)] if dc == 0 else []
        pl.op("vector", rq, waits=w)
        rqv.bump()
        dma(pl, "sync", xbuf[dc, :, :], x_sb[:, dc * T:(dc + 1) * T], xo)
        pl.ops["sync"][-1] = ((pl.ops["sync"][-1][0] + ((rqv, rqv.n),)), pl.ops["sync"][-1][1])
    xo_done = xo.n
    pl.op("sync", lambda e: None, waits=[(xo, xo_done)])
    pl.emit(nc)

    if "sc1" in dbg_out:
        dump_sbuf("sc1", [(dbg_out["sc1"][:, :], tiny[0:1, :])])
    if "mxdbg" in dbg_out:
        dump_sbuf("mxdbg", [(dbg_out["mxdbg"][0:1, 0:DC], mx8[0:1, :]),
                            (dbg_out["mxdbg"][2:3, 0:P], h2stg[0:1, 0:P]),
                            (dbg_out["mxdbg"][3:4, 0:8], gmaxs[0:1, :])])
    dump_sbuf("hq", [(dbg_out.get("hq") and dbg_out["hq"][dc * P:(dc + 1) * P, :], hq_sb[:, dc * T:(dc + 1) * T])
                     for dc in range(DC)] if "hq" in dbg_out else [])
    if stage < 4:
        return nc, dbg_out

    pl = Plan()
    f1r, f1s, f1v, f1d = C["f1r"], C["f1s"], C["f1v"], C["f1d"]
    w1ld, w1rw, w1bc, w1qv, w1tt, w1f = (C[k] for k in ("w1ld", "w1rw", "w1bc", "w1qv", "w1tt", "w1f"))
    w2mx, w2rw, w2bc, w2qv, w2ld = (C[k] for k in ("w2mx", "w2rw", "w2bc", "w2qv", "w2ld"))
    cc2, sc2c, b2p = C["cc2"], C["sc2c"], C["b2p"]

    v_b1 = w1ld.bump(16)
    pl.op("sync", lambda e: e.dma_start(out=b1sb[:, :], in_=b1_d[:, :]).then_inc(w1ld.sem, 16))
    def b1prep(e):
        e = DrainWrap(e)
        e.tensor_scalar(out=p1c[:, :], in0=sw1tmp[:, :], scalar1=s1pp[:, 0:1], scalar2=None, op0=ALU.mult)
        e.reciprocal(out=pb1c[:, :], in_=p1c[:, :])
        e.tensor_tensor(out=i32r[:, :], in0=b1sb[:, :], in1=pb1c[:, :], op=ALU.mult)
        e.tensor_copy(out=bq1f[:, :], in_=i32r[:, :])
        e.tensor_tensor(out=pb1c[:, :], in0=p1c[:, :], in1=bq1f[:, :], op=ALU.mult)
        dinc(e, w1f.sem)
    pl.op("vector", b1prep, waits=[(w1ld, v_b1)])
    w1f.bump()

    def emit_w1_bcast(cc):
        rlw = dma2(pl, "sync", rwstg[cc % 2][0:1, :],
                   rowbuf[0:1, 8 * D + cc * 512:8 * D + cc * 512 + 512], w1rp, cc,
                   waits=[(w1rw, w1rw.n)] if cc == 0 else ([(w1bc, cc - 1)] if cc >= 2 else []))
        def bc_w1(e, cc=cc):
            e.matmul(psA[:, (cc % 2) * 512:(cc % 2) * 512 + 512], onesrow_f[0:1, :],
                     rwstg[cc % 2][0:1, :], start=True, stop=True).then_inc(w1bc.sem, 1)
        bw = [rlw]
        if cc >= 2:
            bw.append((w1qv, (cc - 1) * DC))
        pl.op("tensor", bc_w1, waits=bw)
        w1bc.bump()

    def emit_w1_quant(cc, dc):
        gidx = cc * DC + dc
        qlw = dma2(pl, "sync", w1tstg[dc % 2][:, :],
                   w1t_d[dc * P:(dc + 1) * P, cc * 512:cc * 512 + 512], w1qp, gidx,
                   waits=[(w1tt, gidx - 1)] if gidx >= 2 else [])
        def q_w1(e, cc=cc, dc=dc, gidx=gidx):
            e.tensor_tensor(out=i32w1[gidx % 2][:, :], in0=w1tstg[dc % 2][:, :],
                            in1=psA[:, (cc % 2) * 512:(cc % 2) * 512 + 512],
                            op=ALU.mult).then_inc(w1tt.sem, 1)
        vw = [qlw, (w1bc, cc + 1)]
        if gidx >= 2:
            vw.append((w1qv, gidx - 1))
        pl.op("vector", q_w1, waits=vw)
        w1tt.bump()
        def cp_w1(e, cc=cc, dc=dc, gidx=gidx):
            e.copy(out=wq1t_sb[:, dc * FF + cc * 512:dc * FF + cc * 512 + 512],
                   in_=i32w1[gidx % 2][:, :])
            dinc(e, w1qv.sem)
        pl.op("scalar", cp_w1, waits=[(w1tt, w1tt.n)])
        w1qv.bump()

    def emit_w2_rowmax(idx):
        dc2, piece = idx // 4, idx % 4
        lw2 = dma2(pl, "sync", w2nstg[idx % 2][:, :],
                   w2n_d[dc2 * P:(dc2 + 1) * P, piece * D:piece * D + D], w2mp, idx,
                   waits=[(w2mx, idx - 1)] if idx >= 2 else [])
        def mx_w2(e, dc2=dc2, piece=piece, idx=idx):
            e = DrainWrap(e)
            e.reduce_max(out=mx8[:, piece:piece + 1], in_=w2nstg[idx % 2][:, :],
                         axis=AX.X, apply_absolute_value=True)
            if piece == 3:
                e.reduce_max(out=sw2[:, dc2:dc2 + 1], in_=mx8[:, 0:4], axis=AX.X)
            dinc(e, w2mx.sem)
        pl.op("vector", mx_w2, waits=[lw2])
        w2mx.bump()

    def emit_w2_scales():
        def sw2f(e):
            e = DrainWrap(e)
            e.tensor_scalar(out=sw2tmp[:, :], in0=sw2[:, :], scalar1=EPS_Q, scalar2=1.0 / 127.0,
                            op0=ALU.max, op1=ALU.mult)
            e.reciprocal(out=rw2[:, :], in_=sw2tmp[:, :])
            dinc(e, w2mx.sem)
        pl.op("vector", sw2f)
        w2mx.bump()
        dma(pl, "sync", rowbuf[0:1, 0:D].rearrange("o (a b) -> o a b", b=P).transpose([0, 2, 1]),
            rw2[:, :], w2rw, ncd=True)
        pl.ops["sync"][-1] = ((pl.ops["sync"][-1][0] + ((w2mx, w2mx.n),)), pl.ops["sync"][-1][1])

    def emit_w2_rwload(cc):
        return dma2(pl, "sync", w2rwstg[cc][0:1, :], rowbuf[0:1, cc * 512:cc * 512 + 512],
                    w2rp, cc, waits=[(w2rw, w2rw.n)] if cc == 0 else [])

    def emit_w2_bcast(cc, rlw2):
        def bc_w2(e, cc=cc):
            e.matmul(psA[:, 1024 + (cc % 2) * 512:1024 + (cc % 2) * 512 + 512], onesrow_f[0:1, :],
                     w2rwstg[cc][0:1, :], start=True, stop=True).then_inc(w2bc.sem, 1)
        pl.op("tensor", bc_w2, waits=[rlw2])
        w2bc.bump()

    def emit_w2_quant(cc, fc2):
        gidx2 = cc * NFC + fc2
        qlw2 = dma2(pl, "sync", w2tstg[fc2 % 2][:, :],
                    w2t_d[fc2 * P:(fc2 + 1) * P, cc * 512:cc * 512 + 512], w2qp, gidx2,
                    waits=[(w2qv, gidx2 - 1)] if gidx2 >= 2 else [])
        def q_w2(e, cc=cc, fc2=fc2):
            e.tensor_tensor(out=i32stg[:, :], in0=w2tstg[fc2 % 2][:, :],
                            in1=psA[:, 1024 + (cc % 2) * 512:1024 + (cc % 2) * 512 + 512],
                            op=ALU.mult)
            e.tensor_copy(out=wq2t_sb[:, fc2 * D + cc * 512:fc2 * D + cc * 512 + 512],
                          in_=i32stg[:, :])
            dinc(e, w2qv.sem)
        pl.op("vector", q_w2, waits=[qlw2, (w2bc, cc + 1)])
        w2qv.bump()

    emit_w1_bcast(0)
    for dc in range(DC):
        emit_w1_quant(0, dc)
    pend_w1 = []
    pend_w2q = [(cc, f) for cc in range(2) for f in range(NFC)]
    w2mx_next = 0
    w2rl = [None, None]
    ti = 0
    NT_F1 = NT4 * NFC
    for fc in range(NFC):
        cc_next = fc // 4 + 1
        if fc % 4 == 0 and cc_next < DC:
            emit_w1_bcast(cc_next)
            pend_w1 = [(cc_next, dc) for dc in range(DC)]
        for tc in range(NT4):
            gi = ti; ti += 1
            waits = [(f1s, gi - 3)] if gi >= 4 else []
            if tc == 0:
                waits.append((w1qv, DC * (fc // 4 + 1)))
            sl = psB[:, (gi % 4) * 512:(gi % 4) * 512 + 512]
            def mm_f1(e, tc=tc, fc=fc, sl=sl):
                for dc in range(DC):
                    ins = e.matmul(sl, wq1t_sb[:, dc * FF + fc * 128:dc * FF + fc * 128 + 128],
                                   hq_sb[:, dc * T + tc * 512:dc * T + tc * 512 + 512],
                                   start=(dc == 0), stop=(dc == DC - 1))
                ins.then_inc(f1r.sem, 1)
            pl.op("tensor", mm_f1, waits=waits)
            f1r.bump()
            sw = [(f1r, f1r.n)]
            if gi == 0:
                sw.append((w1f, 1))
            if gi >= 2:
                sw.append((f1d, gi - 1))
            def ev_f1(e, gi=gi, fc=fc, sl=sl):
                e.activation(out=h3stg[gi % 2][:, :], in_=sl, func=AF.Relu,
                             scale=p1c[:, fc:fc + 1], bias=pb1c[:, fc:fc + 1])
                dinc(e, f1s.sem)
            pl.op("scalar", ev_f1, waits=sw)
            f1s.bump()
            def mxf1(e, gi=gi):
                e.reduce_max(out=mxf[:, gi:gi + 1], in_=h3stg[gi % 2][:, :], axis=AX.X)
                dinc(e, f1v.sem)
            pl.op("vector", mxf1, waits=[(f1s, f1s.n)])
            f1v.bump()
            stw = dma2(pl, "sync", h3buf[tc, fc, :, :], h3stg[gi % 2][:, :], f1hp, gi,
                       waits=[(f1s, f1s.n)])
            def fd_mark(e, gi=gi):
                e.nop().then_inc(f1d.sem, 1)
            pl.op("sync", fd_mark, waits=[stw, (f1v, f1v.n)])
            f1d.bump()
            if gi % 2 == 0 and pend_w1:
                emit_w1_quant(*pend_w1.pop(0))
            if gi % 2 == 1 and gi >= 4 and w2mx_next < 32:
                emit_w2_rowmax(w2mx_next); w2mx_next += 1
            if gi == 73:
                emit_w2_scales()
            if gi == 75:
                w2rl[0] = emit_w2_rwload(0)
            if gi == 77:
                emit_w2_bcast(0, w2rl[0])
            if gi == 81:
                w2rl[1] = emit_w2_rwload(1)
            if gi == 83:
                emit_w2_bcast(1, w2rl[1])
            if gi >= 80 and pend_w2q:
                emit_w2_quant(*pend_w2q.pop(0))

    def mxfin(e):
        e = DrainWrap(e)
        for c in range(8):
            e.reduce_max(out=mx8[:, c:c + 1], in_=mxf[:, c * 16:(c + 1) * 16], axis=AX.X)
        dinc(e, f1v.sem)
    pl.op("vector", mxfin, waits=[(f1v, NT_F1)])
    f1v.bump()
    v = C["xo"].bump(16)
    pl.op("sync", lambda e: e.dma_start(
        out=rowbuf[0:1, 4 * D:4 * D + 1024].rearrange("o (p c) -> (o p) c", c=8),
        in_=mx8[:, :]).then_inc(C["xo"].sem, 16),
        waits=[(f1v, NT_F1 + 1)])
    v2 = C["xo"].bump(16)
    pl.op("sync", lambda e: e.dma_start(out=h2stg[0:1, 0:1024],
                                        in_=rowbuf[0:1, 4 * D:4 * D + 1024]).then_inc(C["xo"].sem, 16),
          waits=[(C["xo"], v)])
    for _ in range(min(4, len(pend_w2q))):
        emit_w2_quant(*pend_w2q.pop(0))
    def gmax2_local(e):
        e = DrainWrap(e)
        e.memset(gmaxs[0:1, :], 0.0)
        e.reduce_max(out=gmaxs[0:1, 0:1], in_=h2stg[0:1, 0:1024], axis=AX.X)
        dinc(e, f1v.sem)
    pl.op("vector", gmax2_local, waits=[(C["xo"], v2)])
    f1v.bump(1)
    cc2c = C["cc2c"]
    def coll2(e):
        e.dma_start(out=cc_in2.ap(), in_=gmaxs[0:1, :]).then_inc(cc2.sem, 16)
        e.wait_ge(cc2.sem, 16)
        e.collective_compute("AllReduce", ALU.max, replica_groups=[list(range(8))],
                             ins=[cc_in2.ap().opt()], outs=[cc_out2.ap().opt()]).then_inc(cc2c.sem, 1)
        e.wait_ge(cc2c.sem, 1)
        e.dma_start(out=gmaxs[0:1, :], in_=cc_out2.ap()).then_inc(cc2.sem, 16)
    pl.op("gpsimd", coll2, waits=[(f1v, f1v.n)])
    cc2.bump(32); cc2c.bump(1)
    while pend_w2q:
        emit_w2_quant(*pend_w2q.pop(0))
    def s2row(e):
        e = DrainWrap(e)
        e.tensor_scalar(out=tiny[0:1, 0:8], in0=gmaxs[0:1, 0:8], scalar1=EPS_Q, scalar2=1.0 / 255.0,
                        op0=ALU.max, op1=ALU.mult)
        e.reciprocal(out=mqrow[0:1, 0:8], in_=tiny[0:1, 0:8])
        dinc(e, sc2c.sem)
    pl.op("vector", s2row, waits=[(cc2, cc2.n)])
    sc2c.bump()
    def s2bc(e):
        e.matmul(psB[:, 0:512][0:P, 0:1], onesrow_f[0:1, :], tiny[0:1, 0:1], start=True, stop=True)
        e.matmul(psB[:, 512:1024][0:P, 0:1], onesrow_f[0:1, :], mqrow[0:1, 0:1],
                 start=True, stop=True).then_inc(sc2c.sem, 1)
    pl.op("tensor", s2bc, waits=[(sc2c, sc2c.n), (f1s, NT_F1)])
    sc2c.bump()
    v = w2ld.bump(16)
    pl.op("sync", lambda e: e.dma_start(out=b2sb[:, :], in_=b2_d[:, :]).then_inc(w2ld.sem, 16))
    def b2prep(e):
        e = DrainWrap(e)
        e.tensor_copy(out=r2pp[:, 0:1], in_=psB[:, 512:1024][0:P, 0:1])
        e.tensor_copy(out=s2pp[:, 0:1], in_=psB[:, 0:512][0:P, 0:1])
        e.tensor_scalar(out=p2c[:, :], in0=sw2tmp[:, :], scalar1=s2pp[:, 0:1], scalar2=None, op0=ALU.mult)
        e.reciprocal(out=p2gbb[:, :], in_=p2c[:, :])
        e.tensor_tensor(out=i32r2[:, :], in0=b2sb[:, :], in1=p2gbb[:, :], op=ALU.mult)
        e.tensor_copy(out=bq2f[:, :], in_=i32r2[:, :])
        e.tensor_tensor(out=p2gb[:, :], in0=p2c[:, :], in1=gb_pp[:, :], op=ALU.mult)
        e.tensor_tensor(out=p2gbb[:, :], in0=p2gb[:, :], in1=bq2f[:, :], op=ALU.mult)
        dinc(e, b2p.sem)
    pl.op("vector", b2prep, waits=[(sc2c, sc2c.n), (w2ld, v)])
    b2p.bump()
    pl.emit(nc)

    if "h3dbg" in dbg_out:
        plh = Plan()
        c = C["dbgc"]
        dma(plh, "sync", dbg_out["h3dbg"][:, :, :, :], h3buf[:, :, :, :], c)
        tgt = c.n
        plh.op("sync", lambda e: None, waits=[(c, tgt)])
        plh.emit(nc)
    if "sc2" in dbg_out:
        dump_sbuf("sc2", [(dbg_out["sc2"][:, :], tiny[0:1, :])])
    if "wq2dbg" in dbg_out:
        dump_sbuf("wq2dbg", [(dbg_out["wq2dbg"][:, :], wq2t_sb[:, :])])
    if "b2dbg" in dbg_out:
        dump_sbuf("b2dbg", [(dbg_out["b2dbg"][0:1, 0:DC], p2gb[0:1, :]),
                            (dbg_out["b2dbg"][1:2, 0:DC], p2gbb[0:1, :]),
                            (dbg_out["b2dbg"][2:3, 0:DC], bq2f[0:1, :]),
                            (dbg_out["b2dbg"][3:4, 0:DC], sw2[0:1, :]),
                            (dbg_out["b2dbg"][4:5, 0:DC], sw2tmp[0:1, :]),
                            (dbg_out["b2dbg"][5:6, 0:DC], gb_pp[0:1, :])])
    if "mx2dbg" in dbg_out:
        dump_sbuf("mx2dbg", [(dbg_out["mx2dbg"][0:1, 0:P], mxf[0:1, :]),
                             (dbg_out["mx2dbg"][1:2, 0:8], mx8[0:1, :]),
                             (dbg_out["mx2dbg"][2:3, 0:P], h2stg[0:1, 0:P]),
                             (dbg_out["mx2dbg"][3:4, 0:8], gmaxs[0:1, :])])
    if stage < 5:
        return nc, dbg_out

    pl = Plan()
    rbq, f2r, f2s, f2v, f2x, yo = (C[k] for k in ("rbq", "f2r", "f2s", "f2v", "f2x", "yo"))
    for tc in range(NT4):
        for fc in range(NFC):
            qi = tc * NFC + fc
            w = [(rbq, qi - 5)] if qi >= 6 else []
            if tc >= 2 and fc == 0:
                w.append((f2r, (tc - 1) * DC))
            c6 = rbl6[qi % 6]
            v6 = c6.bump(16)
            pl.op("sync", lambda e, tc=tc, fc=fc, qi=qi, c6=c6: e.dma_start(
                out=rbstg[qi % 6][:, :], in_=h3buf[tc, fc, :, :]).then_inc(c6.sem, 16), waits=w)
            rlw3 = (c6, v6)
            def q_rb(e, tc=tc, fc=fc, qi=qi):
                e.tensor_scalar(out=i32b_stg[:, :], in0=rbstg[qi % 6][:, :],
                                scalar1=r2pp[:, 0:1], scalar2=None, op0=ALU.mult)
                e.tensor_copy(out=h3q_sb[tc % 2][:, fc * 512:fc * 512 + 512],
                              in_=i32b_stg[:, :])
                dinc(e, rbq.sem)
            qrw = [rlw3]
            if qi == 0:
                qrw.append((C["b2p"], 1))
            pl.op("vector", q_rb, waits=qrw)
            rbq.bump()
        for dc in range(DC):
            gi = tc * DC + dc
            waits = [(rbq, (tc + 1) * NFC)]
            if gi >= 2:
                waits.append((f2s, gi - 1))
            sl = psA[:, (gi % 2) * 512:(gi % 2) * 512 + 512]
            def mm_f2(e, tc=tc, dc=dc, sl=sl):
                for fc in range(NFC):
                    ins = e.matmul(sl, wq2t_sb[:, fc * D + dc * 128:fc * D + dc * 128 + 128],
                                   h3q_sb[tc % 2][:, fc * 512:fc * 512 + 512],
                                   start=(fc == 0), stop=(fc == NFC - 1))
                ins.then_inc(f2r.sem, 1)
            pl.op("tensor", mm_f2, waits=waits)
            f2r.bump()
            xlw = dma2(pl, "sync", xinstg[gi % 2][:, :], xbuf[dc, :, tc * 512:tc * 512 + 512],
                       f2xp, gi, waits=[(f2v, gi - 1)] if gi >= 2 else [])
            def ev_f2s(e, dc=dc, sl=sl, gi=gi):
                e.activation(out=o1stg[gi % 2][:, :], in_=sl, func=AF.Identity,
                             scale=p2gb[:, dc:dc + 1], bias=p2gbb[:, dc:dc + 1])
                dinc(e, f2s.sem)
            sw2_ = [(f2r, f2r.n)]
            if gi >= 2:
                sw2_.append((f2v, gi - 1))
            pl.op("scalar", ev_f2s, waits=sw2_)
            f2s.bump()
            def ev_f2v(e, gi=gi):
                e.tensor_tensor(out=o2stg[gi % 2][:, :], in0=o1stg[gi % 2][:, :], in1=xinstg[gi % 2][:, :],
                                op=ALU.add)
                dinc(e, f2v.sem)
            evw = [(f2s, f2s.n), xlw]
            if gi >= 2:
                evw.append((yop[gi % 2], 16 * (gi // 2)))
            pl.op("vector", ev_f2v, waits=evw)
            f2v.bump()
            dma2(pl, "sync", y_out[dc * P:(dc + 1) * P, tc * 512:tc * 512 + 512],
                 o2stg[gi % 2][:, :], yop, gi, waits=[(f2v, f2v.n)])
    pl.op("sync", lambda e: None, waits=[(yop[0], yop[0].n), (yop[1], yop[1].n)])
    pl.emit(nc)

    return nc, dbg_out


def make_host_masks():
    m = np.zeros((4, P, 512), dtype=BF)
    for kt in range(4):
        for r in range(P):
            m[kt, r, kt * P + r:] = 1.0
    return m


def prep_inputs(inputs):
    x = inputs["L0_attn_x"]
    w1 = np.ascontiguousarray(inputs["L0_ffn_w1"])
    w2 = np.ascontiguousarray(inputs["L0_ffn_w2"])
    shared = {
        "wq": np.ascontiguousarray(inputs["L0_attn_WQ"]),
        "wk": np.ascontiguousarray(inputs["L0_attn_WK"]),
        "wv": np.ascontiguousarray(inputs["L0_attn_WV"]),
        "wo": np.ascontiguousarray(inputs["L0_attn_WO"]),
        "wmod": np.ascontiguousarray(inputs["L0_wmod"]),
        "w1n": w1,
        "w1t": np.ascontiguousarray(w1.T),
        "w2n": w2,
        "w2t": np.ascontiguousarray(w2.T),
        "b1c": np.ascontiguousarray(inputs["L0_ffn_b1"].reshape(NFC, P).T),
        "b2c": np.ascontiguousarray(inputs["L0_ffn_b2"].reshape(DC, P).T),
        "masks": make_host_masks(),
    }
    cores = []
    for b in range(8):
        d = dict(shared)
        d["x_fm"] = np.ascontiguousarray(x[b].T)
        d["crow"] = np.ascontiguousarray(inputs["L0_c"][b:b + 1, :])
        cores.append(d)
    return cores


def kernel(**inputs):
    cores = prep_inputs(inputs)
    nc, _ = build_nc(stage=5)
    res = run_bass_kernel_spmd(nc, cores, core_ids=list(range(8)))
    out = np.stack([r["y_fm"].T for r in res.results], axis=0)
    return np.ascontiguousarray(out, dtype=np.float32)
